# version 18
# baseline (speedup 1.0000x reference)
import numpy as np
import concourse.bass as bass
import concourse.mybir as mybir
from concourse.bass_utils import run_bass_kernel_spmd
from contextlib import ExitStack

F32 = mybir.dt.float32
BF16 = mybir.dt.bfloat16
AF = mybir.ActivationFunctionType
ALU = mybir.AluOpType
AX = mybir.AxisListType

T = 4096
D = 2048
H = 16
HD = 128
CH = 64
NCH = T // CH
CB = 128
NCB = T // CB
DFF = 8192
NT = T // 128
NTB = T // 512
LNSCALE = float(np.log(HD ** -0.5))
EPS = 1e-5

DEBUG_TAPS = []


def build_program():
    nc = bass.Bass("TRN2", target_bir_lowering=False, debug=False)

    x_h = nc.dram_tensor("x", [T, D], F32, kind="ExternalInput").ap()
    wq_h = nc.dram_tensor("Wq", [D, D], F32, kind="ExternalInput").ap()
    wk_h = nc.dram_tensor("Wk", [D, D], F32, kind="ExternalInput").ap()
    wv_h = nc.dram_tensor("Wv", [D, D], F32, kind="ExternalInput").ap()
    wgk1_h = nc.dram_tensor("Wgk1", [D, 16], F32, kind="ExternalInput").ap()
    wgk2_h = nc.dram_tensor("Wgk2", [16, D], F32, kind="ExternalInput").ap()
    bgk2_h = nc.dram_tensor("bgk2", [D], F32, kind="ExternalInput").ap()
    wg_h = nc.dram_tensor("Wg", [D, D], F32, kind="ExternalInput").ap()
    gnorm_h = nc.dram_tensor("gnorm", [HD], F32, kind="ExternalInput").ap()
    wo_h = nc.dram_tensor("Wo", [D, D], F32, kind="ExternalInput").ap()
    w0_h = nc.dram_tensor("W0", [DFF, D], F32, kind="ExternalInput").ap()
    b0_h = nc.dram_tensor("bias0", [DFF], F32, kind="ExternalInput").ap()
    w1_h = nc.dram_tensor("W1", [D, DFF], F32, kind="ExternalInput").ap()
    b1_h = nc.dram_tensor("bias1", [D], F32, kind="ExternalInput").ap()
    out_h = nc.dram_tensor("out", [T, D], F32, kind="ExternalOutput").ap()

    def scratch(name, shape, dtype):
        kind = "ExternalOutput" if name in DEBUG_TAPS else "Internal"
        return nc.dram_tensor(name, shape, dtype, kind=kind).ap()

    xn_tm = scratch("xn_tm", [T, D], BF16)
    wq_b = scratch("wq_b", [D, D], BF16)
    wk_b = scratch("wk_b", [D, D], BF16)
    wv_b = scratch("wv_b", [D, D], BF16)
    wg_b = scratch("wg_b", [D, D], BF16)
    wo_b = scratch("wo_b", [D, D], BF16)
    w0_b = scratch("w0_b", [DFF, D], BF16)
    w1_b = scratch("w1_b", [D, DFF], BF16)
    q_fm = scratch("q_fm", [D, T], BF16)
    k_fm = scratch("k_fm", [D, T], BF16)
    sg_fm = scratch("sg_fm", [D, T], F32)
    qg_fm = scratch("qg_fm", [D, T], BF16)
    kh_fm = scratch("kh_fm", [D, T], BF16)
    v_tm = scratch("v_tm", [T, D], BF16)
    gate_tm = scratch("gate_tm", [T, D], BF16)
    d1_fm = scratch("d1_fm", [DFF, T], BF16)
    e_part = scratch("e_part", [T, D], F32)
    g_tm = scratch("g_tm", [T, D], BF16)
    o_tm = scratch("o_tm", [T, D], BF16)
    og_tm = scratch("og_tm", [T, D], BF16)
    dbg_wos = scratch("dbg_wos", [D, D], BF16) if "dbg_wos" in DEBUG_TAPS else None
    dbg_scan = scratch("dbg_scan", [D, T], F32) if "dbg_scan" in DEBUG_TAPS else None
    dbg_gc = scratch("dbg_gc", [D, T], F32) if "dbg_gc" in DEBUG_TAPS else None

    ident = nc.inline_tensor(np.eye(128, dtype=np.float32), "ident").ap()
    maskt = nc.inline_tensor(np.triu(np.ones((CB, CB), np.float32)), "maskt").ap()

    es = ExitStack()
    sbt = lambda name, shape, dt: es.enter_context(nc.sbuf_tensor(name, shape, dt))
    pst = lambda name, shape: es.enter_context(nc.psum_tensor(name, shape, F32))

    sb_ident = sbt("c_ident", [128, 128], F32)
    sb_mask = sbt("c_mask", [CB, CB], F32)
    bias0c = sbt("c_bias0c", [128, 64], F32)
    bias1c = sbt("c_bias1c", [128, 16], F32)
    bgk2c = sbt("c_bgk2c", [128, 16], F32)
    gnormc = sbt("c_gnormc", [128, 1], F32)
    b1row = sbt("c_b1row", [1, D], BF16)
    onesrow = sbt("c_onesrow", [1, 128], BF16)
    wgk1s = sbt("c_wgk1s", [128, 16 * 16], BF16)
    wgk2s = sbt("c_wgk2s", [16, D], BF16)
    decay = sbt("c_decay", [128, H * NCB], F32)
    ldstat = sbt("c_ldstat", [128, 32], F32)
    cEPS = sbt("c_eps", [128, 1], F32)
    cLNS = sbt("c_lns", [128, 1], F32)

    psA = pst("psA", [128, 2048])
    psB = pst("psB", [128, 2048])

    semnames = (["pe", "dve", "act", "c0", "g_b1r", "wld", "ogf"]
                + ["g_" + k for k in ["wq", "wk", "wv", "wg", "w0", "w1", "wo",
                                      "wgk1", "wgk2"]]
                + [f"p1x{i}" for i in range(3)] + [f"p1w{i}" for i in range(2)]
                + [f"pan{i}" for i in range(2)] + [f"st{i}" for i in range(4)]
                + ["p5q", "p5k", "p5s", "p5wq", "p5wk"]
                + [f"vst{i}" for i in range(2)] + [f"w0t{i}" for i in range(2)]
                + [f"d1c{i}" for i in range(2)] + [f"gacc{i}" for i in range(2)]
                + [f"p9e{i}" for i in range(2)] + [f"p9g{i}" for i in range(2)]
                + [f"sc{i}" for i in range(2)] + [f"v1{i}" for i in range(2)]
                + [f"ow{i}" for i in range(2)]
                + [f"p11a{i}" for i in range(2)] + [f"p11b{i}" for i in range(2)]
                + [f"ogw{i}" for i in range(2)]
                + [f"px{i}" for i in range(2)] + [f"pg{i}" for i in range(2)]
                + [f"outw{i}" for i in range(2)])
    sm = {k: es.enter_context(nc.semaphore(k)) for k in semnames}
    n = {k: 0 for k in sm}
    streams = {k: [] for k in ["pe", "dve", "act", "sync", "gq"]}

    def dma(stream, qsem, out, in_, waits=(), **kw):
        n[qsem] += 16
        tgt = n[qsem]

        def f(e, out=out, in_=in_, waits=tuple(waits), kw=dict(kw)):
            for s, v in waits:
                if v > 0:
                    e.wait_ge(sm[s], v)
            e.dma_start(out=out, in_=in_, **kw).then_inc(sm[qsem], 16)
        streams[stream].append(f)
        return tgt

    def op(stream, fn, waits=(), inc=None):
        csem = {"pe": "pe", "dve": "dve", "act": "act"}[stream]
        do_inc = inc is not False
        prior = n[csem]
        if do_inc:
            n[csem] += 1
        tgt = n[csem]
        selfw = (csem, prior) if stream in ("dve", "act") else None

        def f(e, fn=fn, waits=tuple(waits), do_inc=do_inc, selfw=selfw):
            if selfw is not None and selfw[1] > 0:
                e.wait_ge(sm[selfw[0]], selfw[1])
            for s, v in waits:
                if v > 0:
                    e.wait_ge(sm[s], v)
            inst = fn(e)
            if do_inc:
                inst.then_inc(sm[csem], 1)
        streams[stream].append(f)
        return tgt

    def waitonly(stream, waits):
        def f(e, waits=tuple(waits)):
            for s, v in waits:
                if v > 0:
                    e.wait_ge(sm[s], v)
        streams[stream].append(f)

    CASTS = ("g_wq", "g_wk", "g_wv", "g_wg", "g_w0", "g_w1", "g_wo",
             "g_wgk1", "g_wgk2")

    def barrier():
        snap = {k: v for k, v in n.items() if k not in CASTS}
        for s in streams:
            waitonly(s, list(snap.items()))

    cast_done = {}
    cast_done["wgk1"] = ("g_wgk1", dma(
        "gq", "g_wgk1", wgk1s.ap().rearrange("p (c m) -> p c m", m=16),
        wgk1_h.rearrange("(c p) m -> p c m", p=128)))
    cast_done["wgk2"] = ("g_wgk2", dma("gq", "g_wgk2", wgk2s[:], wgk2_h))
    for nm, dst, src in [("wq", wq_b, wq_h), ("wk", wk_b, wk_h),
                         ("wv", wv_b, wv_h), ("wg", wg_b, wg_h),
                         ("w0", w0_b, w0_h), ("w1", w1_b, w1_h),
                         ("wo", wo_b, wo_h)]:
        cast_done[nm] = ("g_" + nm, dma("gq", "g_" + nm, dst, src))

    with ExitStack() as ph:
        sb0 = lambda nm, s, d: ph.enter_context(nc.sbuf_tensor(nm, s, d))
        pre64 = sb0("p0_pre64", [64, 128], F32)
        pre16a = sb0("p0_pre16a", [16, 128], F32)
        pre16b = sb0("p0_pre16b", [16, 128], F32)
        pre1 = sb0("p0_pre1", [1, 128], F32)
        t_ident = dma("sync", "c0", sb_ident[:], ident)
        dma("sync", "c0", sb_mask[:], maskt)
        t_b0 = dma("sync", "c0", pre64[:], b0_h.rearrange("(c p) -> c p", p=128))
        t_b1 = dma("sync", "c0", pre16a[:], b1_h.rearrange("(c p) -> c p", p=128))
        t_bg = dma("sync", "c0", pre16b[:], bgk2_h.rearrange("(c p) -> c p", p=128))
        t_gn = dma("sync", "c0", pre1[:], gnorm_h.rearrange("(o d) -> o d", o=1))
        dma("gq", "g_b1r", b1row[:], b1_h.rearrange("(o d) -> o d", o=1))
        c0_all = n["c0"]
        p_b0 = op("pe", lambda e: e.transpose(psA[:, 0:64], pre64[:], sb_ident[0:64, 0:64]),
                  waits=[("c0", c0_all)])
        p_b1 = op("pe", lambda e: e.transpose(psA[:, 512:528], pre16a[:], sb_ident[0:16, 0:16]))
        p_bg = op("pe", lambda e: e.transpose(psA[:, 1024:1040], pre16b[:], sb_ident[0:16, 0:16]))
        p_gn = op("pe", lambda e: e.transpose(psA[:, 1536:1537], pre1[:], sb_ident[0:1, 0:1]))
        op("dve", lambda e: e.tensor_copy(bias0c[:], psA[:, 0:64]), waits=[("pe", p_b0)])
        op("dve", lambda e: e.tensor_copy(bias1c[:], psA[:, 512:528]), waits=[("pe", p_b1)])
        op("dve", lambda e: e.tensor_copy(bgk2c[:], psA[:, 1024:1040]), waits=[("pe", p_bg)])
        op("dve", lambda e: e.tensor_copy(gnormc[:], psA[:, 1536:1537]), waits=[("pe", p_gn)])
        op("dve", lambda e: e.memset(onesrow[:], 1.0))
        op("dve", lambda e: e.memset(cEPS[:], EPS))
        op("dve", lambda e: e.memset(cLNS[:], LNSCALE))
        barrier()

    xnp_es = ExitStack()
    xnp = [xnp_es.enter_context(nc.sbuf_tensor(f"c_xnp{i}", [128, 16 * 512], BF16))
           for i in range(2)]

    def load_panel(tb, waits, q2="sync"):
        b = tb % 2
        t = 0
        for k in range(16):
            t = dma(q2 if k >= 8 else "sync", f"pan{b}",
                    xnp[b][:, k * 512:(k + 1) * 512],
                    xn_tm[tb * 512:(tb + 1) * 512, k * 128:(k + 1) * 128],
                    waits=waits if k == 0 else [], transpose=True)
        return (f"pan{b}", t)

    with ExitStack() as ph:
        sb1 = lambda nm, s, d: ph.enter_context(nc.sbuf_tensor(nm, s, d))
        xld = [sb1(f"p1_x{i}", [128, D], F32) for i in range(3)]
        sq = [sb1(f"p1_sq{i}", [128, D], F32) for i in range(2)]
        xnst = [sb1(f"p1_xn{i}", [128, D], BF16) for i in range(2)]
        wqs_q = sb1("p3_wqs_wq", [128, 16 * 2048], BF16)
        qstage_q = [sb1(f"p3_st_wq{i}", [128, 512], BF16) for i in range(4)]
        sgst = [sb1(f"p4_sg{i}", [128, 512], F32) for i in range(2)]
        int_sb = sb1("p1_int", [16, T], BF16)
        dve_done = [0] * NT
        axn_mark = [0] * NT
        dma_out = [0] * NT

        def emit_p1_tile(i):
            b3, b2 = i % 3, i % 2
            c = 2 * b2
            w = []
            if i >= 3:
                w = [("dve", dve_done[i - 3]), ("act", axn_mark[i - 3])]
            ld = dma("sync", f"p1x{b3}", xld[b3][:], x_h[i * 128:(i + 1) * 128, :], waits=w)
            a_sq = op("act", lambda e, b3=b3, b2=b2: e.activation(sq[b2][:], xld[b3][:], AF.Square),
                      waits=[(f"p1x{b3}", ld)] + ([("dve", dve_done[i - 2])] if i >= 2 else []))
            op("dve", lambda e, b3=b3, c=c: e.tensor_reduce(
                ldstat[:, c:c + 1], xld[b3][:], AX.X, ALU.add),
                waits=[(f"p1x{b3}", ld)] + ([("act", axn_mark[i - 2])] if i >= 2 else []))
            op("dve", lambda e, b2=b2, c=c: e.tensor_reduce(
                ldstat[:, c + 1:c + 2], sq[b2][:], AX.X, ALU.add), waits=[("act", a_sq)])
            op("dve", lambda e, c=c: e.tensor_scalar(
                ldstat[:, c:c + 1], ldstat[:, c:c + 1], 1.0 / D, None, ALU.mult))
            op("dve", lambda e, c=c: e.tensor_scalar(
                ldstat[:, c + 1:c + 2], ldstat[:, c + 1:c + 2], 1.0 / D, None, ALU.mult))
            dnv = op("dve", lambda e, c=c: e.scalar_tensor_tensor(
                ldstat[:, c + 1:c + 2], ldstat[:, c:c + 1], ldstat[:, c:c + 1],
                ldstat[:, c + 1:c + 2], ALU.mult, ALU.subtract))
            op("act", lambda e, c=c: e.activation(
                ldstat[:, c + 1:c + 2], ldstat[:, c + 1:c + 2], AF.Ln, bias=cEPS[:], scale=-1.0),
                waits=[("dve", dnv)])
            a_rs = op("act", lambda e, c=c: e.activation(
                ldstat[:, c + 1:c + 2], ldstat[:, c + 1:c + 2], AF.Exp, bias=0.0, scale=-0.5))
            axn_mark[i] = a_rs
            op("dve", lambda e, c=c: e.scalar_tensor_tensor(
                ldstat[:, c:c + 1], ldstat[:, c:c + 1], ldstat[:, c + 1:c + 2],
                ldstat[:, c:c + 1], ALU.mult, ALU.bypass), waits=[("act", a_rs)])
            op("dve", lambda e, c=c: e.tensor_scalar(
                ldstat[:, c:c + 1], ldstat[:, c:c + 1], -1.0, None, ALU.mult))
            d_xn = op("dve", lambda e, b3=b3, b2=b2, c=c: e.scalar_tensor_tensor(
                xnst[b2][:], xld[b3][:], ldstat[:, c + 1:c + 2],
                ldstat.ap()[:, c:c + 1].broadcast_to((128, D)),
                ALU.mult, ALU.add),
                waits=([(f"p1w{b2}", dma_out[i - 2])] if i >= 2 else []))
            dve_done[i] = n["dve"]
            dma_out[i] = dma("act", f"p1w{b2}", xn_tm[i * 128:(i + 1) * 128, :], xnst[b2][:],
                             waits=[("dve", d_xn)])

        for i in range(8):
            emit_p1_tile(i)
        wld = 0
        for k in range(16):
            wld = dma("sync", "wld", wqs_q[:, k * 2048:(k + 1) * 2048],
                      wq_b[k * 128:(k + 1) * 128, :],
                      waits=[cast_done["wq"]] if k == 0 else [])
        pan_t = [0] * NTB
        pe_mark = [0] * NTB
        ev_mark = [0] * NTB
        st_dma = [0] * 4
        sg_dma4 = [0, 0]
        ev = [0] * (NTB * 16)
        aev4 = [0] * (NTB * 16)
        for tb in range(NTB):
            for i in range(4 * tb + 8, min(4 * tb + 12, NT)):
                emit_p1_tile(i)
            b = tb % 2
            w = [("p1w0", dma_out[4 * tb + 2]), ("p1w1", dma_out[4 * tb + 3])]
            if tb >= 2:
                w.append(("pe", pe_mark[tb - 2]))
            pan_t[tb] = load_panel(tb, w)
            slot2 = tb % 2
            w2 = [pan_t[tb]]
            if tb == 0:
                w2.append(cast_done["wgk1"])
            if tb >= 2:
                w2.append(("dve", ev_mark[tb - 2]))
            for k in range(16):
                op("pe", lambda e, slot2=slot2, k=k, b=b: e.matmul(
                    psB[0:16, slot2 * 512:(slot2 + 1) * 512],
                    wgk1s[:, k * 16:(k + 1) * 16], xnp[b][:, k * 512:(k + 1) * 512],
                    start=(k == 0), stop=(k == 15)), waits=w2 if k == 0 else [], inc=(k == 15))
            pe2 = n["pe"]
            ev_mark[tb] = op("dve", lambda e, slot2=slot2, tb=tb: e.tensor_copy(
                int_sb[:, tb * 512:(tb + 1) * 512], psB[0:16, slot2 * 512:(slot2 + 1) * 512]),
                waits=[("pe", pe2)])
            for h in range(16):
                slot = h % 4
                for k in range(16):
                    w = [pan_t[tb]]
                    if k == 0 and (tb * 16 + h) >= 4:
                        w.append(("dve", ev[tb * 16 + h - 4]))
                    if k == 0 and tb == 0 and h == 0:
                        w.append(("wld", wld))
                    op("pe", lambda e, slot=slot, k=k, h=h, b=b: e.matmul(
                        psA[:, slot * 512:(slot + 1) * 512],
                        wqs_q[:, k * 2048 + h * 128: k * 2048 + (h + 1) * 128],
                        xnp[b][:, k * 512:(k + 1) * 512],
                        start=(k == 0), stop=(k == 15)), waits=w, inc=(k == 15))
                pe_done = n["pe"]
                wv_ = [("pe", pe_done)]
                if (tb * 16 + h) >= 4:
                    wv_.append((f"st{slot}", st_dma[slot]))
                ev[tb * 16 + h] = op("dve", lambda e, slot=slot: e.tensor_copy(
                    qstage_q[slot][:], psA[:, slot * 512:(slot + 1) * 512]), waits=wv_)
                st_dma[slot] = dma("act", f"st{slot}",
                                   q_fm[h * 128:(h + 1) * 128, tb * 512:(tb + 1) * 512],
                                   qstage_q[slot][:], waits=[("dve", ev[tb * 16 + h])])
            for h in range(16):
                idx4 = tb * 16 + h
                s4 = idx4 % 2
                w4 = []
                if h == 0:
                    w4.append(("dve", ev_mark[tb]))
                    if tb == 0:
                        w4.append(cast_done["wgk2"])
                if idx4 >= 2:
                    w4.append(("act", aev4[idx4 - 2]))
                mm4 = op("pe", lambda e, h=h, tb=tb, s4=s4: e.matmul(
                    psB[:, 1024 + s4 * 512:1024 + (s4 + 1) * 512],
                    wgk2s[:, h * 128:(h + 1) * 128],
                    int_sb[:, tb * 512:(tb + 1) * 512],
                    start=True, stop=True), waits=w4)
                wv_ = [("pe", mm4)]
                if idx4 >= 2:
                    wv_.append((f"vst{s4}", sg_dma4[s4]))
                aev4[idx4] = op("act", lambda e, s4=s4, h=h: e.activation(
                    sgst[s4][:], psB[:, 1024 + s4 * 512:1024 + (s4 + 1) * 512], AF.Sigmoid,
                    bias=bgk2c[:, h:h + 1], scale=1.0), waits=wv_)
                sg_dma4[s4] = dma("act", f"vst{s4}",
                                  sg_fm[h * 128:(h + 1) * 128, tb * 512:(tb + 1) * 512],
                                  sgst[s4][:], waits=[("act", aev4[idx4])])
            pe_mark[tb] = n["pe"]
        pe_q_end = n["pe"]
        dve_q_end = n["dve"]
        act_q_end = n["act"]
        p1w_end = (n["p1w0"], n["p1w1"])
        stv_end = [(f"st{i}", n[f"st{i}"]) for i in range(4)] + \
                  [(f"vst{i}", n[f"vst{i}"]) for i in range(2)]

    with ExitStack() as ph:
        sb3 = lambda nm, s, d: ph.enter_context(nc.sbuf_tensor(nm, s, d))
        wqs_k = sb3("p3_wqs_wk", [128, 16 * 2048], BF16)
        qstage_k = [sb3(f"p3_st_wk{i}", [128, 512], BF16) for i in range(4)]
        wld = 0
        for k in range(16):
            w = []
            if k == 0:
                w = [cast_done["wk"], ("pe", pe_q_end), ("dve", dve_q_end),
                     ("act", act_q_end), ("p1w0", p1w_end[0]), ("p1w1", p1w_end[1])] \
                    + stv_end
            wld = dma("gq", "wld", wqs_k[:, k * 2048:(k + 1) * 2048],
                      wk_b[k * 128:(k + 1) * 128, :], waits=w)
        st_dma = [0] * 4
        ev = [0] * (NTB * 16)
        pe_mark = [0] * NTB
        for tb in range(NTB):
            b = tb % 2
            w = [("wld", wld)] if tb == 0 else []
            if tb >= 2:
                w.append(("pe", pe_mark[tb - 2]))
            else:
                w.append(("pe", pe_q_end))
            pan = load_panel(tb, w)
            for h in range(16):
                slot = h % 4
                for k in range(16):
                    w = [pan]
                    if k == 0 and (tb * 16 + h) >= 4:
                        w.append(("dve", ev[tb * 16 + h - 4]))
                    if k == 0 and tb == 0 and h < 4:
                        w.append(("dve", dve_q_end))
                    op("pe", lambda e, slot=slot, k=k, h=h, b=b: e.matmul(
                        psA[:, slot * 512:(slot + 1) * 512],
                        wqs_k[:, k * 2048 + h * 128: k * 2048 + (h + 1) * 128],
                        xnp[b][:, k * 512:(k + 1) * 512],
                        start=(k == 0), stop=(k == 15)), waits=w, inc=(k == 15))
                pe_done = n["pe"]
                wv_ = [("pe", pe_done)]
                if (tb * 16 + h) >= 4:
                    wv_.append((f"st{slot}", st_dma[slot]))
                else:
                    wv_ += stv_end
                ev[tb * 16 + h] = op("dve", lambda e, slot=slot: e.tensor_copy(
                    qstage_k[slot][:], psA[:, slot * 512:(slot + 1) * 512]), waits=wv_)
                st_dma[slot] = dma("act", f"st{slot}",
                                   k_fm[h * 128:(h + 1) * 128, tb * 512:(tb + 1) * 512],
                                   qstage_k[slot][:], waits=[("dve", ev[tb * 16 + h])])
            pe_mark[tb] = n["pe"]
        barrier()

    with ExitStack() as ph5:
        sb5 = lambda nm, s, d: ph5.enter_context(nc.sbuf_tensor(nm, s, d))
        qF = sb5("p5_qF", [128, T], BF16)
        kF = sb5("p5_kF", [128, T], BF16)
        sgF = sb5("p5_sgF", [128, T], F32)
        lnb = sb5("p5_lnb", [128, T], F32)
        fac = sb5("p5_fac", [128, T], BF16)
        fneg = sb5("p5_fneg", [128, T], BF16)
        qko = sb5("p5_qko", [128, T], BF16)
        kho = sb5("p5_kho", [128, T], BF16)
        ends = sb5("p5_ends", [128, NCB], F32)
        endsx = sb5("p5_endsx", [128, NCB + 2], F32)
        op("dve", lambda e: e.memset(endsx[:, 0:1], 0.0))
        p5st = {"d_gc": 0, "d_qg": 0, "d_kh": 0, "st_k": 0, "st_q": 0}

        def emit_p5_head(h):
            st = dict(p5st)
            tq = dma("sync", "p5q", qF[:], q_fm[h * 128:(h + 1) * 128, :],
                     waits=[("dve", st["d_qg"])] if h else [])
            tk = dma("sync", "p5k", kF[:], k_fm[h * 128:(h + 1) * 128, :],
                     waits=[("dve", st["d_kh"])] if h else [])
            tsg = dma("sync", "p5s", sgF[:], sg_fm[h * 128:(h + 1) * 128, :],
                      waits=[("dve", st["d_gc"])] if h else [])
            a_ln = op("act", lambda e: e.activation(lnb[:], sgF[:], AF.Ln),
                      waits=[("p5s", tsg)])
            op("dve", lambda e: e.tensor_tensor_scan(
                sgF[:], lnb[:], lnb[:], 0.0, ALU.add, ALU.bypass),
                waits=[("act", a_ln)])
            op("dve", lambda e: e.tensor_copy(ends[:], sgF.ap()[:, CB - 1:T:CB]))
            op("dve", lambda e: e.tensor_scalar(
                endsx[:, 1:NCB + 1], ends[:], 1.0 / 16.0, None, ALU.mult))
            obc = endsx.ap()[:, 0:NCB].unsqueeze(2).broadcast_to((128, NCB, CB))
            d_gc = op("dve", lambda e, obc=obc: e.scalar_tensor_tensor(
                lnb.ap().rearrange("p (c t) -> p c t", t=CB),
                sgF.ap().rearrange("p (c t) -> p c t", t=CB), 1.0 / 16.0,
                obc, ALU.mult, ALU.subtract))
            op("act", lambda e, h=h: e.activation(
                decay.ap()[:, h * NCB:(h + 1) * NCB].unsqueeze(2),
                lnb.ap().rearrange("p (c t) -> p c t", t=CB)[:, :, CB - 1:CB], AF.Exp),
                waits=[("dve", d_gc)])
            a_fp = op("act", lambda e: e.activation(fac[:], lnb[:], AF.Exp, bias=cLNS[:]),
                      waits=[("dve", st["d_kh"])] if h else [])
            wq_ = [("act", a_fp), ("p5q", tq)]
            if h:
                wq_.append(("p5wq", st["st_q"]))
            d_qg = op("dve", lambda e: e.tensor_tensor(qko[:], qF[:], fac[:], ALU.mult),
                      waits=wq_)
            st_q = dma("act", "p5wq", qg_fm[h * 128:(h + 1) * 128, :], qko[:],
                       waits=[("dve", d_qg)])
            a_fn = op("act", lambda e: e.activation(fneg[:], lnb[:], AF.Exp, scale=-1.0),
                      waits=[("dve", st["d_kh"])] if h else [])
            d_kh = op("dve", lambda e: e.tensor_tensor(kho[:], kF[:], fneg[:], ALU.mult),
                      waits=[("act", a_fn), ("p5k", tk)] +
                            ([("p5wk", st["st_k"])] if h else []))
            st_k = dma("act", "p5wk", kh_fm[h * 128:(h + 1) * 128, :],
                       kho[:],
                       waits=[("dve", d_kh)])
            p5st.update(d_gc=d_gc, d_qg=d_qg, d_kh=d_kh, st_k=st_k, st_q=st_q)

        for wsrc, dstf, castkey, silu, p5base in [(wv_b, v_tm, "wv", False, 0),
                                                  (wg_b, gate_tm, "wg", True, 8)]:
            with ExitStack() as ph:
                sb6 = lambda nm, s, d: ph.enter_context(nc.sbuf_tensor(nm, s, d))
                wqs = sb6(f"p6_w_{castkey}", [128, 16 * 2048], BF16)
                vstage = [sb6(f"p6_st_{castkey}{i}", [128, 2048], BF16) for i in range(2)]
                wld = 0
                for k in range(16):
                    wld = dma("sync", "wld", wqs[:, k * 2048:(k + 1) * 2048],
                              wsrc[k * 128:(k + 1) * 128, :],
                              waits=[cast_done[castkey]] if k == 0 else [])
                vst_dma = [0, 0]
                vev = [0] * NT
                pe_mark = [0] * NTB
                for tb in range(NTB):
                    b = tb % 2
                    w = [("wld", wld)] if tb == 0 else []
                    if tb >= 2:
                        w.append(("pe", pe_mark[tb - 2]))
                    pan = load_panel(tb, w)
                    for tt in range(4):
                        i = tb * 4 + tt
                        slot = i % 2
                        ps = psA if slot == 0 else psB
                        cstream = "act" if silu else "dve"
                        for k in range(16):
                            w = [pan]
                            if k == 0 and i >= 2:
                                w.append((cstream, vev[i - 2]))
                            for fg in range(4):
                                op("pe", lambda e, ps=ps, k=k, fg=fg, tt=tt, b=b, wqs=wqs: e.matmul(
                                    ps[:, fg * 512:(fg + 1) * 512],
                                    xnp[b][:, k * 512 + tt * 128: k * 512 + (tt + 1) * 128],
                                    wqs[:, k * 2048 + fg * 512: k * 2048 + (fg + 1) * 512],
                                    start=(k == 0), stop=(k == 15)),
                                    waits=w if fg == 0 else [], inc=(k == 15 and fg == 3))
                        pe_done = n["pe"]
                        wv_ = [("pe", pe_done)]
                        if i >= 2:
                            wv_.append((f"vst{slot}", vst_dma[slot]))
                        if silu:
                            vev[i] = op("act", lambda e, ps=ps, slot=slot, vstage=vstage: e.activation(
                                vstage[slot][:], ps[:], AF.Silu), waits=wv_)
                            vst_dma[slot] = dma("act", f"vst{slot}", dstf[i * 128:(i + 1) * 128, :],
                                                vstage[slot][:], waits=[("act", vev[i])])
                        else:
                            vev[i] = op("dve", lambda e, ps=ps, slot=slot, vstage=vstage: e.tensor_copy(
                                vstage[slot][:], ps[:]), waits=wv_)
                            vst_dma[slot] = dma("act", f"vst{slot}", dstf[i * 128:(i + 1) * 128, :],
                                                vstage[slot][:], waits=[("dve", vev[i])])
                    emit_p5_head(p5base + tb)
                    pe_mark[tb] = n["pe"]
                barrier()

    with ExitStack() as ph:
        sb8 = lambda nm, s, d: ph.enter_context(nc.sbuf_tensor(nm, s, d))
        w0t = [sb8(f"p8_w0t{i}", [128, 16 * 1024], BF16) for i in range(2)]
        gst = [sb8(f"p8_gst{i}", [128, 512], BF16) for i in range(4)]
        gst_dma = [0] * 4
        gev = [0] * (8 * NTB * 8)
        oct_pe_mark = [0] * 8
        blk_pe_mark = [0] * (8 * NTB)
        for oct_ in range(8):
            ob = oct_ % 2
            w0ld = 0
            for k in range(16):
                w = []
                if k == 0:
                    w = [cast_done["w0"]]
                    if oct_ >= 2:
                        w.append(("pe", oct_pe_mark[oct_ - 2]))
                w0ld = dma("sync", f"w0t{ob}", w0t[ob][:, k * 1024:(k + 1) * 1024],
                           w0_b[oct_ * 1024:(oct_ + 1) * 1024, k * 128:(k + 1) * 128],
                           waits=w, transpose=True)
            for tb in range(NTB):
                blk = oct_ * NTB + tb
                b = tb % 2
                w = [(f"w0t{ob}", w0ld)]
                if blk >= 2:
                    w.append(("pe", blk_pe_mark[blk - 2]))
                pan = load_panel(tb, w)
                for ns in range(8):
                    idx = blk * 8 + ns
                    slot = idx % 4
                    psl = psA if slot < 2 else psB
                    pcol = (slot % 2) * 512
                    for k in range(16):
                        w = [pan]
                        if k == 0 and idx >= 4:
                            w.append(("act", gev[idx - 4]))
                        op("pe", lambda e, psl=psl, pcol=pcol, k=k, ns=ns, ob=ob, b=b: e.matmul(
                            psl[:, pcol:pcol + 512],
                            w0t[ob][:, k * 1024 + ns * 128: k * 1024 + (ns + 1) * 128],
                            xnp[b][:, k * 512:(k + 1) * 512],
                            start=(k == 0), stop=(k == 15)), waits=w, inc=(k == 15))
                    pe_done = n["pe"]
                    wv_ = [("pe", pe_done)]
                    if idx >= 4:
                        wv_.append((f"st{slot}", gst_dma[slot]))
                    nchunk = oct_ * 8 + ns
                    gev[idx] = op("act", lambda e, psl=psl, pcol=pcol, slot=slot, nchunk=nchunk:
                                  e.activation(gst[slot][:], psl[:, pcol:pcol + 512], AF.Gelu,
                                               bias=bias0c[:, nchunk:nchunk + 1]), waits=wv_)
                    gst_dma[slot] = dma(
                        "act", f"st{slot}",
                        d1_fm[nchunk * 128:(nchunk + 1) * 128, tb * 512:(tb + 1) * 512],
                        gst[slot][:], waits=[("act", gev[idx])])
                blk_pe_mark[blk] = n["pe"]
            oct_pe_mark[oct_] = n["pe"]
        barrier()

    xnp_es.close()

    with ExitStack() as ph:
        sb9 = lambda nm, s, d: ph.enter_context(nc.sbuf_tensor(nm, s, d))
        w1t = [sb9(f"p9_w1t{i}", [128, 16 * 2048], BF16) for i in range(2)]
        d1c = [sb9(f"p9_d1c{i}", [128, 16 * 128], BF16) for i in range(2)]
        est = [sb9(f"p9_est{i}", [128, 2048], F32) for i in range(2)]
        epld = sb9("p9_ep", [128, 2048], F32)
        gsil = sb9("p9_gs", [128, 2048], BF16)
        est_dma = [0, 0]
        gsil_dma = [0, 0]
        sev = [0] * NT
        w1ld = [0, 0]
        qtr_pe_end = [0] * 4
        prev_eev = [0] * NT

        def load_w1_qtr(qtr, pe_guard):
            par = qtr % 2
            t = 0
            for ncl in range(16):
                w = []
                if ncl == 0:
                    w = [cast_done["w1"]]
                    if pe_guard:
                        w.append(("pe", pe_guard))
                t = dma("sync", "wld", w1t[par][:, ncl * 2048:(ncl + 1) * 2048],
                        w1_b[:, (qtr * 16 + ncl) * 128:(qtr * 16 + ncl + 1) * 128],
                        waits=w, transpose=True)
            w1ld[par] = t

        load_w1_qtr(0, 0)
        for qtr in range(4):
            par = qtr % 2
            eev = [0] * NT
            tt_pe_mark = [0] * NT
            d1ld = [0, 0]
            for tt in range(NT):
                b = tt % 2
                w = [("wld", w1ld[par])] if tt == 0 else []
                if tt >= 2:
                    w.append(("pe", tt_pe_mark[tt - 2]))
                elif qtr > 0:
                    w.append(("pe", qtr_pe_end[qtr - 1]))
                srcap = d1_fm.rearrange("(c p) t -> p c t", p=128)[
                    :, qtr * 16:(qtr + 1) * 16, tt * 128:(tt + 1) * 128]
                d1ld[b] = dma("sync", f"d1c{b}",
                              d1c[b].ap().rearrange("p (c t) -> p c t", t=128), srcap, waits=w)
                if tt == 2 and qtr < 3:
                    load_w1_qtr(qtr + 1, qtr_pe_end[qtr - 1] if qtr >= 1 else 0)
                slot = tt % 2
                ps = psA if slot == 0 else psB
                if qtr == 0:
                    for fg in range(4):
                        op("pe", lambda e, ps=ps, fg=fg: e.matmul(
                            ps[:, fg * 512:(fg + 1) * 512], onesrow[0:1, 0:128],
                            b1row[0:1, fg * 512:(fg + 1) * 512],
                            start=True, stop=False),
                            waits=([(f"d1c{b}", d1ld[b])] + ([("dve", eev[tt - 2])] if tt >= 2 else []))
                            if fg == 0 else [], inc=False)
                for ncl in range(16):
                    w = []
                    if ncl == 0 and qtr != 0:
                        w = [(f"d1c{b}", d1ld[b])]
                        if tt >= 2:
                            w.append(("dve", eev[tt - 2]))
                        else:
                            w.append(("dve", prev_eev[NT - 2 + tt]))
                    for fg in range(4):
                        op("pe", lambda e, ps=ps, ncl=ncl, fg=fg, b=b, qtr=qtr, par=par: e.matmul(
                            ps[:, fg * 512:(fg + 1) * 512],
                            d1c[b][:, ncl * 128:(ncl + 1) * 128],
                            w1t[par][:, ncl * 2048 + fg * 512: ncl * 2048 + (fg + 1) * 512],
                            start=(ncl == 0 and qtr != 0), stop=(ncl == 15)),
                            waits=w if fg == 0 else [], inc=(ncl == 15 and fg == 3))
                pe_done = n["pe"]
                tt_pe_mark[tt] = pe_done
                wv_ = [("pe", pe_done)]
                if est_dma[slot]:
                    wv_.append((f"gacc{slot}", est_dma[slot]))
                eev[tt] = op("dve", lambda e, ps=ps, slot=slot: e.tensor_copy(
                    est[slot][:], ps[:]), waits=wv_)
                kw = {"accum_op": ALU.add} if qtr > 0 else {}
                est_dma[slot] = dma("gq", f"gacc{slot}", e_part[tt * 128:(tt + 1) * 128, :],
                                    est[slot][:], waits=[("dve", eev[tt])], **kw)
                if qtr == 3:
                    w = [(f"gacc{slot}", est_dma[slot])]
                    if tt >= 1:
                        w.append(("act", sev[tt - 1]))
                    tld = dma("gq", "p9e0", epld[:],
                              e_part[tt * 128:(tt + 1) * 128, :], waits=w)
                    wv_ = [("p9e0", tld)]
                    if tt >= 1:
                        wv_.append(("p9g0", gsil_dma[0]))
                    sev[tt] = op("act", lambda e: e.activation(
                        gsil[:], epld[:], AF.Silu), waits=wv_)
                    gsil_dma[0] = dma("act", "p9g0", g_tm[tt * 128:(tt + 1) * 128, :],
                                      gsil[:], waits=[("act", sev[tt])])
            qtr_pe_end[qtr] = n["pe"]
            prev_eev = eev
        barrier()

    with ExitStack() as ph:
        sb10 = lambda nm, s, d: ph.enter_context(nc.sbuf_tensor(nm, s, d))
        S32 = sb10("c_S32", [128, H * HD], F32)
        S16 = [sb10("c_S16a", [128, H * HD], BF16), sb10("c_S16b", [128, H * HD], BF16)]
        op("dve", lambda e: e.memset(S32[:], 0.0))
        op("dve", lambda e: e.memset(S16[0][:], 0.0))
        p10_init = op("dve", lambda e: e.memset(S16[1][:], 0.0))
        qg2 = [sb10(f"p10_qg{i}", [128, H * 128], BF16) for i in range(3)]
        kh2 = [sb10(f"p10_kh{i}", [128, H * 128], BF16) for i in range(3)]
        kt2 = [sb10(f"p10_kt{i}", [128, 2048], BF16) for i in range(3)]
        v2 = [sb10(f"p10_v{i}", [128, 2048], BF16) for i in range(3)]
        atm = [sb10(f"p10_atm{i}", [128, 2048], BF16) for i in range(2)]
        ost = [sb10(f"p10_ost{i}", [128, 2048], BF16) for i in range(2)]
        otl = [sb10(f"p11_o{i}", [128, 2048], BF16) for i in range(2)]
        gtl = [sb10(f"p11_g{i}", [128, 2048], BF16) for i in range(2)]
        sqt = [sb10(f"p11_sq{i}", [128, 2048], F32) for i in range(2)]
        rst = sb10("p11_rs", [128, 32], BF16)
        ogt = [sb10(f"p11_og{i}", [128, 2048], BF16) for i in range(2)]
        ldk = [0, 0, 0]
        o_dma = [0, 0]
        og_dma = [0, 0]
        rev = [0] * NCB
        iter_pe_end = [0] * (NCB + 1)
        s16_done = [0] * NCB
        fold_done = [[0, 0] for _ in range(NCB)]
        mask_done = [[0, 0] for _ in range(NCB)]
        inter_pe = [0] * NCB
        o_evac = [0] * NCB
        mbc = sb_mask.ap().unsqueeze(1).broadcast_to((128, 8, CB))
        for i in range(NCB + 1):
            if i < NCB:
                b = i % 3
                w = []
                if i >= 3:
                    w = [("pe", iter_pe_end[i - 2])]
                t0 = i * 128
                dma("sync", f"sc{b % 2}",
                    qg2[b].ap().rearrange("p (h t) -> p h t", t=128),
                    qg_fm.rearrange("(h d) t -> d h t", d=128)[:, :, t0:t0 + 128], waits=w)
                dma("sync", f"sc{b % 2}",
                    kh2[b].ap().rearrange("p (h t) -> p h t", t=128),
                    kh_fm.rearrange("(h d) t -> d h t", d=128)[:, :, t0:t0 + 128])
                dma("sync", f"sc{b % 2}", kt2[b][:], kh_fm[:, t0:t0 + 128], transpose=True)
                ldk[b] = dma("sync", f"sc{b % 2}", v2[b][:], v_tm[t0:t0 + 128, :])
                for uh in range(2):
                    base = 1024 - uh * 1024
                    for hh in range(8):
                        h = uh * 8 + hh
                        w = []
                        if hh == 0:
                            w = [(f"sc{b % 2}", ldk[b])]
                            if i >= 1:
                                w.append(("dve", mask_done[i - 1][uh]))
                        op("pe", lambda e, h=h, hh=hh, b=b, base=base: e.matmul(
                            psB[:, base + hh * 128: base + (hh + 1) * 128],
                            kt2[b][:, h * 128:(h + 1) * 128],
                            v2[b][:, h * 128:(h + 1) * 128],
                            start=True, stop=True), waits=w, inc=(hh == 7))
                pe_u = n["pe"]
            if i >= 1:
                c1 = i - 1
                b1 = c1 % 3
                par1 = c1 % 2
                for h in range(16):
                    w = []
                    if h == 0:
                        if c1 >= 1:
                            w.append(("act", s16_done[c1 - 1]))
                        else:
                            w.append(("dve", p10_init))
                        if c1 >= 1:
                            w.append(("act", o_evac[c1 - 1]))
                    op("pe", lambda e, h=h, b1=b1, par1=par1: e.matmul(
                        psA[:, h * 128:(h + 1) * 128],
                        qg2[b1][:, h * 128:(h + 1) * 128],
                        S16[par1][:, h * 128:(h + 1) * 128],
                        start=(h % 4 == 0), stop=False), waits=w, inc=False)
                inter_pe[c1] = n["pe"]
                for h in range(16):
                    w = []
                    if h == 0:
                        w = [("dve", mask_done[c1][0])]
                    if h == 8:
                        w = [("dve", mask_done[c1][1])]
                    op("pe", lambda e, h=h, b1=b1, par1=par1: e.matmul(
                        psA[:, h * 128:(h + 1) * 128],
                        atm[par1][:, h * 128:(h + 1) * 128],
                        v2[b1][:, h * 128:(h + 1) * 128],
                        start=False, stop=True), waits=w, inc=(h == 15))
                pe_o = n["pe"]
            if i < NCB:
                for uh in range(2):
                    base = 1024 - uh * 1024
                    hc0 = uh * 1024
                    w = [("pe", pe_u)]
                    if uh == 0 and i >= 1:
                        w.append(("act", s16_done[i - 1]))
                    op("dve", lambda e, hc0=hc0, base=base: e.tensor_tensor(
                        S32[:, hc0:hc0 + 1024], S32[:, hc0:hc0 + 1024],
                        psB[:, base:base + 1024], ALU.add), waits=w)
                    dbc = decay.ap()[:, i::NCB][:, uh * 8:(uh + 1) * 8].unsqueeze(2)\
                        .broadcast_to((128, 8, 128))
                    fold_done[i][uh] = op("dve", lambda e, hc0=hc0, dbc=dbc: e.tensor_tensor(
                        S32.ap()[:, hc0:hc0 + 1024].rearrange("p (h v) -> p h v", v=128),
                        S32.ap()[:, hc0:hc0 + 1024].rearrange("p (h v) -> p h v", v=128),
                        dbc, ALU.mult))
                for uh in range(2):
                    hc0 = uh * 1024
                    w = [("dve", fold_done[i][uh])]
                    if uh == 0 and i >= 1:
                        w.append(("pe", inter_pe[i - 1]))
                    op("act", lambda e, hc0=hc0, i=i: e.activation(
                        S16[1 - (i % 2)][:, hc0:hc0 + 1024], S32[:, hc0:hc0 + 1024],
                        AF.Copy), waits=w)
                s16_done[i] = n["act"]
                for ab in range(2):
                    base = 1024 - ab * 1024
                    for hh in range(8):
                        h = ab * 8 + hh
                        w = [("dve", fold_done[i][ab])] if hh == 0 else []
                        op("pe", lambda e, h=h, hh=hh, b=b, base=base: e.matmul(
                            psB[:, base + hh * 128: base + (hh + 1) * 128],
                            kh2[b][:, h * 128:(h + 1) * 128],
                            qg2[b][:, h * 128:(h + 1) * 128],
                            start=True, stop=True), waits=w, inc=(hh == 7))
                    pe_at = n["pe"]
                    mask_done[i][ab] = op("dve", lambda e, i=i, ab=ab, base=base: e.tensor_tensor(
                        atm[i % 2].ap()[:, ab * 1024:(ab + 1) * 1024].rearrange(
                            "p (h c) -> p h c", c=128),
                        psB.ap()[:, base:base + 1024].rearrange("p (h c) -> p h c", c=128),
                        mbc, ALU.mult), waits=[("pe", pe_at)])
            if i >= 1:
                wv_ = [("pe", pe_o)]
                if c1 >= 2:
                    wv_.append((f"ow{par1}", o_dma[par1]))
                a_o = op("act", lambda e, par1=par1: e.activation(
                    ost[par1][:], psA[:], AF.Copy), waits=wv_)
                o_evac[c1] = a_o
                o_dma[par1] = dma("act", f"ow{par1}", o_tm[c1 * 128:(c1 + 1) * 128, :],
                                  ost[par1][:], waits=[("act", a_o)])
                tt = c1
                pb = par1
                c0 = pb * 16
                w = [(f"ow{pb}", o_dma[pb])]
                if tt >= 2:
                    w.append(("dve", rev[tt - 2]))
                tlo = dma("sync", f"p11a{pb}", otl[pb][:], o_tm[tt * 128:(tt + 1) * 128, :],
                          waits=w)
                tlg = dma("sync", f"p11b{pb}", gtl[pb][:], gate_tm[tt * 128:(tt + 1) * 128, :])
                a_sq = op("act", lambda e, pb=pb: e.activation(sqt[pb][:], otl[pb][:], AF.Square),
                          waits=[(f"p11a{pb}", tlo)] + ([("dve", rev[tt - 2])] if tt >= 2 else []))
                d_rd = op("dve", lambda e, pb=pb, c0=c0: e.tensor_reduce(
                    ldstat[:, c0:c0 + 16],
                    sqt[pb].ap().rearrange("p (h d) -> p h d", d=128), AX.X, ALU.add),
                    waits=[("act", a_sq)])
                op("act", lambda e, c0=c0: e.activation(
                    ldstat[:, c0:c0 + 16], ldstat[:, c0:c0 + 16], AF.Ln,
                    bias=cEPS[:], scale=1.0 / 128.0), waits=[("dve", d_rd)])
                a_r = op("act", lambda e, c0=c0: e.activation(
                    rst[:, c0:c0 + 16], ldstat[:, c0:c0 + 16], AF.Exp, scale=-0.5),
                    waits=([("dve", rev[tt - 2])] if tt >= 2 else []))
                rbc = rst.ap()[:, c0:c0 + 16].unsqueeze(2).broadcast_to((128, 16, 128))
                wv_ = [("act", a_r)]
                if tt >= 2:
                    wv_.append((f"ogw{pb}", og_dma[pb]))
                op("dve", lambda e, pb=pb, rbc=rbc: e.tensor_tensor(
                    ogt[pb].ap().rearrange("p (h d) -> p h d", d=128),
                    otl[pb].ap().rearrange("p (h d) -> p h d", d=128),
                    rbc, ALU.mult), waits=wv_)
                rev[tt] = op("dve", lambda e, pb=pb: e.tensor_tensor(
                    ogt[pb][:], ogt[pb][:], gtl[pb][:], ALU.mult), waits=[(f"p11b{pb}", tlg)])
                og_dma[pb] = dma("act", f"ogw{pb}", og_tm[tt * 128:(tt + 1) * 128, :], ogt[pb][:],
                                 waits=[("dve", rev[tt])])
            iter_pe_end[i] = n["pe"]
        barrier()

    with ExitStack() as ph:
        sb12 = lambda nm, s, d: ph.enter_context(nc.sbuf_tensor(nm, s, d))
        wos = sb12("p12_wos", [128, 16 * 2048], BF16)
        ogf = sb12("p12_ogf", [128, 16 * 1024], BF16)
        xtl = [sb12(f"p12_x{i}", [128, 2048], F32) for i in range(2)]
        gft = [sb12(f"p12_gf{i}", [128, 2048], BF16) for i in range(2)]
        outs = [sb12(f"p12_out{i}", [128, 2048], F32) for i in range(2)]
        for k in range(16):
            tw = dma("sync", "wld", wos[:, k * 2048:(k + 1) * 2048],
                     wo_b[k * 128:(k + 1) * 128, :],
                     waits=[cast_done["wo"]] if k == 0 else [])
        wo_all = n["wld"]
        for k in range(16):
            op("dve", lambda e, k=k: e.tensor_scalar(
                wos[:, k * 2048:(k + 1) * 2048], wos[:, k * 2048:(k + 1) * 2048],
                gnormc[:, 0:1], None, ALU.mult), waits=[("wld", wo_all)] if k == 0 else [])
        wo_ready = n["dve"]
        if dbg_wos is not None:
            for k in range(16):
                dma("act", "outw0", dbg_wos[k * 128:(k + 1) * 128, :],
                    wos[:, k * 2048:(k + 1) * 2048], waits=[("dve", wo_ready)])
        out_dma = [0, 0]
        cev = [0] * NT
        qtr_pe_mark = [0] * 4
        for qtr in range(4):
            ogld = 0
            for h in range(16):
                w = []
                if h == 0 and qtr >= 1:
                    w = [("pe", qtr_pe_mark[qtr - 1])]
                ogld = dma("sync", "ogf", ogf[:, h * 1024:(h + 1) * 1024],
                           og_tm[qtr * 1024:(qtr + 1) * 1024, h * 128:(h + 1) * 128],
                           waits=w, transpose=True)
            for tt8 in range(8):
                tt = qtr * 8 + tt8
                b = tt % 2
                w = [("dve", cev[tt - 2])] if tt >= 2 else []
                tx = dma("sync", f"px{b}", xtl[b][:], x_h[tt * 128:(tt + 1) * 128, :], waits=w)
                tg = dma("sync", f"pg{b}", gft[b][:], g_tm[tt * 128:(tt + 1) * 128, :])
                slot = tt % 2
                ps = psA if slot == 0 else psB
                for h in range(16):
                    w = []
                    if h == 0:
                        w = [("ogf", ogld), ("dve", wo_ready)]
                        if tt >= 2:
                            w.append(("dve", cev[tt - 2]))
                    for fg in range(4):
                        op("pe", lambda e, ps=ps, h=h, fg=fg, tt8=tt8: e.matmul(
                            ps[:, fg * 512:(fg + 1) * 512],
                            ogf[:, h * 1024 + tt8 * 128: h * 1024 + (tt8 + 1) * 128],
                            wos[:, h * 2048 + fg * 512: h * 2048 + (fg + 1) * 512],
                            start=(h == 0), stop=(h == 15)),
                            waits=w if fg == 0 else [], inc=(h == 15 and fg == 3))
                pe_done = n["pe"]
                wm_ = [("pe", pe_done), (f"pg{b}", tg)]
                if tt >= 2:
                    wm_.append((f"outw{b}", out_dma[b]))
                d_m = op("dve", lambda e, ps=ps, b=b: e.tensor_tensor(
                    outs[b][:], ps[:], gft[b][:], ALU.mult), waits=wm_)
                cev[tt] = op("dve", lambda e, b=b: e.tensor_tensor(
                    outs[b][:], outs[b][:], xtl[b][:], ALU.add),
                    waits=[(f"px{b}", tx), ("dve", d_m)])
                out_dma[b] = dma("act", f"outw{b}", out_h[tt * 128:(tt + 1) * 128, :], outs[b][:],
                                 waits=[("dve", cev[tt])])
            qtr_pe_mark[qtr] = n["pe"]
        waitonly("act", [("outw0", n["outw0"]), ("outw1", n["outw1"])])
        barrier()

    with nc.Block() as block:
        @block.tensor
        def _(e):
            for f in streams["pe"]:
                f(e)

        @block.vector
        def _(e):
            for f in streams["dve"]:
                f(e)

        @block.scalar
        def _(e):
            for f in streams["act"]:
                f(e)

        @block.sync
        def _(e):
            for f in streams["sync"]:
                f(e)

        @block.gpsimd
        def _(e):
            for f in streams["gq"]:
                f(e)

    es.close()
    return nc


def kernel(L0_attn_x, L0_attn_Wq, L0_attn_Wk, L0_attn_Wv, L0_attn_Wgk1, L0_attn_Wgk2,
           L0_attn_bgk2, L0_attn_Wg, L0_attn_gnorm_w, L0_attn_Wo,
           L0_ffn_W0, L0_ffn_bias0, L0_ffn_W1, L0_ffn_bias1):
    nc = build_program()
    wmap = {
        "Wq": np.ascontiguousarray(np.asarray(L0_attn_Wq, np.float32)),
        "Wk": np.ascontiguousarray(np.asarray(L0_attn_Wk, np.float32)),
        "Wv": np.ascontiguousarray(np.asarray(L0_attn_Wv, np.float32)),
        "Wgk1": np.ascontiguousarray(np.asarray(L0_attn_Wgk1, np.float32)),
        "Wgk2": np.ascontiguousarray(np.asarray(L0_attn_Wgk2, np.float32)),
        "bgk2": np.ascontiguousarray(np.asarray(L0_attn_bgk2, np.float32)),
        "Wg": np.ascontiguousarray(np.asarray(L0_attn_Wg, np.float32)),
        "gnorm": np.ascontiguousarray(np.asarray(L0_attn_gnorm_w, np.float32)),
        "Wo": np.ascontiguousarray(np.asarray(L0_attn_Wo, np.float32)),
        "W0": np.ascontiguousarray(np.asarray(L0_ffn_W0, np.float32)),
        "bias0": np.ascontiguousarray(np.asarray(L0_ffn_bias0, np.float32)),
        "W1": np.ascontiguousarray(np.asarray(L0_ffn_W1, np.float32)),
        "bias1": np.ascontiguousarray(np.asarray(L0_ffn_bias1, np.float32)),
    }
    xfull = np.asarray(L0_attn_x, np.float32)
    in_maps = [dict(wmap, x=np.ascontiguousarray(xfull[i])) for i in range(8)]
    res = run_bass_kernel_spmd(nc, in_maps, core_ids=list(range(8)))
    return np.stack([r["out"] for r in res.results], axis=0)


# version 19
# speedup vs baseline: 1.1148x; 1.1148x over previous
import numpy as np
import concourse.bass as bass
import concourse.mybir as mybir
from concourse.bass_utils import run_bass_kernel_spmd
from contextlib import ExitStack

F32 = mybir.dt.float32
BF16 = mybir.dt.bfloat16
AF = mybir.ActivationFunctionType
ALU = mybir.AluOpType
AX = mybir.AxisListType

T = 4096
D = 2048
H = 16
HD = 128
CH = 64
NCH = T // CH
CB = 128
NCB = T // CB
DFF = 8192
NT = T // 128
NTB = T // 512
LNSCALE = float(np.log(HD ** -0.5))
EPS = 1e-5

DEBUG_TAPS = []


def build_program():
    nc = bass.Bass("TRN2", target_bir_lowering=False, debug=False)

    x_h = nc.dram_tensor("x", [T, D], F32, kind="ExternalInput").ap()
    wq_h = nc.dram_tensor("Wq", [D, D], F32, kind="ExternalInput").ap()
    wk_h = nc.dram_tensor("Wk", [D, D], F32, kind="ExternalInput").ap()
    wv_h = nc.dram_tensor("Wv", [D, D], F32, kind="ExternalInput").ap()
    wgk1_h = nc.dram_tensor("Wgk1", [D, 16], F32, kind="ExternalInput").ap()
    wgk2_h = nc.dram_tensor("Wgk2", [16, D], F32, kind="ExternalInput").ap()
    bgk2_h = nc.dram_tensor("bgk2", [D], F32, kind="ExternalInput").ap()
    wg_h = nc.dram_tensor("Wg", [D, D], F32, kind="ExternalInput").ap()
    gnorm_h = nc.dram_tensor("gnorm", [HD], F32, kind="ExternalInput").ap()
    wo_h = nc.dram_tensor("Wo", [D, D], F32, kind="ExternalInput").ap()
    w0_h = nc.dram_tensor("W0", [DFF, D], F32, kind="ExternalInput").ap()
    b0_h = nc.dram_tensor("bias0", [DFF], F32, kind="ExternalInput").ap()
    w1_h = nc.dram_tensor("W1", [D, DFF], F32, kind="ExternalInput").ap()
    b1_h = nc.dram_tensor("bias1", [D], F32, kind="ExternalInput").ap()
    out_h = nc.dram_tensor("out", [T, D], F32, kind="ExternalOutput").ap()

    def scratch(name, shape, dtype):
        kind = "ExternalOutput" if name in DEBUG_TAPS else "Internal"
        return nc.dram_tensor(name, shape, dtype, kind=kind).ap()

    xn_tm = scratch("xn_tm", [T, D], BF16)
    wq_b = scratch("wq_b", [D, D], BF16)
    wk_b = scratch("wk_b", [D, D], BF16)
    wv_b = scratch("wv_b", [D, D], BF16)
    wg_b = scratch("wg_b", [D, D], BF16)
    wo_b = scratch("wo_b", [D, D], BF16)
    w0_b = scratch("w0_b", [DFF, D], BF16)
    w1_b = scratch("w1_b", [D, DFF], BF16)
    q_fm = scratch("q_fm", [D, T], BF16)
    k_fm = scratch("k_fm", [D, T], BF16)
    sg_fm = scratch("sg_fm", [D, T], F32)
    qg_sc = scratch("qg_sc", [HD, NCB * H * 128], BF16)
    kh_sc = scratch("kh_sc", [HD, NCB * H * 128], BF16)
    kh_fm = scratch("kh_fm", [D, T], BF16)
    v_tm = scratch("v_tm", [T, D], BF16)
    gate_tm = scratch("gate_tm", [T, D], BF16)
    d1_fm = scratch("d1_fm", [DFF, T], BF16)
    e_part = scratch("e_part", [T, D], F32)
    g_tm = scratch("g_tm", [T, D], BF16)
    o_tm = scratch("o_tm", [T, D], BF16)
    og_tm = scratch("og_tm", [T, D], BF16)
    dbg_wos = scratch("dbg_wos", [D, D], BF16) if "dbg_wos" in DEBUG_TAPS else None
    dbg_scan = scratch("dbg_scan", [D, T], F32) if "dbg_scan" in DEBUG_TAPS else None
    dbg_gc = scratch("dbg_gc", [D, T], F32) if "dbg_gc" in DEBUG_TAPS else None

    ident = nc.inline_tensor(np.eye(128, dtype=np.float32), "ident").ap()
    maskt = nc.inline_tensor(np.triu(np.ones((CB, CB), np.float32)), "maskt").ap()

    es = ExitStack()
    sbt = lambda name, shape, dt: es.enter_context(nc.sbuf_tensor(name, shape, dt))
    pst = lambda name, shape: es.enter_context(nc.psum_tensor(name, shape, F32))

    sb_ident = sbt("c_ident", [128, 128], F32)
    sb_mask = sbt("c_mask", [CB, CB], F32)
    bias0c = sbt("c_bias0c", [128, 64], F32)
    bias1c = sbt("c_bias1c", [128, 16], F32)
    bgk2c = sbt("c_bgk2c", [128, 16], F32)
    gnormc = sbt("c_gnormc", [128, 1], F32)
    b1row = sbt("c_b1row", [1, D], BF16)
    onesrow = sbt("c_onesrow", [1, 128], BF16)
    wgk1s = sbt("c_wgk1s", [128, 16 * 16], BF16)
    wgk2s = sbt("c_wgk2s", [16, D], BF16)
    decay = sbt("c_decay", [128, H * NCB], F32)
    ldstat = sbt("c_ldstat", [128, 32], F32)
    cEPS = sbt("c_eps", [128, 1], F32)
    cLNS = sbt("c_lns", [128, 1], F32)

    psA = pst("psA", [128, 2048])
    psB = pst("psB", [128, 2048])

    semnames = (["pe", "dve", "act", "c0", "g_b1r", "wld", "ogf"]
                + ["g_" + k for k in ["wq", "wk", "wv", "wg", "w0", "w1", "wo",
                                      "wgk1", "wgk2"]]
                + [f"p1x{i}" for i in range(3)] + [f"p1w{i}" for i in range(2)]
                + [f"pan{i}" for i in range(2)] + [f"st{i}" for i in range(4)]
                + ["p5q", "p5k", "p5s", "p5wq", "p5wk"]
                + [f"vst{i}" for i in range(2)] + [f"w0t{i}" for i in range(2)]
                + [f"d1c{i}" for i in range(2)] + [f"gacc{i}" for i in range(2)]
                + [f"p9e{i}" for i in range(2)] + [f"p9g{i}" for i in range(2)]
                + [f"sc{i}" for i in range(2)] + [f"v1{i}" for i in range(2)]
                + [f"ow{i}" for i in range(2)]
                + [f"p11a{i}" for i in range(2)] + [f"p11b{i}" for i in range(2)]
                + [f"ogw{i}" for i in range(2)]
                + [f"px{i}" for i in range(2)] + [f"pg{i}" for i in range(2)]
                + [f"outw{i}" for i in range(2)])
    sm = {k: es.enter_context(nc.semaphore(k)) for k in semnames}
    n = {k: 0 for k in sm}
    streams = {k: [] for k in ["pe", "dve", "act", "sync", "gq"]}

    def dma(stream, qsem, out, in_, waits=(), **kw):
        n[qsem] += 16
        tgt = n[qsem]

        def f(e, out=out, in_=in_, waits=tuple(waits), kw=dict(kw)):
            for s, v in waits:
                if v > 0:
                    e.wait_ge(sm[s], v)
            e.dma_start(out=out, in_=in_, **kw).then_inc(sm[qsem], 16)
        streams[stream].append(f)
        return tgt

    def op(stream, fn, waits=(), inc=None):
        csem = {"pe": "pe", "dve": "dve", "act": "act"}[stream]
        do_inc = inc is not False
        prior = n[csem]
        if do_inc:
            n[csem] += 1
        tgt = n[csem]
        selfw = (csem, prior) if stream in ("dve", "act") else None

        def f(e, fn=fn, waits=tuple(waits), do_inc=do_inc, selfw=selfw):
            if selfw is not None and selfw[1] > 0:
                e.wait_ge(sm[selfw[0]], selfw[1])
            for s, v in waits:
                if v > 0:
                    e.wait_ge(sm[s], v)
            inst = fn(e)
            if do_inc:
                inst.then_inc(sm[csem], 1)
        streams[stream].append(f)
        return tgt

    def waitonly(stream, waits):
        def f(e, waits=tuple(waits)):
            for s, v in waits:
                if v > 0:
                    e.wait_ge(sm[s], v)
        streams[stream].append(f)

    CASTS = ("g_wq", "g_wk", "g_wv", "g_wg", "g_w0", "g_w1", "g_wo",
             "g_wgk1", "g_wgk2")

    def barrier():
        snap = {k: v for k, v in n.items() if k not in CASTS}
        for s in streams:
            waitonly(s, list(snap.items()))

    dma("gq", "g_b1r", b1row[:], b1_h.rearrange("(o d) -> o d", o=1))
    cast_done = {}
    cast_done["wgk1"] = ("g_wgk1", dma(
        "gq", "g_wgk1", wgk1s.ap().rearrange("p (c m) -> p c m", m=16),
        wgk1_h.rearrange("(c p) m -> p c m", p=128)))
    cast_done["wgk2"] = ("g_wgk2", dma("gq", "g_wgk2", wgk2s[:], wgk2_h))
    for nm, dst, src in [("wq", wq_b, wq_h), ("wk", wk_b, wk_h),
                         ("wv", wv_b, wv_h), ("wg", wg_b, wg_h),
                         ("w0", w0_b, w0_h), ("w1", w1_b, w1_h),
                         ("wo", wo_b, wo_h)]:
        cast_done[nm] = ("g_" + nm, dma("gq", "g_" + nm, dst, src))

    with ExitStack() as ph:
        sb0 = lambda nm, s, d: ph.enter_context(nc.sbuf_tensor(nm, s, d))
        pre64 = sb0("p0_pre64", [64, 128], F32)
        pre16a = sb0("p0_pre16a", [16, 128], F32)
        pre16b = sb0("p0_pre16b", [16, 128], F32)
        pre1 = sb0("p0_pre1", [1, 128], F32)
        t_ident = dma("sync", "c0", sb_ident[:], ident)
        dma("sync", "c0", sb_mask[:], maskt)
        t_b0 = dma("sync", "c0", pre64[:], b0_h.rearrange("(c p) -> c p", p=128))
        t_b1 = dma("sync", "c0", pre16a[:], b1_h.rearrange("(c p) -> c p", p=128))
        t_bg = dma("sync", "c0", pre16b[:], bgk2_h.rearrange("(c p) -> c p", p=128))
        t_gn = dma("sync", "c0", pre1[:], gnorm_h.rearrange("(o d) -> o d", o=1))
        c0_all = n["c0"]
        p_b0 = op("pe", lambda e: e.transpose(psA[:, 0:64], pre64[:], sb_ident[0:64, 0:64]),
                  waits=[("c0", c0_all)])
        p_b1 = op("pe", lambda e: e.transpose(psA[:, 512:528], pre16a[:], sb_ident[0:16, 0:16]))
        p_bg = op("pe", lambda e: e.transpose(psA[:, 1024:1040], pre16b[:], sb_ident[0:16, 0:16]))
        p_gn = op("pe", lambda e: e.transpose(psA[:, 1536:1537], pre1[:], sb_ident[0:1, 0:1]))
        op("dve", lambda e: e.tensor_copy(bias0c[:], psA[:, 0:64]), waits=[("pe", p_b0)])
        op("dve", lambda e: e.tensor_copy(bias1c[:], psA[:, 512:528]), waits=[("pe", p_b1)])
        op("dve", lambda e: e.tensor_copy(bgk2c[:], psA[:, 1024:1040]), waits=[("pe", p_bg)])
        op("dve", lambda e: e.tensor_copy(gnormc[:], psA[:, 1536:1537]), waits=[("pe", p_gn)])
        op("dve", lambda e: e.memset(onesrow[:], 1.0))
        op("dve", lambda e: e.memset(cEPS[:], EPS))
        op("dve", lambda e: e.memset(cLNS[:], LNSCALE))
        barrier()

    xnp_es = ExitStack()
    xnp = [xnp_es.enter_context(nc.sbuf_tensor(f"c_xnp{i}", [128, 16 * 512], BF16))
           for i in range(2)]

    def load_panel(tb, waits, q2="sync"):
        b = tb % 2
        t = 0
        for k in range(16):
            t = dma(q2 if k >= 8 else "sync", f"pan{b}",
                    xnp[b][:, k * 512:(k + 1) * 512],
                    xn_tm[tb * 512:(tb + 1) * 512, k * 128:(k + 1) * 128],
                    waits=waits if k == 0 else [], transpose=True)
        return (f"pan{b}", t)

    with ExitStack() as ph:
        sb1 = lambda nm, s, d: ph.enter_context(nc.sbuf_tensor(nm, s, d))
        xld = [sb1(f"p1_x{i}", [128, D], F32) for i in range(3)]
        sq = [sb1(f"p1_sq{i}", [128, D], F32) for i in range(2)]
        xnst = [sb1(f"p1_xn{i}", [128, D], BF16) for i in range(2)]
        wqs_q = sb1("p3_wqs_wq", [128, 16 * 2048], BF16)
        qstage_q = [sb1(f"p3_st_wq{i}", [128, 512], BF16) for i in range(4)]
        sgst = [sb1(f"p4_sg{i}", [128, 512], F32) for i in range(2)]
        int_sb = sb1("p1_int", [16, T], BF16)
        dve_done = [0] * NT
        axn_mark = [0] * NT
        dma_out = [0] * NT

        def emit_p1_tile(i):
            b3, b2 = i % 3, i % 2
            c = 2 * b2
            w = []
            if i >= 3:
                w = [("dve", dve_done[i - 3]), ("act", axn_mark[i - 3])]
            ld = dma("sync", f"p1x{b3}", xld[b3][:], x_h[i * 128:(i + 1) * 128, :], waits=w)
            a_sq = op("act", lambda e, b3=b3, b2=b2: e.activation(sq[b2][:], xld[b3][:], AF.Square),
                      waits=[(f"p1x{b3}", ld)] + ([("dve", dve_done[i - 2])] if i >= 2 else []))
            op("dve", lambda e, b3=b3, c=c: e.tensor_reduce(
                ldstat[:, c:c + 1], xld[b3][:], AX.X, ALU.add),
                waits=[(f"p1x{b3}", ld)] + ([("act", axn_mark[i - 2])] if i >= 2 else []))
            op("dve", lambda e, b2=b2, c=c: e.tensor_reduce(
                ldstat[:, c + 1:c + 2], sq[b2][:], AX.X, ALU.add), waits=[("act", a_sq)])
            op("dve", lambda e, c=c: e.tensor_scalar(
                ldstat[:, c:c + 1], ldstat[:, c:c + 1], 1.0 / D, None, ALU.mult))
            op("dve", lambda e, c=c: e.tensor_scalar(
                ldstat[:, c + 1:c + 2], ldstat[:, c + 1:c + 2], 1.0 / D, None, ALU.mult))
            dnv = op("dve", lambda e, c=c: e.scalar_tensor_tensor(
                ldstat[:, c + 1:c + 2], ldstat[:, c:c + 1], ldstat[:, c:c + 1],
                ldstat[:, c + 1:c + 2], ALU.mult, ALU.subtract))
            op("act", lambda e, c=c: e.activation(
                ldstat[:, c + 1:c + 2], ldstat[:, c + 1:c + 2], AF.Ln, bias=cEPS[:], scale=-1.0),
                waits=[("dve", dnv)])
            a_rs = op("act", lambda e, c=c: e.activation(
                ldstat[:, c + 1:c + 2], ldstat[:, c + 1:c + 2], AF.Exp, bias=0.0, scale=-0.5))
            axn_mark[i] = a_rs
            op("dve", lambda e, c=c: e.scalar_tensor_tensor(
                ldstat[:, c:c + 1], ldstat[:, c:c + 1], ldstat[:, c + 1:c + 2],
                ldstat[:, c:c + 1], ALU.mult, ALU.bypass), waits=[("act", a_rs)])
            op("dve", lambda e, c=c: e.tensor_scalar(
                ldstat[:, c:c + 1], ldstat[:, c:c + 1], -1.0, None, ALU.mult))
            d_xn = op("dve", lambda e, b3=b3, b2=b2, c=c: e.scalar_tensor_tensor(
                xnst[b2][:], xld[b3][:], ldstat[:, c + 1:c + 2],
                ldstat.ap()[:, c:c + 1].broadcast_to((128, D)),
                ALU.mult, ALU.add),
                waits=([(f"p1w{b2}", dma_out[i - 2])] if i >= 2 else []))
            dve_done[i] = n["dve"]
            dma_out[i] = dma("act", f"p1w{b2}", xn_tm[i * 128:(i + 1) * 128, :], xnst[b2][:],
                             waits=[("dve", d_xn)])

        for i in range(8):
            emit_p1_tile(i)
        wld = 0
        for k in range(16):
            wld = dma("sync", "wld", wqs_q[:, k * 2048:(k + 1) * 2048],
                      wq_b[k * 128:(k + 1) * 128, :],
                      waits=[cast_done["wq"]] if k == 0 else [])
        pan_t = [0] * NTB
        pe_mark = [0] * NTB
        ev_mark = [0] * NTB
        st_dma = [0] * 4
        sg_dma4 = [0, 0]
        ev = [0] * (NTB * 16)
        aev4 = [0] * (NTB * 16)
        for tb in range(NTB):
            for i in range(4 * tb + 8, min(4 * tb + 12, NT)):
                emit_p1_tile(i)
            b = tb % 2
            w = [("p1w0", dma_out[4 * tb + 2]), ("p1w1", dma_out[4 * tb + 3])]
            if tb >= 2:
                w.append(("pe", pe_mark[tb - 2]))
            pan_t[tb] = load_panel(tb, w)
            slot2 = tb % 2
            w2 = [pan_t[tb]]
            if tb == 0:
                w2.append(cast_done["wgk1"])
            if tb >= 2:
                w2.append(("dve", ev_mark[tb - 2]))
            for k in range(16):
                op("pe", lambda e, slot2=slot2, k=k, b=b: e.matmul(
                    psB[0:16, slot2 * 512:(slot2 + 1) * 512],
                    wgk1s[:, k * 16:(k + 1) * 16], xnp[b][:, k * 512:(k + 1) * 512],
                    start=(k == 0), stop=(k == 15)), waits=w2 if k == 0 else [], inc=(k == 15))
            pe2 = n["pe"]
            ev_mark[tb] = op("dve", lambda e, slot2=slot2, tb=tb: e.tensor_copy(
                int_sb[:, tb * 512:(tb + 1) * 512], psB[0:16, slot2 * 512:(slot2 + 1) * 512]),
                waits=[("pe", pe2)])
            for h in range(16):
                slot = h % 4
                for k in range(16):
                    w = [pan_t[tb]]
                    if k == 0 and (tb * 16 + h) >= 4:
                        w.append(("dve", ev[tb * 16 + h - 4]))
                    if k == 0 and tb == 0 and h == 0:
                        w.append(("wld", wld))
                    op("pe", lambda e, slot=slot, k=k, h=h, b=b: e.matmul(
                        psA[:, slot * 512:(slot + 1) * 512],
                        wqs_q[:, k * 2048 + h * 128: k * 2048 + (h + 1) * 128],
                        xnp[b][:, k * 512:(k + 1) * 512],
                        start=(k == 0), stop=(k == 15)), waits=w, inc=(k == 15))
                pe_done = n["pe"]
                wv_ = [("pe", pe_done)]
                if (tb * 16 + h) >= 4:
                    wv_.append((f"st{slot}", st_dma[slot]))
                ev[tb * 16 + h] = op("dve", lambda e, slot=slot: e.tensor_copy(
                    qstage_q[slot][:], psA[:, slot * 512:(slot + 1) * 512]), waits=wv_)
                st_dma[slot] = dma("act", f"st{slot}",
                                   q_fm[h * 128:(h + 1) * 128, tb * 512:(tb + 1) * 512],
                                   qstage_q[slot][:], waits=[("dve", ev[tb * 16 + h])])
            for h in range(16):
                idx4 = tb * 16 + h
                s4 = idx4 % 2
                w4 = []
                if h == 0:
                    w4.append(("dve", ev_mark[tb]))
                    if tb == 0:
                        w4.append(cast_done["wgk2"])
                if idx4 >= 2:
                    w4.append(("act", aev4[idx4 - 2]))
                mm4 = op("pe", lambda e, h=h, tb=tb, s4=s4: e.matmul(
                    psB[:, 1024 + s4 * 512:1024 + (s4 + 1) * 512],
                    wgk2s[:, h * 128:(h + 1) * 128],
                    int_sb[:, tb * 512:(tb + 1) * 512],
                    start=True, stop=True), waits=w4)
                wv_ = [("pe", mm4)]
                if idx4 >= 2:
                    wv_.append((f"vst{s4}", sg_dma4[s4]))
                aev4[idx4] = op("act", lambda e, s4=s4, h=h: e.activation(
                    sgst[s4][:], psB[:, 1024 + s4 * 512:1024 + (s4 + 1) * 512], AF.Sigmoid,
                    bias=bgk2c[:, h:h + 1], scale=1.0), waits=wv_)
                sg_dma4[s4] = dma("act", f"vst{s4}",
                                  sg_fm[h * 128:(h + 1) * 128, tb * 512:(tb + 1) * 512],
                                  sgst[s4][:], waits=[("act", aev4[idx4])])
            pe_mark[tb] = n["pe"]
        pe_q_end = n["pe"]
        dve_q_end = n["dve"]
        act_q_end = n["act"]
        p1w_end = (n["p1w0"], n["p1w1"])
        stv_end = [(f"st{i}", n[f"st{i}"]) for i in range(4)] + \
                  [(f"vst{i}", n[f"vst{i}"]) for i in range(2)]

    with ExitStack() as ph:
        sb3 = lambda nm, s, d: ph.enter_context(nc.sbuf_tensor(nm, s, d))
        wqs_k = sb3("p3_wqs_wk", [128, 16 * 2048], BF16)
        qstage_k = [sb3(f"p3_st_wk{i}", [128, 512], BF16) for i in range(4)]
        wld = 0
        for k in range(16):
            w = []
            if k == 0:
                w = [cast_done["wk"], ("pe", pe_q_end), ("dve", dve_q_end),
                     ("act", act_q_end), ("p1w0", p1w_end[0]), ("p1w1", p1w_end[1])] \
                    + stv_end
            wld = dma("gq", "wld", wqs_k[:, k * 2048:(k + 1) * 2048],
                      wk_b[k * 128:(k + 1) * 128, :], waits=w)
        st_dma = [0] * 4
        ev = [0] * (NTB * 16)
        pe_mark = [0] * NTB
        for tb in range(NTB):
            b = tb % 2
            w = [("wld", wld)] if tb == 0 else []
            if tb >= 2:
                w.append(("pe", pe_mark[tb - 2]))
            else:
                w.append(("pe", pe_q_end))
            pan = load_panel(tb, w)
            for h in range(16):
                slot = h % 4
                for k in range(16):
                    w = [pan]
                    if k == 0 and (tb * 16 + h) >= 4:
                        w.append(("dve", ev[tb * 16 + h - 4]))
                    if k == 0 and tb == 0 and h < 4:
                        w.append(("dve", dve_q_end))
                    op("pe", lambda e, slot=slot, k=k, h=h, b=b: e.matmul(
                        psA[:, slot * 512:(slot + 1) * 512],
                        wqs_k[:, k * 2048 + h * 128: k * 2048 + (h + 1) * 128],
                        xnp[b][:, k * 512:(k + 1) * 512],
                        start=(k == 0), stop=(k == 15)), waits=w, inc=(k == 15))
                pe_done = n["pe"]
                wv_ = [("pe", pe_done)]
                if (tb * 16 + h) >= 4:
                    wv_.append((f"st{slot}", st_dma[slot]))
                else:
                    wv_ += stv_end
                ev[tb * 16 + h] = op("dve", lambda e, slot=slot: e.tensor_copy(
                    qstage_k[slot][:], psA[:, slot * 512:(slot + 1) * 512]), waits=wv_)
                st_dma[slot] = dma("act", f"st{slot}",
                                   k_fm[h * 128:(h + 1) * 128, tb * 512:(tb + 1) * 512],
                                   qstage_k[slot][:], waits=[("dve", ev[tb * 16 + h])])
            pe_mark[tb] = n["pe"]
        barrier()

    with ExitStack() as ph5:
        sb5 = lambda nm, s, d: ph5.enter_context(nc.sbuf_tensor(nm, s, d))
        qF = sb5("p5_qF", [128, T], BF16)
        kF = sb5("p5_kF", [128, T], BF16)
        sgF = sb5("p5_sgF", [128, T], F32)
        lnb = sb5("p5_lnb", [128, T], F32)
        fac = sb5("p5_fac", [128, T], BF16)
        fneg = sb5("p5_fneg", [128, T], BF16)
        qko = sb5("p5_qko", [128, T], BF16)
        kho = sb5("p5_kho", [128, T], BF16)
        ends = sb5("p5_ends", [128, NCB], F32)
        endsx = sb5("p5_endsx", [128, NCB + 2], F32)
        op("dve", lambda e: e.memset(endsx[:, 0:1], 0.0))
        p5st = {"d_gc": 0, "d_qg": 0, "d_kh": 0, "st_k": 0, "st_q": 0}

        def emit_p5_head(h):
            st = dict(p5st)
            tq = dma("sync", "p5q", qF[:], q_fm[h * 128:(h + 1) * 128, :],
                     waits=[("dve", st["d_qg"])] if h else [])
            tk = dma("sync", "p5k", kF[:], k_fm[h * 128:(h + 1) * 128, :],
                     waits=[("dve", st["d_kh"])] if h else [])
            tsg = dma("sync", "p5s", sgF[:], sg_fm[h * 128:(h + 1) * 128, :],
                      waits=[("dve", st["d_gc"])] if h else [])
            a_ln = op("act", lambda e: e.activation(lnb[:], sgF[:], AF.Ln),
                      waits=[("p5s", tsg)])
            op("dve", lambda e: e.tensor_tensor_scan(
                sgF[:], lnb[:], lnb[:], 0.0, ALU.add, ALU.bypass),
                waits=[("act", a_ln)])
            op("dve", lambda e: e.tensor_copy(ends[:], sgF.ap()[:, CB - 1:T:CB]))
            op("dve", lambda e: e.tensor_scalar(
                endsx[:, 1:NCB + 1], ends[:], 1.0 / 16.0, None, ALU.mult))
            obc = endsx.ap()[:, 0:NCB].unsqueeze(2).broadcast_to((128, NCB, CB))
            d_gc = op("dve", lambda e, obc=obc: e.scalar_tensor_tensor(
                lnb.ap().rearrange("p (c t) -> p c t", t=CB),
                sgF.ap().rearrange("p (c t) -> p c t", t=CB), 1.0 / 16.0,
                obc, ALU.mult, ALU.subtract))
            op("act", lambda e, h=h: e.activation(
                decay.ap()[:, h * NCB:(h + 1) * NCB].unsqueeze(2),
                lnb.ap().rearrange("p (c t) -> p c t", t=CB)[:, :, CB - 1:CB], AF.Exp),
                waits=[("dve", d_gc)])
            a_fp = op("act", lambda e: e.activation(fac[:], lnb[:], AF.Exp, bias=cLNS[:]),
                      waits=[("dve", st["d_kh"])] if h else [])
            wq_ = [("act", a_fp), ("p5q", tq)]
            if h:
                wq_.append(("p5wq", st["st_q"]))
            d_qg = op("dve", lambda e: e.tensor_tensor(qko[:], qF[:], fac[:], ALU.mult),
                      waits=wq_)
            st_q = dma("act", "p5wq",
                       qg_sc.rearrange("d (c h t) -> d c h t", h=H, t=128)[:, :, h, :],
                       qko.ap().rearrange("d (c t) -> d c t", t=128),
                       waits=[("dve", d_qg)])
            a_fn = op("act", lambda e: e.activation(fneg[:], lnb[:], AF.Exp, scale=-1.0),
                      waits=[("dve", st["d_kh"])] if h else [])
            d_kh = op("dve", lambda e: e.tensor_tensor(kho[:], kF[:], fneg[:], ALU.mult),
                      waits=[("act", a_fn), ("p5k", tk)] +
                            ([("p5wk", st["st_k"])] if h else []))
            dma("act", "p5wk", kh_fm[h * 128:(h + 1) * 128, :], kho[:],
                waits=[("dve", d_kh)])
            st_k = dma("act", "p5wk",
                       kh_sc.rearrange("d (c h t) -> d c h t", h=H, t=128)[:, :, h, :],
                       kho.ap().rearrange("d (c t) -> d c t", t=128))
            p5st.update(d_gc=d_gc, d_qg=d_qg, d_kh=d_kh, st_k=st_k, st_q=st_q)

        for wsrc, dstf, castkey, silu, p5base in [(wv_b, v_tm, "wv", False, 0),
                                                  (wg_b, gate_tm, "wg", True, 8)]:
            with ExitStack() as ph:
                sb6 = lambda nm, s, d: ph.enter_context(nc.sbuf_tensor(nm, s, d))
                wqs = sb6(f"p6_w_{castkey}", [128, 16 * 2048], BF16)
                vstage = [sb6(f"p6_st_{castkey}{i}", [128, 2048], BF16) for i in range(2)]
                wld = 0
                for k in range(16):
                    wld = dma("sync", "wld", wqs[:, k * 2048:(k + 1) * 2048],
                              wsrc[k * 128:(k + 1) * 128, :],
                              waits=[cast_done[castkey]] if k == 0 else [])
                vst_dma = [0, 0]
                vev = [0] * NT
                pe_mark = [0] * NTB
                for tb in range(NTB):
                    b = tb % 2
                    w = [("wld", wld)] if tb == 0 else []
                    if tb >= 2:
                        w.append(("pe", pe_mark[tb - 2]))
                    pan = load_panel(tb, w)
                    for tt in range(4):
                        i = tb * 4 + tt
                        slot = i % 2
                        ps = psA if slot == 0 else psB
                        cstream = "act" if silu else "dve"
                        for k in range(16):
                            w = [pan]
                            if k == 0 and i >= 2:
                                w.append((cstream, vev[i - 2]))
                            for fg in range(4):
                                op("pe", lambda e, ps=ps, k=k, fg=fg, tt=tt, b=b, wqs=wqs: e.matmul(
                                    ps[:, fg * 512:(fg + 1) * 512],
                                    xnp[b][:, k * 512 + tt * 128: k * 512 + (tt + 1) * 128],
                                    wqs[:, k * 2048 + fg * 512: k * 2048 + (fg + 1) * 512],
                                    start=(k == 0), stop=(k == 15)),
                                    waits=w if fg == 0 else [], inc=(k == 15 and fg == 3))
                        pe_done = n["pe"]
                        wv_ = [("pe", pe_done)]
                        if i >= 2:
                            wv_.append((f"vst{slot}", vst_dma[slot]))
                        if silu:
                            vev[i] = op("act", lambda e, ps=ps, slot=slot, vstage=vstage: e.activation(
                                vstage[slot][:], ps[:], AF.Silu), waits=wv_)
                            vst_dma[slot] = dma("act", f"vst{slot}", dstf[i * 128:(i + 1) * 128, :],
                                                vstage[slot][:], waits=[("act", vev[i])])
                        else:
                            vev[i] = op("dve", lambda e, ps=ps, slot=slot, vstage=vstage: e.tensor_copy(
                                vstage[slot][:], ps[:]), waits=wv_)
                            vst_dma[slot] = dma("act", f"vst{slot}", dstf[i * 128:(i + 1) * 128, :],
                                                vstage[slot][:], waits=[("dve", vev[i])])
                    emit_p5_head(p5base + tb)
                    pe_mark[tb] = n["pe"]
                barrier()

    with ExitStack() as ph:
        sb8 = lambda nm, s, d: ph.enter_context(nc.sbuf_tensor(nm, s, d))
        w0t = [sb8(f"p8_w0t{i}", [128, 16 * 1024], BF16) for i in range(2)]
        gst = [sb8(f"p8_gst{i}", [128, 512], BF16) for i in range(4)]
        gst_dma = [0] * 4
        gev = [0] * (8 * NTB * 8)
        oct_pe_mark = [0] * 8
        blk_pe_mark = [0] * (8 * NTB)
        for oct_ in range(8):
            ob = oct_ % 2
            w0ld = 0
            for k in range(16):
                w = []
                if k == 0:
                    w = [cast_done["w0"]]
                    if oct_ >= 2:
                        w.append(("pe", oct_pe_mark[oct_ - 2]))
                w0ld = dma("sync", f"w0t{ob}", w0t[ob][:, k * 1024:(k + 1) * 1024],
                           w0_b[oct_ * 1024:(oct_ + 1) * 1024, k * 128:(k + 1) * 128],
                           waits=w, transpose=True)
            for tb in range(NTB):
                blk = oct_ * NTB + tb
                b = tb % 2
                w = [(f"w0t{ob}", w0ld)]
                if blk >= 2:
                    w.append(("pe", blk_pe_mark[blk - 2]))
                pan = load_panel(tb, w)
                for ns in range(8):
                    idx = blk * 8 + ns
                    slot = idx % 4
                    psl = psA if slot < 2 else psB
                    pcol = (slot % 2) * 512
                    for k in range(16):
                        w = [pan]
                        if k == 0 and idx >= 4:
                            w.append(("act", gev[idx - 4]))
                        op("pe", lambda e, psl=psl, pcol=pcol, k=k, ns=ns, ob=ob, b=b: e.matmul(
                            psl[:, pcol:pcol + 512],
                            w0t[ob][:, k * 1024 + ns * 128: k * 1024 + (ns + 1) * 128],
                            xnp[b][:, k * 512:(k + 1) * 512],
                            start=(k == 0), stop=(k == 15)), waits=w, inc=(k == 15))
                    pe_done = n["pe"]
                    wv_ = [("pe", pe_done)]
                    if idx >= 4:
                        wv_.append((f"st{slot}", gst_dma[slot]))
                    nchunk = oct_ * 8 + ns
                    gev[idx] = op("act", lambda e, psl=psl, pcol=pcol, slot=slot, nchunk=nchunk:
                                  e.activation(gst[slot][:], psl[:, pcol:pcol + 512], AF.Gelu,
                                               bias=bias0c[:, nchunk:nchunk + 1]), waits=wv_)
                    gst_dma[slot] = dma(
                        "act", f"st{slot}",
                        d1_fm[nchunk * 128:(nchunk + 1) * 128, tb * 512:(tb + 1) * 512],
                        gst[slot][:], waits=[("act", gev[idx])])
                blk_pe_mark[blk] = n["pe"]
            oct_pe_mark[oct_] = n["pe"]
        barrier()

    xnp_es.close()

    with ExitStack() as ph:
        sb9 = lambda nm, s, d: ph.enter_context(nc.sbuf_tensor(nm, s, d))
        w1t = [sb9(f"p9_w1t{i}", [128, 16 * 2048], BF16) for i in range(2)]
        d1c = [sb9(f"p9_d1c{i}", [128, 16 * 128], BF16) for i in range(3)]
        est = [sb9(f"p9_est{i}", [128, 2048], F32) for i in range(2)]
        epld = sb9("p9_ep", [128, 2048], F32)
        gsil = sb9("p9_gs", [128, 2048], BF16)
        est_dma = [0, 0]
        gsil_dma = [0, 0]
        sev = [0] * NT
        w1ld = [0, 0]
        qtr_pe_end = [0] * 4
        prev_eev = [0] * NT

        def load_w1_qtr(qtr, pe_guard):
            par = qtr % 2
            t = 0
            for ncl in range(16):
                w = []
                if ncl == 0:
                    w = [cast_done["w1"]]
                    if pe_guard:
                        w.append(("pe", pe_guard))
                t = dma("sync", "wld", w1t[par][:, ncl * 2048:(ncl + 1) * 2048],
                        w1_b[:, (qtr * 16 + ncl) * 128:(qtr * 16 + ncl + 1) * 128],
                        waits=w, transpose=True)
            w1ld[par] = t

        load_w1_qtr(0, 0)
        for qtr in range(4):
            par = qtr % 2
            eev = [0] * NT
            tt_pe_mark = [0] * NT
            d1ld = [0, 0, 0]
            for tt in range(NT):
                b = tt % 3
                w = [("wld", w1ld[par])] if tt == 0 else []
                if tt >= 3:
                    w.append(("pe", tt_pe_mark[tt - 3]))
                elif qtr > 0:
                    w.append(("pe", qtr_pe_end[qtr - 1]))
                srcap = d1_fm.rearrange("(c p) t -> p c t", p=128)[
                    :, qtr * 16:(qtr + 1) * 16, tt * 128:(tt + 1) * 128]
                d1ld[b] = dma("sync", f"d1c{b % 2}",
                              d1c[b].ap().rearrange("p (c t) -> p c t", t=128), srcap, waits=w)
                if tt == 2 and qtr < 3:
                    load_w1_qtr(qtr + 1, qtr_pe_end[qtr - 1] if qtr >= 1 else 0)
                slot = tt % 2
                ps = psA if slot == 0 else psB
                if qtr == 0:
                    for fg in range(4):
                        op("pe", lambda e, ps=ps, fg=fg: e.matmul(
                            ps[:, fg * 512:(fg + 1) * 512], onesrow[0:1, 0:128],
                            b1row[0:1, fg * 512:(fg + 1) * 512],
                            start=True, stop=False),
                            waits=([(f"d1c{b % 2}", d1ld[b])] + ([("dve", eev[tt - 2])] if tt >= 2 else []))
                            if fg == 0 else [], inc=False)
                for ncl in range(16):
                    w = []
                    if ncl == 0 and qtr != 0:
                        w = [(f"d1c{b % 2}", d1ld[b])]
                        if tt >= 2:
                            w.append(("dve", eev[tt - 2]))
                        else:
                            w.append(("dve", prev_eev[NT - 2 + tt]))
                    for fg in range(4):
                        op("pe", lambda e, ps=ps, ncl=ncl, fg=fg, b=b, qtr=qtr, par=par: e.matmul(
                            ps[:, fg * 512:(fg + 1) * 512],
                            d1c[b][:, ncl * 128:(ncl + 1) * 128],
                            w1t[par][:, ncl * 2048 + fg * 512: ncl * 2048 + (fg + 1) * 512],
                            start=(ncl == 0 and qtr != 0), stop=(ncl == 15)),
                            waits=w if fg == 0 else [], inc=(ncl == 15 and fg == 3))
                pe_done = n["pe"]
                tt_pe_mark[tt] = pe_done
                wv_ = [("pe", pe_done)]
                if est_dma[slot]:
                    wv_.append((f"gacc{slot}", est_dma[slot]))
                eev[tt] = op("dve", lambda e, ps=ps, slot=slot: e.tensor_copy(
                    est[slot][:], ps[:]), waits=wv_)
                kw = {"accum_op": ALU.add} if qtr > 0 else {}
                est_dma[slot] = dma("gq", f"gacc{slot}", e_part[tt * 128:(tt + 1) * 128, :],
                                    est[slot][:], waits=[("dve", eev[tt])], **kw)
                if qtr == 3:
                    w = [(f"gacc{slot}", est_dma[slot])]
                    if tt >= 1:
                        w.append(("act", sev[tt - 1]))
                    tld = dma("gq", "p9e0", epld[:],
                              e_part[tt * 128:(tt + 1) * 128, :], waits=w)
                    wv_ = [("p9e0", tld)]
                    if tt >= 1:
                        wv_.append(("p9g0", gsil_dma[0]))
                    sev[tt] = op("act", lambda e: e.activation(
                        gsil[:], epld[:], AF.Silu), waits=wv_)
                    gsil_dma[0] = dma("act", "p9g0", g_tm[tt * 128:(tt + 1) * 128, :],
                                      gsil[:], waits=[("act", sev[tt])])
            qtr_pe_end[qtr] = n["pe"]
            prev_eev = eev
        barrier()

    tail_es = ExitStack()
    ogf = tail_es.enter_context(nc.sbuf_tensor("p12_ogf", [128, 16 * 1024], BF16))

    with ExitStack() as ph:
        sb10 = lambda nm, s, d: ph.enter_context(nc.sbuf_tensor(nm, s, d))
        S32 = sb10("c_S32", [128, H * HD], F32)
        S16 = [sb10("c_S16a", [128, H * HD], BF16), sb10("c_S16b", [128, H * HD], BF16)]
        op("dve", lambda e: e.memset(S32[:], 0.0))
        op("dve", lambda e: e.memset(S16[0][:], 0.0))
        p10_init = op("dve", lambda e: e.memset(S16[1][:], 0.0))
        qg2 = [sb10(f"p10_qg{i}", [128, H * 128], BF16) for i in range(3)]
        kh2 = [sb10(f"p10_kh{i}", [128, H * 128], BF16) for i in range(3)]
        kt2 = [sb10(f"p10_kt{i}", [128, 2048], BF16) for i in range(3)]
        v2 = [sb10(f"p10_v{i}", [128, 2048], BF16) for i in range(3)]
        atm = [sb10(f"p10_atm{i}", [128, 2048], BF16) for i in range(2)]
        ost = [sb10(f"p10_ost{i}", [128, 2048], BF16) for i in range(2)]
        otl = [sb10(f"p11_o{i}", [128, 2048], BF16) for i in range(2)]
        gtl = [sb10(f"p11_g{i}", [128, 2048], BF16) for i in range(2)]
        sqt = [sb10(f"p11_sq{i}", [128, 2048], F32) for i in range(2)]
        rst = sb10("p11_rs", [128, 32], BF16)
        ogt = [sb10(f"p11_og{i}", [128, 2048], BF16) for i in range(2)]
        ldk = [0, 0, 0]
        o_dma = [0, 0]
        og_dma = [0, 0]
        rev = [0] * NCB
        iter_pe_end = [0] * (NCB + 1)
        s16_done = [0] * NCB
        fold_done = [[0, 0] for _ in range(NCB)]
        mask_done = [[0, 0] for _ in range(NCB)]
        inter_pe = [0] * NCB
        o_evac = [0] * NCB
        mbc = sb_mask.ap().unsqueeze(1).broadcast_to((128, 8, CB))
        for i in range(NCB + 1):
            if i < NCB:
                b = i % 3
                w = []
                if i >= 3:
                    w = [("pe", iter_pe_end[i - 2])]
                t0 = i * 128
                dma("sync", f"sc{b % 2}", qg2[b][:],
                    qg_sc[:, i * 2048:(i + 1) * 2048], waits=w)
                dma("sync", f"sc{b % 2}", kh2[b][:],
                    kh_sc[:, i * 2048:(i + 1) * 2048])
                dma("sync", f"sc{b % 2}", v2[b][:], v_tm[t0:t0 + 128, :])
                ldk[b] = dma("act", f"sc{b % 2}", kt2[b][:], kh_fm[:, t0:t0 + 128],
                             transpose=True, waits=list(w))
                for uh in range(2):
                    base = 1024 - uh * 1024
                    for hh in range(8):
                        h = uh * 8 + hh
                        w = []
                        if hh == 0:
                            w = [(f"sc{b % 2}", ldk[b])]
                            if i >= 1:
                                w.append(("dve", mask_done[i - 1][uh]))
                        op("pe", lambda e, h=h, hh=hh, b=b, base=base: e.matmul(
                            psB[:, base + hh * 128: base + (hh + 1) * 128],
                            kt2[b][:, h * 128:(h + 1) * 128],
                            v2[b][:, h * 128:(h + 1) * 128],
                            start=True, stop=True), waits=w, inc=(hh == 7))
                pe_u = n["pe"]
            if i >= 1:
                c1 = i - 1
                b1 = c1 % 3
                par1 = c1 % 2
                for h in range(16):
                    w = []
                    if h == 0:
                        if c1 >= 1:
                            w.append(("act", s16_done[c1 - 1]))
                        else:
                            w.append(("dve", p10_init))
                        if c1 >= 1:
                            w.append(("act", o_evac[c1 - 1]))
                    op("pe", lambda e, h=h, b1=b1, par1=par1: e.matmul(
                        psA[:, h * 128:(h + 1) * 128],
                        qg2[b1][:, h * 128:(h + 1) * 128],
                        S16[par1][:, h * 128:(h + 1) * 128],
                        start=(h % 4 == 0), stop=False), waits=w, inc=False)
                inter_pe[c1] = n["pe"]
                for h in range(16):
                    w = []
                    if h == 0:
                        w = [("dve", mask_done[c1][0])]
                    if h == 8:
                        w = [("dve", mask_done[c1][1])]
                    op("pe", lambda e, h=h, b1=b1, par1=par1: e.matmul(
                        psA[:, h * 128:(h + 1) * 128],
                        atm[par1][:, h * 128:(h + 1) * 128],
                        v2[b1][:, h * 128:(h + 1) * 128],
                        start=False, stop=True), waits=w, inc=(h == 15))
                pe_o = n["pe"]
            if i < NCB:
                for uh in range(2):
                    base = 1024 - uh * 1024
                    hc0 = uh * 1024
                    w = [("pe", pe_u)]
                    if uh == 0 and i >= 1:
                        w.append(("act", s16_done[i - 1]))
                    op("dve", lambda e, hc0=hc0, base=base: e.tensor_tensor(
                        S32[:, hc0:hc0 + 1024], S32[:, hc0:hc0 + 1024],
                        psB[:, base:base + 1024], ALU.add), waits=w)
                    dbc = decay.ap()[:, i::NCB][:, uh * 8:(uh + 1) * 8].unsqueeze(2)\
                        .broadcast_to((128, 8, 128))
                    fold_done[i][uh] = op("dve", lambda e, hc0=hc0, dbc=dbc: e.tensor_tensor(
                        S32.ap()[:, hc0:hc0 + 1024].rearrange("p (h v) -> p h v", v=128),
                        S32.ap()[:, hc0:hc0 + 1024].rearrange("p (h v) -> p h v", v=128),
                        dbc, ALU.mult))
                for uh in range(2):
                    hc0 = uh * 1024
                    w = [("dve", fold_done[i][uh])]
                    if uh == 0 and i >= 1:
                        w.append(("pe", inter_pe[i - 1]))
                    op("act", lambda e, hc0=hc0, i=i: e.activation(
                        S16[1 - (i % 2)][:, hc0:hc0 + 1024], S32[:, hc0:hc0 + 1024],
                        AF.Copy), waits=w)
                s16_done[i] = n["act"]
                for ab in range(2):
                    base = 1024 - ab * 1024
                    for hh in range(8):
                        h = ab * 8 + hh
                        w = [("dve", fold_done[i][ab])] if hh == 0 else []
                        op("pe", lambda e, h=h, hh=hh, b=b, base=base: e.matmul(
                            psB[:, base + hh * 128: base + (hh + 1) * 128],
                            kh2[b][:, h * 128:(h + 1) * 128],
                            qg2[b][:, h * 128:(h + 1) * 128],
                            start=True, stop=True), waits=w, inc=(hh == 7))
                    pe_at = n["pe"]
                    mask_done[i][ab] = op("dve", lambda e, i=i, ab=ab, base=base: e.tensor_tensor(
                        atm[i % 2].ap()[:, ab * 1024:(ab + 1) * 1024].rearrange(
                            "p (h c) -> p h c", c=128),
                        psB.ap()[:, base:base + 1024].rearrange("p (h c) -> p h c", c=128),
                        mbc, ALU.mult), waits=[("pe", pe_at)])
            if i >= 1:
                wv_ = [("pe", pe_o)]
                if c1 >= 2:
                    wv_.append((f"ow{par1}", o_dma[par1]))
                a_o = op("act", lambda e, par1=par1: e.activation(
                    ost[par1][:], psA[:], AF.Copy), waits=wv_)
                o_evac[c1] = a_o
                o_dma[par1] = dma("act", f"ow{par1}", o_tm[c1 * 128:(c1 + 1) * 128, :],
                                  ost[par1][:], waits=[("act", a_o)])
                tt = c1
                pb = par1
                c0 = pb * 16
                w = [(f"ow{pb}", o_dma[pb])]
                if tt >= 2:
                    w.append(("dve", rev[tt - 2]))
                tlo = dma("sync", f"p11a{pb}", otl[pb][:], o_tm[tt * 128:(tt + 1) * 128, :],
                          waits=w)
                tlg = dma("sync", f"p11b{pb}", gtl[pb][:], gate_tm[tt * 128:(tt + 1) * 128, :])
                a_sq = op("act", lambda e, pb=pb: e.activation(sqt[pb][:], otl[pb][:], AF.Square),
                          waits=[(f"p11a{pb}", tlo)] + ([("dve", rev[tt - 2])] if tt >= 2 else []))
                d_rd = op("dve", lambda e, pb=pb, c0=c0: e.tensor_reduce(
                    ldstat[:, c0:c0 + 16],
                    sqt[pb].ap().rearrange("p (h d) -> p h d", d=128), AX.X, ALU.add),
                    waits=[("act", a_sq)])
                op("act", lambda e, c0=c0: e.activation(
                    ldstat[:, c0:c0 + 16], ldstat[:, c0:c0 + 16], AF.Ln,
                    bias=cEPS[:], scale=1.0 / 128.0), waits=[("dve", d_rd)])
                a_r = op("act", lambda e, c0=c0: e.activation(
                    rst[:, c0:c0 + 16], ldstat[:, c0:c0 + 16], AF.Exp, scale=-0.5),
                    waits=([("dve", rev[tt - 2])] if tt >= 2 else []))
                rbc = rst.ap()[:, c0:c0 + 16].unsqueeze(2).broadcast_to((128, 16, 128))
                wv_ = [("act", a_r)]
                if tt >= 2:
                    wv_.append((f"ogw{pb}", og_dma[pb]))
                op("dve", lambda e, pb=pb, rbc=rbc: e.tensor_tensor(
                    ogt[pb].ap().rearrange("p (h d) -> p h d", d=128),
                    otl[pb].ap().rearrange("p (h d) -> p h d", d=128),
                    rbc, ALU.mult), waits=wv_)
                rev[tt] = op("dve", lambda e, pb=pb: e.tensor_tensor(
                    ogt[pb][:], ogt[pb][:], gtl[pb][:], ALU.mult), waits=[(f"p11b{pb}", tlg)])
                og_dma[pb] = dma("act", f"ogw{pb}", og_tm[tt * 128:(tt + 1) * 128, :], ogt[pb][:],
                                 waits=[("dve", rev[tt])])
            if i == 16:
                ogw_snap = [("ogw0", n["ogw0"]), ("ogw1", n["ogw1"])]
                for hh in range(16):
                    ogf_pre = dma("sync", "ogf", ogf[:, hh * 1024:(hh + 1) * 1024],
                                  og_tm[0:1024, hh * 128:(hh + 1) * 128],
                                  waits=ogw_snap if hh == 0 else [], transpose=True)
            iter_pe_end[i] = n["pe"]
        barrier()

    with ExitStack() as ph:
        sb12 = lambda nm, s, d: ph.enter_context(nc.sbuf_tensor(nm, s, d))
        wos = sb12("p12_wos", [128, 16 * 2048], BF16)
        xtl = [sb12(f"p12_x{i}", [128, 2048], F32) for i in range(2)]
        gft = [sb12(f"p12_gf{i}", [128, 2048], BF16) for i in range(2)]
        outs = [sb12(f"p12_out{i}", [128, 2048], F32) for i in range(2)]
        for k in range(16):
            tw = dma("sync", "wld", wos[:, k * 2048:(k + 1) * 2048],
                     wo_b[k * 128:(k + 1) * 128, :],
                     waits=[cast_done["wo"]] if k == 0 else [])
        wo_all = n["wld"]
        for k in range(16):
            op("dve", lambda e, k=k: e.tensor_scalar(
                wos[:, k * 2048:(k + 1) * 2048], wos[:, k * 2048:(k + 1) * 2048],
                gnormc[:, 0:1], None, ALU.mult), waits=[("wld", wo_all)] if k == 0 else [])
        wo_ready = n["dve"]
        if dbg_wos is not None:
            for k in range(16):
                dma("act", "outw0", dbg_wos[k * 128:(k + 1) * 128, :],
                    wos[:, k * 2048:(k + 1) * 2048], waits=[("dve", wo_ready)])
        out_dma = [0, 0]
        cev = [0] * NT
        qtr_pe_mark = [0] * 4
        for qtr in range(4):
            if qtr == 0:
                ogld = ogf_pre
            else:
                for h in range(16):
                    w = []
                    if h == 0:
                        w = [("pe", qtr_pe_mark[qtr - 1])]
                    ogld = dma("sync", "ogf", ogf[:, h * 1024:(h + 1) * 1024],
                               og_tm[qtr * 1024:(qtr + 1) * 1024, h * 128:(h + 1) * 128],
                               waits=w, transpose=True)
            for tt8 in range(8):
                tt = qtr * 8 + tt8
                b = tt % 2
                w = [("dve", cev[tt - 2])] if tt >= 2 else []
                tx = dma("sync", f"px{b}", xtl[b][:], x_h[tt * 128:(tt + 1) * 128, :], waits=w)
                tg = dma("sync", f"pg{b}", gft[b][:], g_tm[tt * 128:(tt + 1) * 128, :])
                slot = tt % 2
                ps = psA if slot == 0 else psB
                for h in range(16):
                    w = []
                    if h == 0:
                        w = [("ogf", ogld), ("dve", wo_ready)]
                        if tt >= 2:
                            w.append(("dve", cev[tt - 2]))
                    for fg in range(4):
                        op("pe", lambda e, ps=ps, h=h, fg=fg, tt8=tt8: e.matmul(
                            ps[:, fg * 512:(fg + 1) * 512],
                            ogf[:, h * 1024 + tt8 * 128: h * 1024 + (tt8 + 1) * 128],
                            wos[:, h * 2048 + fg * 512: h * 2048 + (fg + 1) * 512],
                            start=(h == 0), stop=(h == 15)),
                            waits=w if fg == 0 else [], inc=(h == 15 and fg == 3))
                pe_done = n["pe"]
                wm_ = [("pe", pe_done), (f"pg{b}", tg)]
                if tt >= 2:
                    wm_.append((f"outw{b}", out_dma[b]))
                d_m = op("dve", lambda e, ps=ps, b=b: e.tensor_tensor(
                    outs[b][:], ps[:], gft[b][:], ALU.mult), waits=wm_)
                cev[tt] = op("dve", lambda e, b=b: e.tensor_tensor(
                    outs[b][:], outs[b][:], xtl[b][:], ALU.add),
                    waits=[(f"px{b}", tx), ("dve", d_m)])
                out_dma[b] = dma("act", f"outw{b}", out_h[tt * 128:(tt + 1) * 128, :], outs[b][:],
                                 waits=[("dve", cev[tt])])
            qtr_pe_mark[qtr] = n["pe"]
        waitonly("act", [("outw0", n["outw0"]), ("outw1", n["outw1"])])
        barrier()
    tail_es.close()

    with nc.Block() as block:
        @block.tensor
        def _(e):
            for f in streams["pe"]:
                f(e)

        @block.vector
        def _(e):
            for f in streams["dve"]:
                f(e)

        @block.scalar
        def _(e):
            for f in streams["act"]:
                f(e)

        @block.sync
        def _(e):
            for f in streams["sync"]:
                f(e)

        @block.gpsimd
        def _(e):
            for f in streams["gq"]:
                f(e)

    es.close()
    return nc


def kernel(L0_attn_x, L0_attn_Wq, L0_attn_Wk, L0_attn_Wv, L0_attn_Wgk1, L0_attn_Wgk2,
           L0_attn_bgk2, L0_attn_Wg, L0_attn_gnorm_w, L0_attn_Wo,
           L0_ffn_W0, L0_ffn_bias0, L0_ffn_W1, L0_ffn_bias1):
    nc = build_program()
    wmap = {
        "Wq": np.ascontiguousarray(np.asarray(L0_attn_Wq, np.float32)),
        "Wk": np.ascontiguousarray(np.asarray(L0_attn_Wk, np.float32)),
        "Wv": np.ascontiguousarray(np.asarray(L0_attn_Wv, np.float32)),
        "Wgk1": np.ascontiguousarray(np.asarray(L0_attn_Wgk1, np.float32)),
        "Wgk2": np.ascontiguousarray(np.asarray(L0_attn_Wgk2, np.float32)),
        "bgk2": np.ascontiguousarray(np.asarray(L0_attn_bgk2, np.float32)),
        "Wg": np.ascontiguousarray(np.asarray(L0_attn_Wg, np.float32)),
        "gnorm": np.ascontiguousarray(np.asarray(L0_attn_gnorm_w, np.float32)),
        "Wo": np.ascontiguousarray(np.asarray(L0_attn_Wo, np.float32)),
        "W0": np.ascontiguousarray(np.asarray(L0_ffn_W0, np.float32)),
        "bias0": np.ascontiguousarray(np.asarray(L0_ffn_bias0, np.float32)),
        "W1": np.ascontiguousarray(np.asarray(L0_ffn_W1, np.float32)),
        "bias1": np.ascontiguousarray(np.asarray(L0_ffn_bias1, np.float32)),
    }
    xfull = np.asarray(L0_attn_x, np.float32)
    in_maps = [dict(wmap, x=np.ascontiguousarray(xfull[i])) for i in range(8)]
    res = run_bass_kernel_spmd(nc, in_maps, core_ids=list(range(8)))
    return np.stack([r["out"] for r in res.results], axis=0)


# version 22
# speedup vs baseline: 1.1274x; 1.0113x over previous
import numpy as np
import concourse.bass as bass
import concourse.mybir as mybir
from concourse.bass_utils import run_bass_kernel_spmd
from contextlib import ExitStack

F32 = mybir.dt.float32
BF16 = mybir.dt.bfloat16
AF = mybir.ActivationFunctionType
ALU = mybir.AluOpType
AX = mybir.AxisListType

T = 4096
D = 2048
H = 16
HD = 128
CH = 64
NCH = T // CH
CB = 128
NCB = T // CB
DFF = 8192
NT = T // 128
NTB = T // 512
LNSCALE = float(np.log(HD ** -0.5))
EPS = 1e-5

DEBUG_TAPS = []


def build_program():
    nc = bass.Bass("TRN2", target_bir_lowering=False, debug=False)

    x_h = nc.dram_tensor("x", [T, D], F32, kind="ExternalInput").ap()
    wq_h = nc.dram_tensor("Wq", [D, D], F32, kind="ExternalInput").ap()
    wk_h = nc.dram_tensor("Wk", [D, D], F32, kind="ExternalInput").ap()
    wv_h = nc.dram_tensor("Wv", [D, D], F32, kind="ExternalInput").ap()
    wgk1_h = nc.dram_tensor("Wgk1", [D, 16], F32, kind="ExternalInput").ap()
    wgk2_h = nc.dram_tensor("Wgk2", [16, D], F32, kind="ExternalInput").ap()
    bgk2_h = nc.dram_tensor("bgk2", [D], F32, kind="ExternalInput").ap()
    wg_h = nc.dram_tensor("Wg", [D, D], F32, kind="ExternalInput").ap()
    gnorm_h = nc.dram_tensor("gnorm", [HD], F32, kind="ExternalInput").ap()
    wo_h = nc.dram_tensor("Wo", [D, D], F32, kind="ExternalInput").ap()
    w0_h = nc.dram_tensor("W0", [DFF, D], F32, kind="ExternalInput").ap()
    b0_h = nc.dram_tensor("bias0", [DFF], F32, kind="ExternalInput").ap()
    w1_h = nc.dram_tensor("W1", [D, DFF], F32, kind="ExternalInput").ap()
    b1_h = nc.dram_tensor("bias1", [D], F32, kind="ExternalInput").ap()
    out_h = nc.dram_tensor("out", [T, D], F32, kind="ExternalOutput").ap()

    def scratch(name, shape, dtype):
        kind = "ExternalOutput" if name in DEBUG_TAPS else "Internal"
        return nc.dram_tensor(name, shape, dtype, kind=kind).ap()

    xn_tm = scratch("xn_tm", [T, D], BF16)
    wq_b = scratch("wq_b", [D, D], BF16)
    wk_b = scratch("wk_b", [D, D], BF16)
    wv_b = scratch("wv_b", [D, D], BF16)
    wg_b = scratch("wg_b", [D, D], BF16)
    wo_b = scratch("wo_b", [D, D], BF16)
    w0_b = scratch("w0_b", [DFF, D], BF16)
    w1_b = scratch("w1_b", [D, DFF], BF16)
    q_fm = scratch("q_fm", [D, T], BF16)
    k_fm = scratch("k_fm", [D, T], BF16)
    sg_fm = scratch("sg_fm", [D, T], F32)
    qg_fm = scratch("qg_fm", [D, T], BF16)
    kh_fm = scratch("kh_fm", [D, T], BF16)
    v_tm = scratch("v_tm", [T, D], BF16)
    gate_tm = scratch("gate_tm", [T, D], BF16)
    d1_fm = scratch("d1_fm", [DFF, T], BF16)
    e_part = scratch("e_part", [T, D], F32)
    g_tm = scratch("g_tm", [T, D], BF16)
    o_tm = scratch("o_tm", [T, D], BF16)
    og_tm = scratch("og_tm", [T, D], BF16)
    dbg_wos = scratch("dbg_wos", [D, D], BF16) if "dbg_wos" in DEBUG_TAPS else None
    dbg_scan = scratch("dbg_scan", [D, T], F32) if "dbg_scan" in DEBUG_TAPS else None
    dbg_gc = scratch("dbg_gc", [D, T], F32) if "dbg_gc" in DEBUG_TAPS else None

    ident = nc.inline_tensor(np.eye(128, dtype=np.float32), "ident").ap()
    maskt = nc.inline_tensor(np.triu(np.ones((CB, CB), np.float32)), "maskt").ap()

    es = ExitStack()
    sbt = lambda name, shape, dt: es.enter_context(nc.sbuf_tensor(name, shape, dt))
    pst = lambda name, shape: es.enter_context(nc.psum_tensor(name, shape, F32))

    sb_ident = sbt("c_ident", [128, 128], F32)
    sb_mask = sbt("c_mask", [CB, CB], F32)
    bias0c = sbt("c_bias0c", [128, 64], F32)
    bias1c = sbt("c_bias1c", [128, 16], F32)
    bgk2c = sbt("c_bgk2c", [128, 16], F32)
    gnormc = sbt("c_gnormc", [128, 1], F32)
    b1row = sbt("c_b1row", [1, D], BF16)
    onesrow = sbt("c_onesrow", [1, 128], BF16)
    wgk1s = sbt("c_wgk1s", [128, 16 * 16], BF16)
    wgk2s = sbt("c_wgk2s", [16, D], BF16)
    decay = sbt("c_decay", [128, H * NCB], F32)
    ldstat = sbt("c_ldstat", [128, 32], F32)
    cEPS = sbt("c_eps", [128, 1], F32)
    cLNS = sbt("c_lns", [128, 1], F32)

    psA = pst("psA", [128, 2048])
    psB = pst("psB", [128, 2048])

    semnames = (["pe", "dve", "act", "c0", "g_b1r", "wld", "w1ld", "ogf",
                 "kt0", "kt1"]
                + ["g_" + k for k in ["wq", "wk", "wv", "wg", "w0", "w1", "wo",
                                      "wgk1", "wgk2"]]
                + [f"p1x{i}" for i in range(3)] + [f"p1w{i}" for i in range(2)]
                + [f"pan{i}" for i in range(2)] + [f"st{i}" for i in range(4)]
                + ["p5q", "p5k", "p5s", "p5wq", "p5wk"]
                + [f"vst{i}" for i in range(2)] + [f"w0t{i}" for i in range(2)]
                + [f"d1c{i}" for i in range(2)] + [f"gacc{i}" for i in range(2)]
                + [f"p9e{i}" for i in range(2)] + [f"p9g{i}" for i in range(2)]
                + [f"sc{i}" for i in range(2)] + [f"v1{i}" for i in range(2)]
                + [f"ow{i}" for i in range(2)]
                + [f"p11a{i}" for i in range(2)] + [f"p11b{i}" for i in range(2)]
                + [f"ogw{i}" for i in range(2)]
                + [f"px{i}" for i in range(2)] + [f"pg{i}" for i in range(2)]
                + [f"outw{i}" for i in range(2)])
    sm = {k: es.enter_context(nc.semaphore(k)) for k in semnames}
    n = {k: 0 for k in sm}
    streams = {k: [] for k in ["pe", "dve", "act", "sync", "gq"]}

    def dma(stream, qsem, out, in_, waits=(), **kw):
        n[qsem] += 16
        tgt = n[qsem]

        def f(e, out=out, in_=in_, waits=tuple(waits), kw=dict(kw)):
            for s, v in waits:
                if v > 0:
                    e.wait_ge(sm[s], v)
            e.dma_start(out=out, in_=in_, **kw).then_inc(sm[qsem], 16)
        streams[stream].append(f)
        return tgt

    def op(stream, fn, waits=(), inc=None):
        csem = {"pe": "pe", "dve": "dve", "act": "act"}[stream]
        do_inc = inc is not False
        prior = n[csem]
        if do_inc:
            n[csem] += 1
        tgt = n[csem]
        selfw = (csem, prior) if stream in ("dve", "act") else None

        def f(e, fn=fn, waits=tuple(waits), do_inc=do_inc, selfw=selfw):
            if selfw is not None and selfw[1] > 0:
                e.wait_ge(sm[selfw[0]], selfw[1])
            for s, v in waits:
                if v > 0:
                    e.wait_ge(sm[s], v)
            inst = fn(e)
            if do_inc:
                inst.then_inc(sm[csem], 1)
        streams[stream].append(f)
        return tgt

    def waitonly(stream, waits):
        def f(e, waits=tuple(waits)):
            for s, v in waits:
                if v > 0:
                    e.wait_ge(sm[s], v)
        streams[stream].append(f)

    CASTS = ("g_wq", "g_wk", "g_wv", "g_wg", "g_w0", "g_w1", "g_wo",
             "g_wgk1", "g_wgk2")

    def barrier():
        snap = {k: v for k, v in n.items() if k not in CASTS}
        for s in streams:
            waitonly(s, list(snap.items()))

    cast_done = {}
    cast_done["wgk1"] = ("g_wgk1", dma(
        "gq", "g_wgk1", wgk1s.ap().rearrange("p (c m) -> p c m", m=16),
        wgk1_h.rearrange("(c p) m -> p c m", p=128)))
    cast_done["wgk2"] = ("g_wgk2", dma("gq", "g_wgk2", wgk2s[:], wgk2_h))
    for nm, dst, src in [("wq", wq_b, wq_h), ("wk", wk_b, wk_h),
                         ("wv", wv_b, wv_h), ("wg", wg_b, wg_h),
                         ("w0", w0_b, w0_h), ("w1", w1_b, w1_h),
                         ("wo", wo_b, wo_h)]:
        cast_done[nm] = ("g_" + nm, dma("gq", "g_" + nm, dst, src))

    with ExitStack() as ph:
        sb0 = lambda nm, s, d: ph.enter_context(nc.sbuf_tensor(nm, s, d))
        pre64 = sb0("p0_pre64", [64, 128], F32)
        pre16a = sb0("p0_pre16a", [16, 128], F32)
        pre16b = sb0("p0_pre16b", [16, 128], F32)
        pre1 = sb0("p0_pre1", [1, 128], F32)
        t_ident = dma("sync", "c0", sb_ident[:], ident)
        dma("sync", "c0", sb_mask[:], maskt)
        t_b0 = dma("sync", "c0", pre64[:], b0_h.rearrange("(c p) -> c p", p=128))
        t_b1 = dma("sync", "c0", pre16a[:], b1_h.rearrange("(c p) -> c p", p=128))
        t_bg = dma("sync", "c0", pre16b[:], bgk2_h.rearrange("(c p) -> c p", p=128))
        t_gn = dma("sync", "c0", pre1[:], gnorm_h.rearrange("(o d) -> o d", o=1))
        dma("gq", "g_b1r", b1row[:], b1_h.rearrange("(o d) -> o d", o=1))
        c0_all = n["c0"]
        p_b0 = op("pe", lambda e: e.transpose(psA[:, 0:64], pre64[:], sb_ident[0:64, 0:64]),
                  waits=[("c0", c0_all)])
        p_b1 = op("pe", lambda e: e.transpose(psA[:, 512:528], pre16a[:], sb_ident[0:16, 0:16]))
        p_bg = op("pe", lambda e: e.transpose(psA[:, 1024:1040], pre16b[:], sb_ident[0:16, 0:16]))
        p_gn = op("pe", lambda e: e.transpose(psA[:, 1536:1537], pre1[:], sb_ident[0:1, 0:1]))
        op("dve", lambda e: e.tensor_copy(bias0c[:], psA[:, 0:64]), waits=[("pe", p_b0)])
        op("dve", lambda e: e.tensor_copy(bias1c[:], psA[:, 512:528]), waits=[("pe", p_b1)])
        op("dve", lambda e: e.tensor_copy(bgk2c[:], psA[:, 1024:1040]), waits=[("pe", p_bg)])
        op("dve", lambda e: e.tensor_copy(gnormc[:], psA[:, 1536:1537]), waits=[("pe", p_gn)])
        op("dve", lambda e: e.memset(onesrow[:], 1.0))
        op("dve", lambda e: e.memset(cEPS[:], EPS))
        op("dve", lambda e: e.memset(cLNS[:], LNSCALE))
        barrier()

    xnp_es = ExitStack()
    xnp = [xnp_es.enter_context(nc.sbuf_tensor(f"c_xnp{i}", [128, 16 * 512], BF16))
           for i in range(2)]

    def load_panel(tb, waits, q2="sync"):
        b = tb % 2
        t = 0
        for k in range(16):
            t = dma(q2 if k >= 8 else "sync", f"pan{b}",
                    xnp[b][:, k * 512:(k + 1) * 512],
                    xn_tm[tb * 512:(tb + 1) * 512, k * 128:(k + 1) * 128],
                    waits=waits if k == 0 else [], transpose=True)
        return (f"pan{b}", t)

    with ExitStack() as ph:
        sb1 = lambda nm, s, d: ph.enter_context(nc.sbuf_tensor(nm, s, d))
        xld = [sb1(f"p1_x{i}", [128, D], F32) for i in range(3)]
        sq = [sb1(f"p1_sq{i}", [128, D], F32) for i in range(2)]
        xnst = [sb1(f"p1_xn{i}", [128, D], BF16) for i in range(2)]
        wqs_q = sb1("p3_wqs_wq", [128, 16 * 2048], BF16)
        qstage_q = [sb1(f"p3_st_wq{i}", [128, 512], BF16) for i in range(4)]
        sgst = [sb1(f"p4_sg{i}", [128, 512], F32) for i in range(2)]
        int_sb = sb1("p1_int", [16, T], BF16)
        dve_done = [0] * NT
        axn_mark = [0] * NT
        dma_out = [0] * NT

        def emit_p1_tile(i):
            b3, b2 = i % 3, i % 2
            c = 2 * b2
            w = []
            if i >= 3:
                w = [("dve", dve_done[i - 3]), ("act", axn_mark[i - 3])]
            ld = dma("sync", f"p1x{b3}", xld[b3][:], x_h[i * 128:(i + 1) * 128, :], waits=w)
            a_sq = op("act", lambda e, b3=b3, b2=b2: e.activation(sq[b2][:], xld[b3][:], AF.Square),
                      waits=[(f"p1x{b3}", ld)] + ([("dve", dve_done[i - 2])] if i >= 2 else []))
            op("dve", lambda e, b3=b3, c=c: e.tensor_reduce(
                ldstat[:, c:c + 1], xld[b3][:], AX.X, ALU.add),
                waits=[(f"p1x{b3}", ld)] + ([("act", axn_mark[i - 2])] if i >= 2 else []))
            op("dve", lambda e, b2=b2, c=c: e.tensor_reduce(
                ldstat[:, c + 1:c + 2], sq[b2][:], AX.X, ALU.add), waits=[("act", a_sq)])
            op("dve", lambda e, c=c: e.tensor_scalar(
                ldstat[:, c:c + 1], ldstat[:, c:c + 1], 1.0 / D, None, ALU.mult))
            op("dve", lambda e, c=c: e.tensor_scalar(
                ldstat[:, c + 1:c + 2], ldstat[:, c + 1:c + 2], 1.0 / D, None, ALU.mult))
            dnv = op("dve", lambda e, c=c: e.scalar_tensor_tensor(
                ldstat[:, c + 1:c + 2], ldstat[:, c:c + 1], ldstat[:, c:c + 1],
                ldstat[:, c + 1:c + 2], ALU.mult, ALU.subtract))
            op("act", lambda e, c=c: e.activation(
                ldstat[:, c + 1:c + 2], ldstat[:, c + 1:c + 2], AF.Ln, bias=cEPS[:], scale=-1.0),
                waits=[("dve", dnv)])
            a_rs = op("act", lambda e, c=c: e.activation(
                ldstat[:, c + 1:c + 2], ldstat[:, c + 1:c + 2], AF.Exp, bias=0.0, scale=-0.5))
            axn_mark[i] = a_rs
            op("dve", lambda e, c=c: e.scalar_tensor_tensor(
                ldstat[:, c:c + 1], ldstat[:, c:c + 1], ldstat[:, c + 1:c + 2],
                ldstat[:, c:c + 1], ALU.mult, ALU.bypass), waits=[("act", a_rs)])
            op("dve", lambda e, c=c: e.tensor_scalar(
                ldstat[:, c:c + 1], ldstat[:, c:c + 1], -1.0, None, ALU.mult))
            d_xn = op("dve", lambda e, b3=b3, b2=b2, c=c: e.scalar_tensor_tensor(
                xnst[b2][:], xld[b3][:], ldstat[:, c + 1:c + 2],
                ldstat.ap()[:, c:c + 1].broadcast_to((128, D)),
                ALU.mult, ALU.add),
                waits=([(f"p1w{b2}", dma_out[i - 2])] if i >= 2 else []))
            dve_done[i] = n["dve"]
            dma_out[i] = dma("act", f"p1w{b2}", xn_tm[i * 128:(i + 1) * 128, :], xnst[b2][:],
                             waits=[("dve", d_xn)])

        for i in range(8):
            emit_p1_tile(i)
        wld = 0
        for k in range(16):
            wld = dma("sync", "wld", wqs_q[:, k * 2048:(k + 1) * 2048],
                      wq_b[k * 128:(k + 1) * 128, :],
                      waits=[cast_done["wq"]] if k == 0 else [])
        pan_t = [0] * NTB
        pe_mark = [0] * NTB
        ev_mark = [0] * NTB
        st_dma = [0] * 4
        sg_dma4 = [0, 0]
        ev = [0] * (NTB * 16)
        aev4 = [0] * (NTB * 16)
        for tb in range(NTB):
            for i in range(4 * tb + 8, min(4 * tb + 12, NT)):
                emit_p1_tile(i)
            b = tb % 2
            w = [("p1w0", dma_out[4 * tb + 2]), ("p1w1", dma_out[4 * tb + 3])]
            if tb >= 2:
                w.append(("pe", pe_mark[tb - 2]))
            pan_t[tb] = load_panel(tb, w)
            slot2 = tb % 2
            w2 = [pan_t[tb]]
            if tb == 0:
                w2.append(cast_done["wgk1"])
            if tb >= 2:
                w2.append(("dve", ev_mark[tb - 2]))
            for k in range(16):
                op("pe", lambda e, slot2=slot2, k=k, b=b: e.matmul(
                    psB[0:16, slot2 * 512:(slot2 + 1) * 512],
                    wgk1s[:, k * 16:(k + 1) * 16], xnp[b][:, k * 512:(k + 1) * 512],
                    start=(k == 0), stop=(k == 15)), waits=w2 if k == 0 else [], inc=(k == 15))
            pe2 = n["pe"]
            ev_mark[tb] = op("dve", lambda e, slot2=slot2, tb=tb: e.tensor_copy(
                int_sb[:, tb * 512:(tb + 1) * 512], psB[0:16, slot2 * 512:(slot2 + 1) * 512]),
                waits=[("pe", pe2)])
            for h in range(16):
                slot = h % 4
                for k in range(16):
                    w = [pan_t[tb]]
                    if k == 0 and (tb * 16 + h) >= 4:
                        w.append(("dve", ev[tb * 16 + h - 4]))
                    if k == 0 and tb == 0 and h == 0:
                        w.append(("wld", wld))
                    op("pe", lambda e, slot=slot, k=k, h=h, b=b: e.matmul(
                        psA[:, slot * 512:(slot + 1) * 512],
                        wqs_q[:, k * 2048 + h * 128: k * 2048 + (h + 1) * 128],
                        xnp[b][:, k * 512:(k + 1) * 512],
                        start=(k == 0), stop=(k == 15)), waits=w, inc=(k == 15))
                pe_done = n["pe"]
                wv_ = [("pe", pe_done)]
                if (tb * 16 + h) >= 4:
                    wv_.append((f"st{slot}", st_dma[slot]))
                ev[tb * 16 + h] = op("dve", lambda e, slot=slot: e.tensor_copy(
                    qstage_q[slot][:], psA[:, slot * 512:(slot + 1) * 512]), waits=wv_)
                st_dma[slot] = dma("act", f"st{slot}",
                                   q_fm[h * 128:(h + 1) * 128, tb * 512:(tb + 1) * 512],
                                   qstage_q[slot][:], waits=[("dve", ev[tb * 16 + h])])
            for h in range(16):
                idx4 = tb * 16 + h
                s4 = idx4 % 2
                w4 = []
                if h == 0:
                    w4.append(("dve", ev_mark[tb]))
                    if tb == 0:
                        w4.append(cast_done["wgk2"])
                if idx4 >= 2:
                    w4.append(("act", aev4[idx4 - 2]))
                mm4 = op("pe", lambda e, h=h, tb=tb, s4=s4: e.matmul(
                    psB[:, 1024 + s4 * 512:1024 + (s4 + 1) * 512],
                    wgk2s[:, h * 128:(h + 1) * 128],
                    int_sb[:, tb * 512:(tb + 1) * 512],
                    start=True, stop=True), waits=w4)
                wv_ = [("pe", mm4)]
                if idx4 >= 2:
                    wv_.append((f"vst{s4}", sg_dma4[s4]))
                aev4[idx4] = op("act", lambda e, s4=s4, h=h: e.activation(
                    sgst[s4][:], psB[:, 1024 + s4 * 512:1024 + (s4 + 1) * 512], AF.Sigmoid,
                    bias=bgk2c[:, h:h + 1], scale=1.0), waits=wv_)
                sg_dma4[s4] = dma("act", f"vst{s4}",
                                  sg_fm[h * 128:(h + 1) * 128, tb * 512:(tb + 1) * 512],
                                  sgst[s4][:], waits=[("act", aev4[idx4])])
            pe_mark[tb] = n["pe"]
        pe_q_end = n["pe"]
        dve_q_end = n["dve"]
        act_q_end = n["act"]
        p1w_end = (n["p1w0"], n["p1w1"])
        stv_end = [(f"st{i}", n[f"st{i}"]) for i in range(4)] + \
                  [(f"vst{i}", n[f"vst{i}"]) for i in range(2)]

    with ExitStack() as ph:
        sb3 = lambda nm, s, d: ph.enter_context(nc.sbuf_tensor(nm, s, d))
        wqs_k = sb3("p3_wqs_wk", [128, 16 * 2048], BF16)
        qstage_k = [sb3(f"p3_st_wk{i}", [128, 512], BF16) for i in range(4)]
        wld = 0
        for k in range(16):
            w = []
            if k == 0:
                w = [cast_done["wk"], ("pe", pe_q_end), ("dve", dve_q_end),
                     ("act", act_q_end), ("p1w0", p1w_end[0]), ("p1w1", p1w_end[1])] \
                    + stv_end
            wld = dma("gq", "wld", wqs_k[:, k * 2048:(k + 1) * 2048],
                      wk_b[k * 128:(k + 1) * 128, :], waits=w)
        st_dma = [0] * 4
        ev = [0] * (NTB * 16)
        pe_mark = [0] * NTB
        for tb in range(NTB):
            b = tb % 2
            w = [("wld", wld)] if tb == 0 else []
            if tb >= 2:
                w.append(("pe", pe_mark[tb - 2]))
            else:
                w.append(("pe", pe_q_end))
            pan = load_panel(tb, w)
            for h in range(16):
                slot = h % 4
                for k in range(16):
                    w = [pan]
                    if k == 0 and (tb * 16 + h) >= 4:
                        w.append(("dve", ev[tb * 16 + h - 4]))
                    if k == 0 and tb == 0 and h < 4:
                        w.append(("dve", dve_q_end))
                    op("pe", lambda e, slot=slot, k=k, h=h, b=b: e.matmul(
                        psA[:, slot * 512:(slot + 1) * 512],
                        wqs_k[:, k * 2048 + h * 128: k * 2048 + (h + 1) * 128],
                        xnp[b][:, k * 512:(k + 1) * 512],
                        start=(k == 0), stop=(k == 15)), waits=w, inc=(k == 15))
                pe_done = n["pe"]
                wv_ = [("pe", pe_done)]
                if (tb * 16 + h) >= 4:
                    wv_.append((f"st{slot}", st_dma[slot]))
                else:
                    wv_ += stv_end
                ev[tb * 16 + h] = op("dve", lambda e, slot=slot: e.tensor_copy(
                    qstage_k[slot][:], psA[:, slot * 512:(slot + 1) * 512]), waits=wv_)
                st_dma[slot] = dma("act", f"st{slot}",
                                   k_fm[h * 128:(h + 1) * 128, tb * 512:(tb + 1) * 512],
                                   qstage_k[slot][:], waits=[("dve", ev[tb * 16 + h])])
            pe_mark[tb] = n["pe"]
        barrier()

    with ExitStack() as ph5:
        sb5 = lambda nm, s, d: ph5.enter_context(nc.sbuf_tensor(nm, s, d))
        qF = sb5("p5_qF", [128, T], BF16)
        kF = sb5("p5_kF", [128, T], BF16)
        sgF = sb5("p5_sgF", [128, T], F32)
        lnb = sb5("p5_lnb", [128, T], F32)
        fac = sb5("p5_fac", [128, T], BF16)
        fneg = sb5("p5_fneg", [128, T], BF16)
        qko = sb5("p5_qko", [128, T], BF16)
        kho = sb5("p5_kho", [128, T], BF16)
        ends = sb5("p5_ends", [128, NCB], F32)
        endsx = sb5("p5_endsx", [128, NCB + 2], F32)
        op("dve", lambda e: e.memset(endsx[:, 0:1], 0.0))
        p5st = {"d_gc": 0, "d_qg": 0, "d_kh": 0, "st_k": 0, "st_q": 0}

        def emit_p5_head(h):
            st = dict(p5st)
            tq = dma("sync", "p5q", qF[:], q_fm[h * 128:(h + 1) * 128, :],
                     waits=[("dve", st["d_qg"])] if h else [])
            tk = dma("sync", "p5k", kF[:], k_fm[h * 128:(h + 1) * 128, :],
                     waits=[("dve", st["d_kh"])] if h else [])
            tsg = dma("sync", "p5s", sgF[:], sg_fm[h * 128:(h + 1) * 128, :],
                      waits=[("dve", st["d_gc"])] if h else [])
            a_ln = op("act", lambda e: e.activation(lnb[:], sgF[:], AF.Ln),
                      waits=[("p5s", tsg)])
            op("dve", lambda e: e.tensor_tensor_scan(
                sgF[:], lnb[:], lnb[:], 0.0, ALU.add, ALU.bypass),
                waits=[("act", a_ln)])
            op("dve", lambda e: e.tensor_copy(ends[:], sgF.ap()[:, CB - 1:T:CB]))
            op("dve", lambda e: e.tensor_scalar(
                endsx[:, 1:NCB + 1], ends[:], 1.0 / 16.0, None, ALU.mult))
            obc = endsx.ap()[:, 0:NCB].unsqueeze(2).broadcast_to((128, NCB, CB))
            d_gc = op("dve", lambda e, obc=obc: e.scalar_tensor_tensor(
                lnb.ap().rearrange("p (c t) -> p c t", t=CB),
                sgF.ap().rearrange("p (c t) -> p c t", t=CB), 1.0 / 16.0,
                obc, ALU.mult, ALU.subtract))
            op("act", lambda e, h=h: e.activation(
                decay.ap()[:, h * NCB:(h + 1) * NCB].unsqueeze(2),
                lnb.ap().rearrange("p (c t) -> p c t", t=CB)[:, :, CB - 1:CB], AF.Exp),
                waits=[("dve", d_gc)])
            a_fp = op("act", lambda e: e.activation(fac[:], lnb[:], AF.Exp, bias=cLNS[:]),
                      waits=[("dve", st["d_kh"])] if h else [])
            wq_ = [("act", a_fp), ("p5q", tq)]
            if h:
                wq_.append(("p5wq", st["st_q"]))
            d_qg = op("dve", lambda e: e.tensor_tensor(qko[:], qF[:], fac[:], ALU.mult),
                      waits=wq_)
            st_q = dma("act", "p5wq", qg_fm[h * 128:(h + 1) * 128, :], qko[:],
                       waits=[("dve", d_qg)])
            a_fn = op("act", lambda e: e.activation(fneg[:], lnb[:], AF.Exp, scale=-1.0),
                      waits=[("dve", st["d_kh"])] if h else [])
            d_kh = op("dve", lambda e: e.tensor_tensor(kho[:], kF[:], fneg[:], ALU.mult),
                      waits=[("act", a_fn), ("p5k", tk)] +
                            ([("p5wk", st["st_k"])] if h else []))
            st_k = dma("act", "p5wk", kh_fm[h * 128:(h + 1) * 128, :],
                       kho[:],
                       waits=[("dve", d_kh)])
            p5st.update(d_gc=d_gc, d_qg=d_qg, d_kh=d_kh, st_k=st_k, st_q=st_q)

        for wsrc, dstf, castkey, silu, p5base in [(wv_b, v_tm, "wv", False, 0),
                                                  (wg_b, gate_tm, "wg", True, 8)]:
            with ExitStack() as ph:
                sb6 = lambda nm, s, d: ph.enter_context(nc.sbuf_tensor(nm, s, d))
                wqs = sb6(f"p6_w_{castkey}", [128, 16 * 2048], BF16)
                vstage = [sb6(f"p6_st_{castkey}{i}", [128, 2048], BF16) for i in range(2)]
                wld = 0
                for k in range(16):
                    wld = dma("sync", "wld", wqs[:, k * 2048:(k + 1) * 2048],
                              wsrc[k * 128:(k + 1) * 128, :],
                              waits=[cast_done[castkey]] if k == 0 else [])
                vst_dma = [0, 0]
                vev = [0] * NT
                pe_mark = [0] * NTB
                for tb in range(NTB):
                    b = tb % 2
                    w = [("wld", wld)] if tb == 0 else []
                    if tb >= 2:
                        w.append(("pe", pe_mark[tb - 2]))
                    pan = load_panel(tb, w)
                    for tt in range(4):
                        i = tb * 4 + tt
                        slot = i % 2
                        ps = psA if slot == 0 else psB
                        cstream = "act" if silu else "dve"
                        for k in range(16):
                            w = [pan]
                            if k == 0 and i >= 2:
                                w.append((cstream, vev[i - 2]))
                            for fg in range(4):
                                op("pe", lambda e, ps=ps, k=k, fg=fg, tt=tt, b=b, wqs=wqs: e.matmul(
                                    ps[:, fg * 512:(fg + 1) * 512],
                                    xnp[b][:, k * 512 + tt * 128: k * 512 + (tt + 1) * 128],
                                    wqs[:, k * 2048 + fg * 512: k * 2048 + (fg + 1) * 512],
                                    start=(k == 0), stop=(k == 15)),
                                    waits=w if fg == 0 else [], inc=(k == 15 and fg == 3))
                        pe_done = n["pe"]
                        wv_ = [("pe", pe_done)]
                        if i >= 2:
                            wv_.append((f"vst{slot}", vst_dma[slot]))
                        if silu:
                            vev[i] = op("act", lambda e, ps=ps, slot=slot, vstage=vstage: e.activation(
                                vstage[slot][:], ps[:], AF.Silu), waits=wv_)
                            vst_dma[slot] = dma("act", f"vst{slot}", dstf[i * 128:(i + 1) * 128, :],
                                                vstage[slot][:], waits=[("act", vev[i])])
                        else:
                            vev[i] = op("dve", lambda e, ps=ps, slot=slot, vstage=vstage: e.tensor_copy(
                                vstage[slot][:], ps[:]), waits=wv_)
                            vst_dma[slot] = dma("act", f"vst{slot}", dstf[i * 128:(i + 1) * 128, :],
                                                vstage[slot][:], waits=[("dve", vev[i])])
                    emit_p5_head(p5base + tb)
                    pe_mark[tb] = n["pe"]
                barrier()

    with ExitStack() as ph:
        sb8 = lambda nm, s, d: ph.enter_context(nc.sbuf_tensor(nm, s, d))
        w0t = [sb8(f"p8_w0t{i}", [128, 16 * 1024], BF16) for i in range(2)]
        gst = [sb8(f"p8_gst{i}", [128, 512], BF16) for i in range(4)]
        gst_dma = [0] * 4
        gev = [0] * (8 * NTB * 8)
        oct_pe_mark = [0] * 8
        blk_pe_mark = [0] * (8 * NTB)
        for oct_ in range(8):
            ob = oct_ % 2
            w0ld = 0
            for k in range(16):
                w = []
                if k == 0:
                    w = [cast_done["w0"]]
                    if oct_ >= 2:
                        w.append(("pe", oct_pe_mark[oct_ - 2]))
                w0ld = dma("sync", f"w0t{ob}", w0t[ob][:, k * 1024:(k + 1) * 1024],
                           w0_b[oct_ * 1024:(oct_ + 1) * 1024, k * 128:(k + 1) * 128],
                           waits=w, transpose=True)
            for tb in range(NTB):
                blk = oct_ * NTB + tb
                b = tb % 2
                w = [(f"w0t{ob}", w0ld)]
                if blk >= 2:
                    w.append(("pe", blk_pe_mark[blk - 2]))
                pan = load_panel(tb, w)
                for ns in range(8):
                    idx = blk * 8 + ns
                    slot = idx % 4
                    psl = psA if slot < 2 else psB
                    pcol = (slot % 2) * 512
                    for k in range(16):
                        w = [pan]
                        if k == 0 and idx >= 4:
                            w.append(("act", gev[idx - 4]))
                        op("pe", lambda e, psl=psl, pcol=pcol, k=k, ns=ns, ob=ob, b=b: e.matmul(
                            psl[:, pcol:pcol + 512],
                            w0t[ob][:, k * 1024 + ns * 128: k * 1024 + (ns + 1) * 128],
                            xnp[b][:, k * 512:(k + 1) * 512],
                            start=(k == 0), stop=(k == 15)), waits=w, inc=(k == 15))
                    pe_done = n["pe"]
                    wv_ = [("pe", pe_done)]
                    if idx >= 4:
                        wv_.append((f"st{slot}", gst_dma[slot]))
                    nchunk = oct_ * 8 + ns
                    gev[idx] = op("act", lambda e, psl=psl, pcol=pcol, slot=slot, nchunk=nchunk:
                                  e.activation(gst[slot][:], psl[:, pcol:pcol + 512], AF.Gelu,
                                               bias=bias0c[:, nchunk:nchunk + 1]), waits=wv_)
                    gst_dma[slot] = dma(
                        "act", f"st{slot}",
                        d1_fm[nchunk * 128:(nchunk + 1) * 128, tb * 512:(tb + 1) * 512],
                        gst[slot][:], waits=[("act", gev[idx])])
                blk_pe_mark[blk] = n["pe"]
            oct_pe_mark[oct_] = n["pe"]
        barrier()

    xnp_es.close()

    with ExitStack() as ph:
        sb9 = lambda nm, s, d: ph.enter_context(nc.sbuf_tensor(nm, s, d))
        w1t = [sb9(f"p9_w1t{i}", [128, 16 * 2048], BF16) for i in range(2)]
        d1c = [sb9(f"p9_d1c{i}", [128, 16 * 128], BF16) for i in range(2)]
        est = [sb9(f"p9_est{i}", [128, 2048], F32) for i in range(2)]
        epld = sb9("p9_ep", [128, 2048], F32)
        gsil = sb9("p9_gs", [128, 2048], BF16)
        est_dma = [0, 0]
        gsil_dma = [0, 0]
        sev = [0] * NT
        w1ld = [0, 0]
        qtr_pe_end = [0] * 4
        prev_eev = [0] * NT

        def load_w1_qtr(qtr, pe_guard):
            par = qtr % 2
            t = 0
            for ncl in range(16):
                w = []
                if ncl == 0:
                    w = [cast_done["w1"]]
                    if pe_guard:
                        w.append(("pe", pe_guard))
                t = dma("sync", "w1ld", w1t[par][:, ncl * 2048:(ncl + 1) * 2048],
                        w1_b[:, (qtr * 16 + ncl) * 128:(qtr * 16 + ncl + 1) * 128],
                        waits=w, transpose=True)
            w1ld[par] = t

        load_w1_qtr(0, 0)
        for qtr in range(4):
            par = qtr % 2
            eev = [0] * NT
            tt_pe_mark = [0] * NT
            d1ld = [0, 0]
            for tt in range(NT):
                b = tt % 2
                w = [("w1ld", w1ld[par])] if tt == 0 else []
                if tt >= 2:
                    w.append(("pe", tt_pe_mark[tt - 2]))
                elif qtr > 0:
                    w.append(("pe", qtr_pe_end[qtr - 1]))
                srcap = d1_fm.rearrange("(c p) t -> p c t", p=128)[
                    :, qtr * 16:(qtr + 1) * 16, tt * 128:(tt + 1) * 128]
                d1ld[b] = dma("sync", f"d1c{b}",
                              d1c[b].ap().rearrange("p (c t) -> p c t", t=128), srcap, waits=w)
                if tt == 2 and qtr < 3:
                    load_w1_qtr(qtr + 1, qtr_pe_end[qtr - 1] if qtr >= 1 else 0)
                slot = tt % 2
                ps = psA if slot == 0 else psB
                if qtr == 0:
                    for fg in range(4):
                        op("pe", lambda e, ps=ps, fg=fg: e.matmul(
                            ps[:, fg * 512:(fg + 1) * 512], onesrow[0:1, 0:128],
                            b1row[0:1, fg * 512:(fg + 1) * 512],
                            start=True, stop=False),
                            waits=([(f"d1c{b}", d1ld[b])] + ([("dve", eev[tt - 2])] if tt >= 2 else []))
                            if fg == 0 else [], inc=False)
                for ncl in range(16):
                    w = []
                    if ncl == 0 and qtr != 0:
                        w = [(f"d1c{b}", d1ld[b])]
                        if tt >= 2:
                            w.append(("dve", eev[tt - 2]))
                        else:
                            w.append(("dve", prev_eev[NT - 2 + tt]))
                    for fg in range(4):
                        op("pe", lambda e, ps=ps, ncl=ncl, fg=fg, b=b, qtr=qtr, par=par: e.matmul(
                            ps[:, fg * 512:(fg + 1) * 512],
                            d1c[b][:, ncl * 128:(ncl + 1) * 128],
                            w1t[par][:, ncl * 2048 + fg * 512: ncl * 2048 + (fg + 1) * 512],
                            start=(ncl == 0 and qtr != 0), stop=(ncl == 15)),
                            waits=w if fg == 0 else [], inc=(ncl == 15 and fg == 3))
                pe_done = n["pe"]
                tt_pe_mark[tt] = pe_done
                wv_ = [("pe", pe_done)]
                if est_dma[slot]:
                    wv_.append((f"gacc{slot}", est_dma[slot]))
                eev[tt] = op("dve", lambda e, ps=ps, slot=slot: e.tensor_copy(
                    est[slot][:], ps[:]), waits=wv_)
                kw = {"accum_op": ALU.add} if qtr > 0 else {}
                est_dma[slot] = dma("gq", f"gacc{slot}", e_part[tt * 128:(tt + 1) * 128, :],
                                    est[slot][:], waits=[("dve", eev[tt])], **kw)
                if qtr == 3:
                    w = [(f"gacc{slot}", est_dma[slot])]
                    if tt >= 1:
                        w.append(("act", sev[tt - 1]))
                    tld = dma("gq", "p9e0", epld[:],
                              e_part[tt * 128:(tt + 1) * 128, :], waits=w)
                    wv_ = [("p9e0", tld)]
                    if tt >= 1:
                        wv_.append(("p9g0", gsil_dma[0]))
                    sev[tt] = op("act", lambda e: e.activation(
                        gsil[:], epld[:], AF.Silu), waits=wv_)
                    gsil_dma[0] = dma("act", "p9g0", g_tm[tt * 128:(tt + 1) * 128, :],
                                      gsil[:], waits=[("act", sev[tt])])
            qtr_pe_end[qtr] = n["pe"]
            prev_eev = eev
        barrier()

    with ExitStack() as ph:
        sb10 = lambda nm, s, d: ph.enter_context(nc.sbuf_tensor(nm, s, d))
        S32 = sb10("c_S32", [128, H * HD], F32)
        S16 = [sb10("c_S16a", [128, H * HD], BF16), sb10("c_S16b", [128, H * HD], BF16)]
        op("dve", lambda e: e.memset(S32[:], 0.0))
        op("dve", lambda e: e.memset(S16[0][:], 0.0))
        p10_init = op("dve", lambda e: e.memset(S16[1][:], 0.0))
        qg2 = [sb10(f"p10_qg{i}", [128, H * 128], BF16) for i in range(3)]
        kh2 = [sb10(f"p10_kh{i}", [128, H * 128], BF16) for i in range(3)]
        kt2 = [sb10(f"p10_kt{i}", [128, 2048], BF16) for i in range(3)]
        v2 = [sb10(f"p10_v{i}", [128, 2048], BF16) for i in range(3)]
        atm = [sb10(f"p10_atm{i}", [128, 2048], BF16) for i in range(2)]
        ost = [sb10(f"p10_ost{i}", [128, 2048], BF16) for i in range(2)]
        otl = [sb10(f"p11_o{i}", [128, 2048], BF16) for i in range(2)]
        gtl = [sb10(f"p11_g{i}", [128, 2048], BF16) for i in range(2)]
        sqt = [sb10(f"p11_sq{i}", [128, 2048], F32) for i in range(2)]
        rst = sb10("p11_rs", [128, 32], BF16)
        ogt = [sb10(f"p11_og{i}", [128, 2048], BF16) for i in range(2)]
        ldk = [0, 0, 0]
        ktv = [0, 0, 0]
        o_dma = [0, 0]
        og_dma = [0, 0]
        rev = [0] * NCB
        iter_pe_end = [0] * (NCB + 1)
        s16_done = [0] * NCB
        fold_done = [[0, 0] for _ in range(NCB)]
        mask_done = [[0, 0] for _ in range(NCB)]
        inter_pe = [0] * NCB
        o_evac = [0] * NCB
        mbc = sb_mask.ap().unsqueeze(1).broadcast_to((128, 8, CB))
        for i in range(NCB + 1):
            if i < NCB:
                b = i % 3
                w = []
                if i >= 3:
                    w = [("pe", iter_pe_end[i - 2])]
                t0 = i * 128
                dma("sync", f"sc{b % 2}",
                    qg2[b].ap().rearrange("p (h t) -> p h t", t=128),
                    qg_fm.rearrange("(h d) t -> d h t", d=128)[:, :, t0:t0 + 128], waits=w)
                dma("sync", f"sc{b % 2}",
                    kh2[b].ap().rearrange("p (h t) -> p h t", t=128),
                    kh_fm.rearrange("(h d) t -> d h t", d=128)[:, :, t0:t0 + 128])
                ktv[b] = dma("sync", f"kt{b % 2}", kt2[b][:], kh_fm[:, t0:t0 + 128],
                             transpose=True)
                ldk[b] = dma("sync", f"sc{b % 2}", v2[b][:], v_tm[t0:t0 + 128, :])
                for uh in range(2):
                    base = 1024 - uh * 1024
                    for hh in range(8):
                        h = uh * 8 + hh
                        w = []
                        if hh == 0:
                            w = [(f"sc{b % 2}", ldk[b]), (f"kt{b % 2}", ktv[b])]
                            if i >= 1:
                                w.append(("dve", mask_done[i - 1][uh]))
                        op("pe", lambda e, h=h, hh=hh, b=b, base=base: e.matmul(
                            psB[:, base + hh * 128: base + (hh + 1) * 128],
                            kt2[b][:, h * 128:(h + 1) * 128],
                            v2[b][:, h * 128:(h + 1) * 128],
                            start=True, stop=True), waits=w, inc=(hh == 7))
                pe_u = n["pe"]
            if i >= 1:
                c1 = i - 1
                b1 = c1 % 3
                par1 = c1 % 2
                for h in range(16):
                    w = []
                    if h == 0:
                        if c1 >= 1:
                            w.append(("act", s16_done[c1 - 1]))
                        else:
                            w.append(("dve", p10_init))
                        if c1 >= 1:
                            w.append(("act", o_evac[c1 - 1]))
                    op("pe", lambda e, h=h, b1=b1, par1=par1: e.matmul(
                        psA[:, h * 128:(h + 1) * 128],
                        qg2[b1][:, h * 128:(h + 1) * 128],
                        S16[par1][:, h * 128:(h + 1) * 128],
                        start=(h % 4 == 0), stop=False), waits=w, inc=False)
                inter_pe[c1] = n["pe"]
                for h in range(16):
                    w = []
                    if h == 0:
                        w = [("dve", mask_done[c1][0])]
                    if h == 8:
                        w = [("dve", mask_done[c1][1])]
                    op("pe", lambda e, h=h, b1=b1, par1=par1: e.matmul(
                        psA[:, h * 128:(h + 1) * 128],
                        atm[par1][:, h * 128:(h + 1) * 128],
                        v2[b1][:, h * 128:(h + 1) * 128],
                        start=False, stop=True), waits=w, inc=(h == 15))
                pe_o = n["pe"]
            if i < NCB:
                for uh in range(2):
                    base = 1024 - uh * 1024
                    hc0 = uh * 1024
                    w = [("pe", pe_u)]
                    if uh == 0 and i >= 1:
                        w.append(("act", s16_done[i - 1]))
                    op("dve", lambda e, hc0=hc0, base=base: e.tensor_tensor(
                        S32[:, hc0:hc0 + 1024], S32[:, hc0:hc0 + 1024],
                        psB[:, base:base + 1024], ALU.add), waits=w)
                    dbc = decay.ap()[:, i::NCB][:, uh * 8:(uh + 1) * 8].unsqueeze(2)\
                        .broadcast_to((128, 8, 128))
                    fold_done[i][uh] = op("dve", lambda e, hc0=hc0, dbc=dbc: e.tensor_tensor(
                        S32.ap()[:, hc0:hc0 + 1024].rearrange("p (h v) -> p h v", v=128),
                        S32.ap()[:, hc0:hc0 + 1024].rearrange("p (h v) -> p h v", v=128),
                        dbc, ALU.mult))
                for uh in range(2):
                    hc0 = uh * 1024
                    w = [("dve", fold_done[i][uh])]
                    if uh == 0 and i >= 1:
                        w.append(("pe", inter_pe[i - 1]))
                    op("act", lambda e, hc0=hc0, i=i: e.activation(
                        S16[1 - (i % 2)][:, hc0:hc0 + 1024], S32[:, hc0:hc0 + 1024],
                        AF.Copy), waits=w)
                s16_done[i] = n["act"]
                for ab in range(2):
                    base = 1024 - ab * 1024
                    for hh in range(8):
                        h = ab * 8 + hh
                        w = [("dve", fold_done[i][ab])] if hh == 0 else []
                        op("pe", lambda e, h=h, hh=hh, b=b, base=base: e.matmul(
                            psB[:, base + hh * 128: base + (hh + 1) * 128],
                            kh2[b][:, h * 128:(h + 1) * 128],
                            qg2[b][:, h * 128:(h + 1) * 128],
                            start=True, stop=True), waits=w, inc=(hh == 7))
                    pe_at = n["pe"]
                    mask_done[i][ab] = op("dve", lambda e, i=i, ab=ab, base=base: e.tensor_tensor(
                        atm[i % 2].ap()[:, ab * 1024:(ab + 1) * 1024].rearrange(
                            "p (h c) -> p h c", c=128),
                        psB.ap()[:, base:base + 1024].rearrange("p (h c) -> p h c", c=128),
                        mbc, ALU.mult), waits=[("pe", pe_at)])
            if i >= 1:
                wv_ = [("pe", pe_o)]
                if c1 >= 2:
                    wv_.append((f"ow{par1}", o_dma[par1]))
                a_o = op("act", lambda e, par1=par1: e.activation(
                    ost[par1][:], psA[:], AF.Copy), waits=wv_)
                o_evac[c1] = a_o
                o_dma[par1] = dma("act", f"ow{par1}", o_tm[c1 * 128:(c1 + 1) * 128, :],
                                  ost[par1][:], waits=[("act", a_o)])
                tt = c1
                pb = par1
                c0 = pb * 16
                w = [(f"ow{pb}", o_dma[pb])]
                if tt >= 2:
                    w.append(("dve", rev[tt - 2]))
                tlo = dma("sync", f"p11a{pb}", otl[pb][:], o_tm[tt * 128:(tt + 1) * 128, :],
                          waits=w)
                tlg = dma("sync", f"p11b{pb}", gtl[pb][:], gate_tm[tt * 128:(tt + 1) * 128, :])
                a_sq = op("act", lambda e, pb=pb: e.activation(sqt[pb][:], otl[pb][:], AF.Square),
                          waits=[(f"p11a{pb}", tlo)] + ([("dve", rev[tt - 2])] if tt >= 2 else []))
                d_rd = op("dve", lambda e, pb=pb, c0=c0: e.tensor_reduce(
                    ldstat[:, c0:c0 + 16],
                    sqt[pb].ap().rearrange("p (h d) -> p h d", d=128), AX.X, ALU.add),
                    waits=[("act", a_sq)])
                op("act", lambda e, c0=c0: e.activation(
                    ldstat[:, c0:c0 + 16], ldstat[:, c0:c0 + 16], AF.Ln,
                    bias=cEPS[:], scale=1.0 / 128.0), waits=[("dve", d_rd)])
                a_r = op("act", lambda e, c0=c0: e.activation(
                    rst[:, c0:c0 + 16], ldstat[:, c0:c0 + 16], AF.Exp, scale=-0.5),
                    waits=([("dve", rev[tt - 2])] if tt >= 2 else []))
                rbc = rst.ap()[:, c0:c0 + 16].unsqueeze(2).broadcast_to((128, 16, 128))
                wv_ = [("act", a_r)]
                if tt >= 2:
                    wv_.append((f"ogw{pb}", og_dma[pb]))
                op("dve", lambda e, pb=pb, rbc=rbc: e.tensor_tensor(
                    ogt[pb].ap().rearrange("p (h d) -> p h d", d=128),
                    otl[pb].ap().rearrange("p (h d) -> p h d", d=128),
                    rbc, ALU.mult), waits=wv_)
                rev[tt] = op("dve", lambda e, pb=pb: e.tensor_tensor(
                    ogt[pb][:], ogt[pb][:], gtl[pb][:], ALU.mult), waits=[(f"p11b{pb}", tlg)])
                og_dma[pb] = dma("act", f"ogw{pb}", og_tm[tt * 128:(tt + 1) * 128, :], ogt[pb][:],
                                 waits=[("dve", rev[tt])])
            iter_pe_end[i] = n["pe"]
        barrier()

    with ExitStack() as ph:
        sb12 = lambda nm, s, d: ph.enter_context(nc.sbuf_tensor(nm, s, d))
        wos = sb12("p12_wos", [128, 16 * 2048], BF16)
        ogf = sb12("p12_ogf", [128, 16 * 1024], BF16)
        xtl = [sb12(f"p12_x{i}", [128, 2048], F32) for i in range(2)]
        gft = [sb12(f"p12_gf{i}", [128, 2048], BF16) for i in range(2)]
        outs = [sb12(f"p12_out{i}", [128, 2048], F32) for i in range(2)]
        for k in range(16):
            tw = dma("sync", "wld", wos[:, k * 2048:(k + 1) * 2048],
                     wo_b[k * 128:(k + 1) * 128, :],
                     waits=[cast_done["wo"]] if k == 0 else [])
        wo_all = n["wld"]
        for k in range(16):
            op("dve", lambda e, k=k: e.tensor_scalar(
                wos[:, k * 2048:(k + 1) * 2048], wos[:, k * 2048:(k + 1) * 2048],
                gnormc[:, 0:1], None, ALU.mult), waits=[("wld", wo_all)] if k == 0 else [])
        wo_ready = n["dve"]
        if dbg_wos is not None:
            for k in range(16):
                dma("act", "outw0", dbg_wos[k * 128:(k + 1) * 128, :],
                    wos[:, k * 2048:(k + 1) * 2048], waits=[("dve", wo_ready)])
        out_dma = [0, 0]
        cev = [0] * NT
        qtr_pe_mark = [0] * 4
        for qtr in range(4):
            ogld = 0
            for h in range(16):
                w = []
                if h == 0 and qtr >= 1:
                    w = [("pe", qtr_pe_mark[qtr - 1])]
                ogld = dma("sync", "ogf", ogf[:, h * 1024:(h + 1) * 1024],
                           og_tm[qtr * 1024:(qtr + 1) * 1024, h * 128:(h + 1) * 128],
                           waits=w, transpose=True)
            for tt8 in range(8):
                tt = qtr * 8 + tt8
                b = tt % 2
                w = [("dve", cev[tt - 2])] if tt >= 2 else []
                tx = dma("sync", f"px{b}", xtl[b][:], x_h[tt * 128:(tt + 1) * 128, :], waits=w)
                tg = dma("sync", f"pg{b}", gft[b][:], g_tm[tt * 128:(tt + 1) * 128, :])
                slot = tt % 2
                ps = psA if slot == 0 else psB
                for h in range(16):
                    w = []
                    if h == 0:
                        w = [("ogf", ogld), ("dve", wo_ready)]
                        if tt >= 2:
                            w.append(("dve", cev[tt - 2]))
                    for fg in range(4):
                        op("pe", lambda e, ps=ps, h=h, fg=fg, tt8=tt8: e.matmul(
                            ps[:, fg * 512:(fg + 1) * 512],
                            ogf[:, h * 1024 + tt8 * 128: h * 1024 + (tt8 + 1) * 128],
                            wos[:, h * 2048 + fg * 512: h * 2048 + (fg + 1) * 512],
                            start=(h == 0), stop=(h == 15)),
                            waits=w if fg == 0 else [], inc=(h == 15 and fg == 3))
                pe_done = n["pe"]
                wm_ = [("pe", pe_done), (f"pg{b}", tg)]
                if tt >= 2:
                    wm_.append((f"outw{b}", out_dma[b]))
                d_m = op("dve", lambda e, ps=ps, b=b: e.tensor_tensor(
                    outs[b][:], ps[:], gft[b][:], ALU.mult), waits=wm_)
                cev[tt] = op("dve", lambda e, b=b: e.tensor_tensor(
                    outs[b][:], outs[b][:], xtl[b][:], ALU.add),
                    waits=[(f"px{b}", tx), ("dve", d_m)])
                out_dma[b] = dma("act", f"outw{b}", out_h[tt * 128:(tt + 1) * 128, :], outs[b][:],
                                 waits=[("dve", cev[tt])])
            qtr_pe_mark[qtr] = n["pe"]
        waitonly("act", [("outw0", n["outw0"]), ("outw1", n["outw1"])])
        barrier()

    with nc.Block() as block:
        @block.tensor
        def _(e):
            for f in streams["pe"]:
                f(e)

        @block.vector
        def _(e):
            for f in streams["dve"]:
                f(e)

        @block.scalar
        def _(e):
            for f in streams["act"]:
                f(e)

        @block.sync
        def _(e):
            for f in streams["sync"]:
                f(e)

        @block.gpsimd
        def _(e):
            for f in streams["gq"]:
                f(e)

    es.close()
    return nc


def kernel(L0_attn_x, L0_attn_Wq, L0_attn_Wk, L0_attn_Wv, L0_attn_Wgk1, L0_attn_Wgk2,
           L0_attn_bgk2, L0_attn_Wg, L0_attn_gnorm_w, L0_attn_Wo,
           L0_ffn_W0, L0_ffn_bias0, L0_ffn_W1, L0_ffn_bias1):
    nc = build_program()
    wmap = {
        "Wq": np.ascontiguousarray(np.asarray(L0_attn_Wq, np.float32)),
        "Wk": np.ascontiguousarray(np.asarray(L0_attn_Wk, np.float32)),
        "Wv": np.ascontiguousarray(np.asarray(L0_attn_Wv, np.float32)),
        "Wgk1": np.ascontiguousarray(np.asarray(L0_attn_Wgk1, np.float32)),
        "Wgk2": np.ascontiguousarray(np.asarray(L0_attn_Wgk2, np.float32)),
        "bgk2": np.ascontiguousarray(np.asarray(L0_attn_bgk2, np.float32)),
        "Wg": np.ascontiguousarray(np.asarray(L0_attn_Wg, np.float32)),
        "gnorm": np.ascontiguousarray(np.asarray(L0_attn_gnorm_w, np.float32)),
        "Wo": np.ascontiguousarray(np.asarray(L0_attn_Wo, np.float32)),
        "W0": np.ascontiguousarray(np.asarray(L0_ffn_W0, np.float32)),
        "bias0": np.ascontiguousarray(np.asarray(L0_ffn_bias0, np.float32)),
        "W1": np.ascontiguousarray(np.asarray(L0_ffn_W1, np.float32)),
        "bias1": np.ascontiguousarray(np.asarray(L0_ffn_bias1, np.float32)),
    }
    xfull = np.asarray(L0_attn_x, np.float32)
    in_maps = [dict(wmap, x=np.ascontiguousarray(xfull[i])) for i in range(8)]
    res = run_bass_kernel_spmd(nc, in_maps, core_ids=list(range(8)))
    return np.stack([r["out"] for r in res.results], axis=0)


# version 23 (final)
# speedup vs baseline: 1.1283x; 1.0009x over previous
import numpy as np
import concourse.bass as bass
import concourse.mybir as mybir
from concourse.bass_utils import run_bass_kernel_spmd
from contextlib import ExitStack

F32 = mybir.dt.float32
BF16 = mybir.dt.bfloat16
AF = mybir.ActivationFunctionType
ALU = mybir.AluOpType
AX = mybir.AxisListType

T = 4096
D = 2048
H = 16
HD = 128
CH = 64
NCH = T // CH
CB = 128
NCB = T // CB
DFF = 8192
NT = T // 128
NTB = T // 512
LNSCALE = float(np.log(HD ** -0.5))
EPS = 1e-5

DEBUG_TAPS = []


def build_program():
    nc = bass.Bass("TRN2", target_bir_lowering=False, debug=False)

    x_h = nc.dram_tensor("x", [T, D], F32, kind="ExternalInput").ap()
    wq_h = nc.dram_tensor("Wq", [D, D], F32, kind="ExternalInput").ap()
    wk_h = nc.dram_tensor("Wk", [D, D], F32, kind="ExternalInput").ap()
    wv_h = nc.dram_tensor("Wv", [D, D], F32, kind="ExternalInput").ap()
    wgk1_h = nc.dram_tensor("Wgk1", [D, 16], F32, kind="ExternalInput").ap()
    wgk2_h = nc.dram_tensor("Wgk2", [16, D], F32, kind="ExternalInput").ap()
    bgk2_h = nc.dram_tensor("bgk2", [D], F32, kind="ExternalInput").ap()
    wg_h = nc.dram_tensor("Wg", [D, D], F32, kind="ExternalInput").ap()
    gnorm_h = nc.dram_tensor("gnorm", [HD], F32, kind="ExternalInput").ap()
    wo_h = nc.dram_tensor("Wo", [D, D], F32, kind="ExternalInput").ap()
    w0_h = nc.dram_tensor("W0", [DFF, D], F32, kind="ExternalInput").ap()
    b0_h = nc.dram_tensor("bias0", [DFF], F32, kind="ExternalInput").ap()
    w1_h = nc.dram_tensor("W1", [D, DFF], F32, kind="ExternalInput").ap()
    b1_h = nc.dram_tensor("bias1", [D], F32, kind="ExternalInput").ap()
    out_h = nc.dram_tensor("out", [T, D], F32, kind="ExternalOutput").ap()

    def scratch(name, shape, dtype):
        kind = "ExternalOutput" if name in DEBUG_TAPS else "Internal"
        return nc.dram_tensor(name, shape, dtype, kind=kind).ap()

    xn_tm = scratch("xn_tm", [T, D], BF16)
    wq_b = scratch("wq_b", [D, D], BF16)
    wk_b = scratch("wk_b", [D, D], BF16)
    wv_b = scratch("wv_b", [D, D], BF16)
    wg_b = scratch("wg_b", [D, D], BF16)
    wo_b = scratch("wo_b", [D, D], BF16)
    w0_b = scratch("w0_b", [DFF, D], BF16)
    w1_b = scratch("w1_b", [D, DFF], BF16)
    q_fm = scratch("q_fm", [D, T], BF16)
    k_fm = scratch("k_fm", [D, T], BF16)
    sg_fm = scratch("sg_fm", [D, T], F32)
    qg_fm = scratch("qg_fm", [D, T], BF16)
    kh_fm = scratch("kh_fm", [D, T], BF16)
    v_tm = scratch("v_tm", [T, D], BF16)
    gate_tm = scratch("gate_tm", [T, D], BF16)
    d1_fm = scratch("d1_fm", [DFF, T], BF16)
    e_part = scratch("e_part", [T, D], F32)
    g_tm = scratch("g_tm", [T, D], BF16)
    o_tm = scratch("o_tm", [T, D], BF16)
    og_tm = scratch("og_tm", [T, D], BF16)
    dbg_wos = scratch("dbg_wos", [D, D], BF16) if "dbg_wos" in DEBUG_TAPS else None
    dbg_scan = scratch("dbg_scan", [D, T], F32) if "dbg_scan" in DEBUG_TAPS else None
    dbg_gc = scratch("dbg_gc", [D, T], F32) if "dbg_gc" in DEBUG_TAPS else None

    ident = nc.inline_tensor(np.eye(128, dtype=np.float32), "ident").ap()
    maskt = nc.inline_tensor(np.triu(np.ones((CB, CB), np.float32)), "maskt").ap()

    es = ExitStack()
    sbt = lambda name, shape, dt: es.enter_context(nc.sbuf_tensor(name, shape, dt))
    pst = lambda name, shape: es.enter_context(nc.psum_tensor(name, shape, F32))

    sb_ident = sbt("c_ident", [128, 128], F32)
    sb_mask = sbt("c_mask", [CB, CB], F32)
    bias0c = sbt("c_bias0c", [128, 64], F32)
    bias1c = sbt("c_bias1c", [128, 16], F32)
    bgk2c = sbt("c_bgk2c", [128, 16], F32)
    gnormc = sbt("c_gnormc", [128, 1], F32)
    b1row = sbt("c_b1row", [1, D], BF16)
    onesrow = sbt("c_onesrow", [1, 128], BF16)
    wgk1s = sbt("c_wgk1s", [128, 16 * 16], BF16)
    wgk2s = sbt("c_wgk2s", [16, D], BF16)
    decay = sbt("c_decay", [128, H * NCB], F32)
    ldstat = sbt("c_ldstat", [128, 32], F32)
    cEPS = sbt("c_eps", [128, 1], F32)
    cLNS = sbt("c_lns", [128, 1], F32)

    psA = pst("psA", [128, 2048])
    psB = pst("psB", [128, 2048])

    semnames = (["pe", "dve", "act", "c0", "g_b1r", "wld", "w1ld", "wldk", "ogf",
                 "kt0", "kt1"]
                + ["g_" + k for k in ["wq", "wk", "wv", "wg", "w0", "w1", "wo",
                                      "wgk1", "wgk2"]]
                + [f"p1x{i}" for i in range(3)] + [f"p1w{i}" for i in range(2)]
                + [f"pan{i}" for i in range(2)] + [f"st{i}" for i in range(4)]
                + ["p5q", "p5k", "p5s", "p5wq", "p5wk"]
                + [f"vst{i}" for i in range(2)] + [f"w0t{i}" for i in range(2)]
                + [f"d1c{i}" for i in range(2)] + [f"gacc{i}" for i in range(2)]
                + [f"p9e{i}" for i in range(2)] + [f"p9g{i}" for i in range(2)]
                + [f"sc{i}" for i in range(2)] + [f"v1{i}" for i in range(2)]
                + [f"ow{i}" for i in range(2)]
                + [f"p11a{i}" for i in range(2)] + [f"p11b{i}" for i in range(2)]
                + [f"ogw{i}" for i in range(2)]
                + [f"px{i}" for i in range(2)] + [f"pg{i}" for i in range(2)]
                + [f"outw{i}" for i in range(2)])
    sm = {k: es.enter_context(nc.semaphore(k)) for k in semnames}
    n = {k: 0 for k in sm}
    streams = {k: [] for k in ["pe", "dve", "act", "sync", "gq"]}

    def dma(stream, qsem, out, in_, waits=(), **kw):
        n[qsem] += 16
        tgt = n[qsem]

        def f(e, out=out, in_=in_, waits=tuple(waits), kw=dict(kw)):
            for s, v in waits:
                if v > 0:
                    e.wait_ge(sm[s], v)
            e.dma_start(out=out, in_=in_, **kw).then_inc(sm[qsem], 16)
        streams[stream].append(f)
        return tgt

    def op(stream, fn, waits=(), inc=None):
        csem = {"pe": "pe", "dve": "dve", "act": "act"}[stream]
        do_inc = inc is not False
        prior = n[csem]
        if do_inc:
            n[csem] += 1
        tgt = n[csem]
        selfw = (csem, prior) if stream in ("dve", "act") else None

        def f(e, fn=fn, waits=tuple(waits), do_inc=do_inc, selfw=selfw):
            if selfw is not None and selfw[1] > 0:
                e.wait_ge(sm[selfw[0]], selfw[1])
            for s, v in waits:
                if v > 0:
                    e.wait_ge(sm[s], v)
            inst = fn(e)
            if do_inc:
                inst.then_inc(sm[csem], 1)
        streams[stream].append(f)
        return tgt

    def waitonly(stream, waits):
        def f(e, waits=tuple(waits)):
            for s, v in waits:
                if v > 0:
                    e.wait_ge(sm[s], v)
        streams[stream].append(f)

    CASTS = ("g_wq", "g_wk", "g_wv", "g_wg", "g_w0", "g_w1", "g_wo",
             "g_wgk1", "g_wgk2")

    def barrier():
        snap = {k: v for k, v in n.items() if k not in CASTS}
        for s in streams:
            waitonly(s, list(snap.items()))

    cast_done = {}
    cast_done["wgk1"] = ("g_wgk1", dma(
        "gq", "g_wgk1", wgk1s.ap().rearrange("p (c m) -> p c m", m=16),
        wgk1_h.rearrange("(c p) m -> p c m", p=128)))
    cast_done["wgk2"] = ("g_wgk2", dma("gq", "g_wgk2", wgk2s[:], wgk2_h))
    for nm, dst, src in [("wq", wq_b, wq_h), ("wk", wk_b, wk_h),
                         ("wv", wv_b, wv_h), ("wg", wg_b, wg_h),
                         ("w0", w0_b, w0_h), ("w1", w1_b, w1_h),
                         ("wo", wo_b, wo_h)]:
        cast_done[nm] = ("g_" + nm, dma("gq", "g_" + nm, dst, src))

    with ExitStack() as ph:
        sb0 = lambda nm, s, d: ph.enter_context(nc.sbuf_tensor(nm, s, d))
        pre64 = sb0("p0_pre64", [64, 128], F32)
        pre16a = sb0("p0_pre16a", [16, 128], F32)
        pre16b = sb0("p0_pre16b", [16, 128], F32)
        pre1 = sb0("p0_pre1", [1, 128], F32)
        t_ident = dma("sync", "c0", sb_ident[:], ident)
        dma("sync", "c0", sb_mask[:], maskt)
        t_b0 = dma("sync", "c0", pre64[:], b0_h.rearrange("(c p) -> c p", p=128))
        t_b1 = dma("sync", "c0", pre16a[:], b1_h.rearrange("(c p) -> c p", p=128))
        t_bg = dma("sync", "c0", pre16b[:], bgk2_h.rearrange("(c p) -> c p", p=128))
        t_gn = dma("sync", "c0", pre1[:], gnorm_h.rearrange("(o d) -> o d", o=1))
        dma("gq", "g_b1r", b1row[:], b1_h.rearrange("(o d) -> o d", o=1))
        c0_all = n["c0"]
        p_b0 = op("pe", lambda e: e.transpose(psA[:, 0:64], pre64[:], sb_ident[0:64, 0:64]),
                  waits=[("c0", c0_all)])
        p_b1 = op("pe", lambda e: e.transpose(psA[:, 512:528], pre16a[:], sb_ident[0:16, 0:16]))
        p_bg = op("pe", lambda e: e.transpose(psA[:, 1024:1040], pre16b[:], sb_ident[0:16, 0:16]))
        p_gn = op("pe", lambda e: e.transpose(psA[:, 1536:1537], pre1[:], sb_ident[0:1, 0:1]))
        op("dve", lambda e: e.tensor_copy(bias0c[:], psA[:, 0:64]), waits=[("pe", p_b0)])
        op("dve", lambda e: e.tensor_copy(bias1c[:], psA[:, 512:528]), waits=[("pe", p_b1)])
        op("dve", lambda e: e.tensor_copy(bgk2c[:], psA[:, 1024:1040]), waits=[("pe", p_bg)])
        op("dve", lambda e: e.tensor_copy(gnormc[:], psA[:, 1536:1537]), waits=[("pe", p_gn)])
        op("dve", lambda e: e.memset(onesrow[:], 1.0))
        op("dve", lambda e: e.memset(cEPS[:], EPS))
        op("dve", lambda e: e.memset(cLNS[:], LNSCALE))
        barrier()

    xnp_es = ExitStack()
    xnp = [xnp_es.enter_context(nc.sbuf_tensor(f"c_xnp{i}", [128, 16 * 512], BF16))
           for i in range(2)]

    def load_panel(tb, waits, q2="sync"):
        b = tb % 2
        t = 0
        for k in range(16):
            t = dma(q2 if k >= 8 else "sync", f"pan{b}",
                    xnp[b][:, k * 512:(k + 1) * 512],
                    xn_tm[tb * 512:(tb + 1) * 512, k * 128:(k + 1) * 128],
                    waits=waits if k == 0 else [], transpose=True)
        return (f"pan{b}", t)

    with ExitStack() as ph:
        sb1 = lambda nm, s, d: ph.enter_context(nc.sbuf_tensor(nm, s, d))
        xld = [sb1(f"p1_x{i}", [128, D], F32) for i in range(3)]
        sq = [sb1(f"p1_sq{i}", [128, D], F32) for i in range(2)]
        xnst = [sb1(f"p1_xn{i}", [128, D], BF16) for i in range(2)]
        wqs_q = sb1("p3_wqs_wq", [128, 16 * 2048], BF16)
        qstage_q = [sb1(f"p3_st_wq{i}", [128, 512], BF16) for i in range(4)]
        sgst = [sb1(f"p4_sg{i}", [128, 512], F32) for i in range(2)]
        int_sb = sb1("p1_int", [16, T], BF16)
        dve_done = [0] * NT
        axn_mark = [0] * NT
        dma_out = [0] * NT

        def emit_p1_tile(i):
            b3, b2 = i % 3, i % 2
            c = 2 * b2
            w = []
            if i >= 3:
                w = [("dve", dve_done[i - 3]), ("act", axn_mark[i - 3])]
            ld = dma("sync", f"p1x{b3}", xld[b3][:], x_h[i * 128:(i + 1) * 128, :], waits=w)
            a_sq = op("act", lambda e, b3=b3, b2=b2: e.activation(sq[b2][:], xld[b3][:], AF.Square),
                      waits=[(f"p1x{b3}", ld)] + ([("dve", dve_done[i - 2])] if i >= 2 else []))
            op("dve", lambda e, b3=b3, c=c: e.tensor_reduce(
                ldstat[:, c:c + 1], xld[b3][:], AX.X, ALU.add),
                waits=[(f"p1x{b3}", ld)] + ([("act", axn_mark[i - 2])] if i >= 2 else []))
            op("dve", lambda e, b2=b2, c=c: e.tensor_reduce(
                ldstat[:, c + 1:c + 2], sq[b2][:], AX.X, ALU.add), waits=[("act", a_sq)])
            op("dve", lambda e, c=c: e.tensor_scalar(
                ldstat[:, c:c + 1], ldstat[:, c:c + 1], 1.0 / D, None, ALU.mult))
            op("dve", lambda e, c=c: e.tensor_scalar(
                ldstat[:, c + 1:c + 2], ldstat[:, c + 1:c + 2], 1.0 / D, None, ALU.mult))
            dnv = op("dve", lambda e, c=c: e.scalar_tensor_tensor(
                ldstat[:, c + 1:c + 2], ldstat[:, c:c + 1], ldstat[:, c:c + 1],
                ldstat[:, c + 1:c + 2], ALU.mult, ALU.subtract))
            op("act", lambda e, c=c: e.activation(
                ldstat[:, c + 1:c + 2], ldstat[:, c + 1:c + 2], AF.Ln, bias=cEPS[:], scale=-1.0),
                waits=[("dve", dnv)])
            a_rs = op("act", lambda e, c=c: e.activation(
                ldstat[:, c + 1:c + 2], ldstat[:, c + 1:c + 2], AF.Exp, bias=0.0, scale=-0.5))
            axn_mark[i] = a_rs
            op("dve", lambda e, c=c: e.scalar_tensor_tensor(
                ldstat[:, c:c + 1], ldstat[:, c:c + 1], ldstat[:, c + 1:c + 2],
                ldstat[:, c:c + 1], ALU.mult, ALU.bypass), waits=[("act", a_rs)])
            op("dve", lambda e, c=c: e.tensor_scalar(
                ldstat[:, c:c + 1], ldstat[:, c:c + 1], -1.0, None, ALU.mult))
            d_xn = op("dve", lambda e, b3=b3, b2=b2, c=c: e.scalar_tensor_tensor(
                xnst[b2][:], xld[b3][:], ldstat[:, c + 1:c + 2],
                ldstat.ap()[:, c:c + 1].broadcast_to((128, D)),
                ALU.mult, ALU.add),
                waits=([(f"p1w{b2}", dma_out[i - 2])] if i >= 2 else []))
            dve_done[i] = n["dve"]
            dma_out[i] = dma("act", f"p1w{b2}", xn_tm[i * 128:(i + 1) * 128, :], xnst[b2][:],
                             waits=[("dve", d_xn)])

        for i in range(8):
            emit_p1_tile(i)
        wld = 0
        for k in range(16):
            wld = dma("sync", "wld", wqs_q[:, k * 2048:(k + 1) * 2048],
                      wq_b[k * 128:(k + 1) * 128, :],
                      waits=[cast_done["wq"]] if k == 0 else [])
        pan_t = [0] * NTB
        pe_mark = [0] * NTB
        ev_mark = [0] * NTB
        st_dma = [0] * 4
        sg_dma4 = [0, 0]
        ev = [0] * (NTB * 16)
        aev4 = [0] * (NTB * 16)
        for tb in range(NTB):
            for i in range(4 * tb + 8, min(4 * tb + 12, NT)):
                emit_p1_tile(i)
            b = tb % 2
            w = [("p1w0", dma_out[4 * tb + 2]), ("p1w1", dma_out[4 * tb + 3])]
            if tb >= 2:
                w.append(("pe", pe_mark[tb - 2]))
            pan_t[tb] = load_panel(tb, w)
            slot2 = tb % 2
            w2 = [pan_t[tb]]
            if tb == 0:
                w2.append(cast_done["wgk1"])
            if tb >= 2:
                w2.append(("dve", ev_mark[tb - 2]))
            for k in range(16):
                op("pe", lambda e, slot2=slot2, k=k, b=b: e.matmul(
                    psB[0:16, slot2 * 512:(slot2 + 1) * 512],
                    wgk1s[:, k * 16:(k + 1) * 16], xnp[b][:, k * 512:(k + 1) * 512],
                    start=(k == 0), stop=(k == 15)), waits=w2 if k == 0 else [], inc=(k == 15))
            pe2 = n["pe"]
            ev_mark[tb] = op("dve", lambda e, slot2=slot2, tb=tb: e.tensor_copy(
                int_sb[:, tb * 512:(tb + 1) * 512], psB[0:16, slot2 * 512:(slot2 + 1) * 512]),
                waits=[("pe", pe2)])
            for h in range(16):
                slot = h % 4
                for k in range(16):
                    w = [pan_t[tb]]
                    if k == 0 and (tb * 16 + h) >= 4:
                        w.append(("dve", ev[tb * 16 + h - 4]))
                    if k == 0 and tb == 0 and h == 0:
                        w.append(("wld", wld))
                    op("pe", lambda e, slot=slot, k=k, h=h, b=b: e.matmul(
                        psA[:, slot * 512:(slot + 1) * 512],
                        wqs_q[:, k * 2048 + h * 128: k * 2048 + (h + 1) * 128],
                        xnp[b][:, k * 512:(k + 1) * 512],
                        start=(k == 0), stop=(k == 15)), waits=w, inc=(k == 15))
                pe_done = n["pe"]
                wv_ = [("pe", pe_done)]
                if (tb * 16 + h) >= 4:
                    wv_.append((f"st{slot}", st_dma[slot]))
                ev[tb * 16 + h] = op("dve", lambda e, slot=slot: e.tensor_copy(
                    qstage_q[slot][:], psA[:, slot * 512:(slot + 1) * 512]), waits=wv_)
                st_dma[slot] = dma("act", f"st{slot}",
                                   q_fm[h * 128:(h + 1) * 128, tb * 512:(tb + 1) * 512],
                                   qstage_q[slot][:], waits=[("dve", ev[tb * 16 + h])])
            for h in range(16):
                idx4 = tb * 16 + h
                s4 = idx4 % 2
                w4 = []
                if h == 0:
                    w4.append(("dve", ev_mark[tb]))
                    if tb == 0:
                        w4.append(cast_done["wgk2"])
                if idx4 >= 2:
                    w4.append(("act", aev4[idx4 - 2]))
                mm4 = op("pe", lambda e, h=h, tb=tb, s4=s4: e.matmul(
                    psB[:, 1024 + s4 * 512:1024 + (s4 + 1) * 512],
                    wgk2s[:, h * 128:(h + 1) * 128],
                    int_sb[:, tb * 512:(tb + 1) * 512],
                    start=True, stop=True), waits=w4)
                wv_ = [("pe", mm4)]
                if idx4 >= 2:
                    wv_.append((f"vst{s4}", sg_dma4[s4]))
                aev4[idx4] = op("act", lambda e, s4=s4, h=h: e.activation(
                    sgst[s4][:], psB[:, 1024 + s4 * 512:1024 + (s4 + 1) * 512], AF.Sigmoid,
                    bias=bgk2c[:, h:h + 1], scale=1.0), waits=wv_)
                sg_dma4[s4] = dma("act", f"vst{s4}",
                                  sg_fm[h * 128:(h + 1) * 128, tb * 512:(tb + 1) * 512],
                                  sgst[s4][:], waits=[("act", aev4[idx4])])
            pe_mark[tb] = n["pe"]
        pe_q_end = n["pe"]
        dve_q_end = n["dve"]
        act_q_end = n["act"]
        p1w_end = (n["p1w0"], n["p1w1"])
        stv_end = [(f"st{i}", n[f"st{i}"]) for i in range(4)] + \
                  [(f"vst{i}", n[f"vst{i}"]) for i in range(2)]

    with ExitStack() as ph:
        sb3 = lambda nm, s, d: ph.enter_context(nc.sbuf_tensor(nm, s, d))
        wqs_k = sb3("p3_wqs_wk", [128, 16 * 2048], BF16)
        qstage_k = [sb3(f"p3_st_wk{i}", [128, 512], BF16) for i in range(4)]
        wld = 0
        for k in range(16):
            w = []
            if k == 0:
                w = [cast_done["wk"], ("pe", pe_q_end), ("dve", dve_q_end),
                     ("act", act_q_end), ("p1w0", p1w_end[0]), ("p1w1", p1w_end[1])] \
                    + stv_end
            wld = dma("gq", "wldk", wqs_k[:, k * 2048:(k + 1) * 2048],
                      wk_b[k * 128:(k + 1) * 128, :], waits=w)
        st_dma = [0] * 4
        ev = [0] * (NTB * 16)
        pe_mark = [0] * NTB
        for tb in range(NTB):
            b = tb % 2
            w = [("wldk", wld)] if tb == 0 else []
            if tb >= 2:
                w.append(("pe", pe_mark[tb - 2]))
            else:
                w.append(("pe", pe_q_end))
            pan = load_panel(tb, w)
            for h in range(16):
                slot = h % 4
                for k in range(16):
                    w = [pan]
                    if k == 0 and (tb * 16 + h) >= 4:
                        w.append(("dve", ev[tb * 16 + h - 4]))
                    if k == 0 and tb == 0 and h < 4:
                        w.append(("dve", dve_q_end))
                    op("pe", lambda e, slot=slot, k=k, h=h, b=b: e.matmul(
                        psA[:, slot * 512:(slot + 1) * 512],
                        wqs_k[:, k * 2048 + h * 128: k * 2048 + (h + 1) * 128],
                        xnp[b][:, k * 512:(k + 1) * 512],
                        start=(k == 0), stop=(k == 15)), waits=w, inc=(k == 15))
                pe_done = n["pe"]
                wv_ = [("pe", pe_done)]
                if (tb * 16 + h) >= 4:
                    wv_.append((f"st{slot}", st_dma[slot]))
                else:
                    wv_ += stv_end
                ev[tb * 16 + h] = op("dve", lambda e, slot=slot: e.tensor_copy(
                    qstage_k[slot][:], psA[:, slot * 512:(slot + 1) * 512]), waits=wv_)
                st_dma[slot] = dma("act", f"st{slot}",
                                   k_fm[h * 128:(h + 1) * 128, tb * 512:(tb + 1) * 512],
                                   qstage_k[slot][:], waits=[("dve", ev[tb * 16 + h])])
            pe_mark[tb] = n["pe"]
        barrier()

    with ExitStack() as ph5:
        sb5 = lambda nm, s, d: ph5.enter_context(nc.sbuf_tensor(nm, s, d))
        qF = sb5("p5_qF", [128, T], BF16)
        kF = sb5("p5_kF", [128, T], BF16)
        sgF = sb5("p5_sgF", [128, T], F32)
        lnb = sb5("p5_lnb", [128, T], F32)
        fac = sb5("p5_fac", [128, T], BF16)
        fneg = sb5("p5_fneg", [128, T], BF16)
        qko = sb5("p5_qko", [128, T], BF16)
        kho = sb5("p5_kho", [128, T], BF16)
        ends = sb5("p5_ends", [128, NCB], F32)
        endsx = sb5("p5_endsx", [128, NCB + 2], F32)
        op("dve", lambda e: e.memset(endsx[:, 0:1], 0.0))
        p5st = {"d_gc": 0, "d_qg": 0, "d_kh": 0, "st_k": 0, "st_q": 0}

        def emit_p5_head(h):
            st = dict(p5st)
            tq = dma("sync", "p5q", qF[:], q_fm[h * 128:(h + 1) * 128, :],
                     waits=[("dve", st["d_qg"])] if h else [])
            tk = dma("sync", "p5k", kF[:], k_fm[h * 128:(h + 1) * 128, :],
                     waits=[("dve", st["d_kh"])] if h else [])
            tsg = dma("sync", "p5s", sgF[:], sg_fm[h * 128:(h + 1) * 128, :],
                      waits=[("dve", st["d_gc"])] if h else [])
            a_ln = op("act", lambda e: e.activation(lnb[:], sgF[:], AF.Ln),
                      waits=[("p5s", tsg)])
            op("dve", lambda e: e.tensor_tensor_scan(
                sgF[:], lnb[:], lnb[:], 0.0, ALU.add, ALU.bypass),
                waits=[("act", a_ln)])
            op("dve", lambda e: e.tensor_copy(ends[:], sgF.ap()[:, CB - 1:T:CB]))
            op("dve", lambda e: e.tensor_scalar(
                endsx[:, 1:NCB + 1], ends[:], 1.0 / 16.0, None, ALU.mult))
            obc = endsx.ap()[:, 0:NCB].unsqueeze(2).broadcast_to((128, NCB, CB))
            d_gc = op("dve", lambda e, obc=obc: e.scalar_tensor_tensor(
                lnb.ap().rearrange("p (c t) -> p c t", t=CB),
                sgF.ap().rearrange("p (c t) -> p c t", t=CB), 1.0 / 16.0,
                obc, ALU.mult, ALU.subtract))
            op("act", lambda e, h=h: e.activation(
                decay.ap()[:, h * NCB:(h + 1) * NCB].unsqueeze(2),
                lnb.ap().rearrange("p (c t) -> p c t", t=CB)[:, :, CB - 1:CB], AF.Exp),
                waits=[("dve", d_gc)])
            a_fp = op("act", lambda e: e.activation(fac[:], lnb[:], AF.Exp, bias=cLNS[:]),
                      waits=[("dve", st["d_kh"])] if h else [])
            wq_ = [("act", a_fp), ("p5q", tq)]
            if h:
                wq_.append(("p5wq", st["st_q"]))
            d_qg = op("dve", lambda e: e.tensor_tensor(qko[:], qF[:], fac[:], ALU.mult),
                      waits=wq_)
            st_q = dma("act", "p5wq", qg_fm[h * 128:(h + 1) * 128, :], qko[:],
                       waits=[("dve", d_qg)])
            a_fn = op("act", lambda e: e.activation(fneg[:], lnb[:], AF.Exp, scale=-1.0),
                      waits=[("dve", st["d_kh"])] if h else [])
            d_kh = op("dve", lambda e: e.tensor_tensor(kho[:], kF[:], fneg[:], ALU.mult),
                      waits=[("act", a_fn), ("p5k", tk)] +
                            ([("p5wk", st["st_k"])] if h else []))
            st_k = dma("act", "p5wk", kh_fm[h * 128:(h + 1) * 128, :],
                       kho[:],
                       waits=[("dve", d_kh)])
            p5st.update(d_gc=d_gc, d_qg=d_qg, d_kh=d_kh, st_k=st_k, st_q=st_q)

        for wsrc, dstf, castkey, silu, p5base in [(wv_b, v_tm, "wv", False, 0),
                                                  (wg_b, gate_tm, "wg", True, 8)]:
            with ExitStack() as ph:
                sb6 = lambda nm, s, d: ph.enter_context(nc.sbuf_tensor(nm, s, d))
                wqs = sb6(f"p6_w_{castkey}", [128, 16 * 2048], BF16)
                vstage = [sb6(f"p6_st_{castkey}{i}", [128, 2048], BF16) for i in range(2)]
                wld = 0
                for k in range(16):
                    wld = dma("sync", "wld", wqs[:, k * 2048:(k + 1) * 2048],
                              wsrc[k * 128:(k + 1) * 128, :],
                              waits=[cast_done[castkey]] if k == 0 else [])
                vst_dma = [0, 0]
                vev = [0] * NT
                pe_mark = [0] * NTB
                for tb in range(NTB):
                    b = tb % 2
                    w = [("wld", wld)] if tb == 0 else []
                    if tb >= 2:
                        w.append(("pe", pe_mark[tb - 2]))
                    pan = load_panel(tb, w)
                    for tt in range(4):
                        i = tb * 4 + tt
                        slot = i % 2
                        ps = psA if slot == 0 else psB
                        cstream = "act" if silu else "dve"
                        for k in range(16):
                            w = [pan]
                            if k == 0 and i >= 2:
                                w.append((cstream, vev[i - 2]))
                            for fg in range(4):
                                op("pe", lambda e, ps=ps, k=k, fg=fg, tt=tt, b=b, wqs=wqs: e.matmul(
                                    ps[:, fg * 512:(fg + 1) * 512],
                                    xnp[b][:, k * 512 + tt * 128: k * 512 + (tt + 1) * 128],
                                    wqs[:, k * 2048 + fg * 512: k * 2048 + (fg + 1) * 512],
                                    start=(k == 0), stop=(k == 15)),
                                    waits=w if fg == 0 else [], inc=(k == 15 and fg == 3))
                        pe_done = n["pe"]
                        wv_ = [("pe", pe_done)]
                        if i >= 2:
                            wv_.append((f"vst{slot}", vst_dma[slot]))
                        if silu:
                            vev[i] = op("act", lambda e, ps=ps, slot=slot, vstage=vstage: e.activation(
                                vstage[slot][:], ps[:], AF.Silu), waits=wv_)
                            vst_dma[slot] = dma("act", f"vst{slot}", dstf[i * 128:(i + 1) * 128, :],
                                                vstage[slot][:], waits=[("act", vev[i])])
                        else:
                            vev[i] = op("dve", lambda e, ps=ps, slot=slot, vstage=vstage: e.tensor_copy(
                                vstage[slot][:], ps[:]), waits=wv_)
                            vst_dma[slot] = dma("act", f"vst{slot}", dstf[i * 128:(i + 1) * 128, :],
                                                vstage[slot][:], waits=[("dve", vev[i])])
                    emit_p5_head(p5base + tb)
                    pe_mark[tb] = n["pe"]
                barrier()

    with ExitStack() as ph:
        sb8 = lambda nm, s, d: ph.enter_context(nc.sbuf_tensor(nm, s, d))
        w0t = [sb8(f"p8_w0t{i}", [128, 16 * 1024], BF16) for i in range(2)]
        gst = [sb8(f"p8_gst{i}", [128, 512], BF16) for i in range(4)]
        gst_dma = [0] * 4
        gev = [0] * (8 * NTB * 8)
        oct_pe_mark = [0] * 8
        blk_pe_mark = [0] * (8 * NTB)
        for oct_ in range(8):
            ob = oct_ % 2
            w0ld = 0
            for k in range(16):
                w = []
                if k == 0:
                    w = [cast_done["w0"]]
                    if oct_ >= 2:
                        w.append(("pe", oct_pe_mark[oct_ - 2]))
                w0ld = dma("sync", f"w0t{ob}", w0t[ob][:, k * 1024:(k + 1) * 1024],
                           w0_b[oct_ * 1024:(oct_ + 1) * 1024, k * 128:(k + 1) * 128],
                           waits=w, transpose=True)
            for tb in range(NTB):
                blk = oct_ * NTB + tb
                b = tb % 2
                w = [(f"w0t{ob}", w0ld)]
                if blk >= 2:
                    w.append(("pe", blk_pe_mark[blk - 2]))
                pan = load_panel(tb, w)
                for ns in range(8):
                    idx = blk * 8 + ns
                    slot = idx % 4
                    psl = psA if slot < 2 else psB
                    pcol = (slot % 2) * 512
                    for k in range(16):
                        w = [pan]
                        if k == 0 and idx >= 4:
                            w.append(("act", gev[idx - 4]))
                        op("pe", lambda e, psl=psl, pcol=pcol, k=k, ns=ns, ob=ob, b=b: e.matmul(
                            psl[:, pcol:pcol + 512],
                            w0t[ob][:, k * 1024 + ns * 128: k * 1024 + (ns + 1) * 128],
                            xnp[b][:, k * 512:(k + 1) * 512],
                            start=(k == 0), stop=(k == 15)), waits=w, inc=(k == 15))
                    pe_done = n["pe"]
                    wv_ = [("pe", pe_done)]
                    if idx >= 4:
                        wv_.append((f"st{slot}", gst_dma[slot]))
                    nchunk = oct_ * 8 + ns
                    gev[idx] = op("act", lambda e, psl=psl, pcol=pcol, slot=slot, nchunk=nchunk:
                                  e.activation(gst[slot][:], psl[:, pcol:pcol + 512], AF.Gelu,
                                               bias=bias0c[:, nchunk:nchunk + 1]), waits=wv_)
                    gst_dma[slot] = dma(
                        "act", f"st{slot}",
                        d1_fm[nchunk * 128:(nchunk + 1) * 128, tb * 512:(tb + 1) * 512],
                        gst[slot][:], waits=[("act", gev[idx])])
                blk_pe_mark[blk] = n["pe"]
            oct_pe_mark[oct_] = n["pe"]
        barrier()

    xnp_es.close()

    with ExitStack() as ph:
        sb9 = lambda nm, s, d: ph.enter_context(nc.sbuf_tensor(nm, s, d))
        w1t = [sb9(f"p9_w1t{i}", [128, 16 * 2048], BF16) for i in range(2)]
        d1c = [sb9(f"p9_d1c{i}", [128, 16 * 128], BF16) for i in range(2)]
        est = [sb9(f"p9_est{i}", [128, 2048], F32) for i in range(2)]
        epld = sb9("p9_ep", [128, 2048], F32)
        gsil = sb9("p9_gs", [128, 2048], BF16)
        est_dma = [0, 0]
        gsil_dma = [0, 0]
        sev = [0] * NT
        w1ld = [0, 0]
        qtr_pe_end = [0] * 4
        prev_eev = [0] * NT

        def load_w1_qtr(qtr, pe_guard):
            par = qtr % 2
            t = 0
            for ncl in range(16):
                w = []
                if ncl == 0:
                    w = [cast_done["w1"]]
                    if pe_guard:
                        w.append(("pe", pe_guard))
                t = dma("sync", "w1ld", w1t[par][:, ncl * 2048:(ncl + 1) * 2048],
                        w1_b[:, (qtr * 16 + ncl) * 128:(qtr * 16 + ncl + 1) * 128],
                        waits=w, transpose=True)
            w1ld[par] = t

        load_w1_qtr(0, 0)
        for qtr in range(4):
            par = qtr % 2
            eev = [0] * NT
            tt_pe_mark = [0] * NT
            d1ld = [0, 0]
            for tt in range(NT):
                b = tt % 2
                w = [("w1ld", w1ld[par])] if tt == 0 else []
                if tt >= 2:
                    w.append(("pe", tt_pe_mark[tt - 2]))
                elif qtr > 0:
                    w.append(("pe", qtr_pe_end[qtr - 1]))
                srcap = d1_fm.rearrange("(c p) t -> p c t", p=128)[
                    :, qtr * 16:(qtr + 1) * 16, tt * 128:(tt + 1) * 128]
                d1ld[b] = dma("sync", f"d1c{b}",
                              d1c[b].ap().rearrange("p (c t) -> p c t", t=128), srcap, waits=w)
                if tt == 2 and qtr < 3:
                    load_w1_qtr(qtr + 1, qtr_pe_end[qtr - 1] if qtr >= 1 else 0)
                slot = tt % 2
                ps = psA if slot == 0 else psB
                if qtr == 0:
                    for fg in range(4):
                        op("pe", lambda e, ps=ps, fg=fg: e.matmul(
                            ps[:, fg * 512:(fg + 1) * 512], onesrow[0:1, 0:128],
                            b1row[0:1, fg * 512:(fg + 1) * 512],
                            start=True, stop=False),
                            waits=([(f"d1c{b}", d1ld[b])] + ([("dve", eev[tt - 2])] if tt >= 2 else []))
                            if fg == 0 else [], inc=False)
                for ncl in range(16):
                    w = []
                    if ncl == 0 and qtr != 0:
                        w = [(f"d1c{b}", d1ld[b])]
                        if tt >= 2:
                            w.append(("dve", eev[tt - 2]))
                        else:
                            w.append(("dve", prev_eev[NT - 2 + tt]))
                    for fg in range(4):
                        op("pe", lambda e, ps=ps, ncl=ncl, fg=fg, b=b, qtr=qtr, par=par: e.matmul(
                            ps[:, fg * 512:(fg + 1) * 512],
                            d1c[b][:, ncl * 128:(ncl + 1) * 128],
                            w1t[par][:, ncl * 2048 + fg * 512: ncl * 2048 + (fg + 1) * 512],
                            start=(ncl == 0 and qtr != 0), stop=(ncl == 15)),
                            waits=w if fg == 0 else [], inc=(ncl == 15 and fg == 3))
                pe_done = n["pe"]
                tt_pe_mark[tt] = pe_done
                wv_ = [("pe", pe_done)]
                if est_dma[slot]:
                    wv_.append((f"gacc{slot}", est_dma[slot]))
                eev[tt] = op("dve", lambda e, ps=ps, slot=slot: e.tensor_copy(
                    est[slot][:], ps[:]), waits=wv_)
                kw = {"accum_op": ALU.add} if qtr > 0 else {}
                est_dma[slot] = dma("gq", f"gacc{slot}", e_part[tt * 128:(tt + 1) * 128, :],
                                    est[slot][:], waits=[("dve", eev[tt])], **kw)
                if qtr == 3:
                    w = [(f"gacc{slot}", est_dma[slot])]
                    if tt >= 1:
                        w.append(("act", sev[tt - 1]))
                    tld = dma("gq", "p9e0", epld[:],
                              e_part[tt * 128:(tt + 1) * 128, :], waits=w)
                    wv_ = [("p9e0", tld)]
                    if tt >= 1:
                        wv_.append(("p9g0", gsil_dma[0]))
                    sev[tt] = op("act", lambda e: e.activation(
                        gsil[:], epld[:], AF.Silu), waits=wv_)
                    gsil_dma[0] = dma("act", "p9g0", g_tm[tt * 128:(tt + 1) * 128, :],
                                      gsil[:], waits=[("act", sev[tt])])
            qtr_pe_end[qtr] = n["pe"]
            prev_eev = eev
        barrier()

    with ExitStack() as ph:
        sb10 = lambda nm, s, d: ph.enter_context(nc.sbuf_tensor(nm, s, d))
        S32 = sb10("c_S32", [128, H * HD], F32)
        S16 = [sb10("c_S16a", [128, H * HD], BF16), sb10("c_S16b", [128, H * HD], BF16)]
        op("dve", lambda e: e.memset(S32[:], 0.0))
        op("dve", lambda e: e.memset(S16[0][:], 0.0))
        p10_init = op("dve", lambda e: e.memset(S16[1][:], 0.0))
        qg2 = [sb10(f"p10_qg{i}", [128, H * 128], BF16) for i in range(3)]
        kh2 = [sb10(f"p10_kh{i}", [128, H * 128], BF16) for i in range(3)]
        kt2 = [sb10(f"p10_kt{i}", [128, 2048], BF16) for i in range(3)]
        v2 = [sb10(f"p10_v{i}", [128, 2048], BF16) for i in range(3)]
        atm = [sb10(f"p10_atm{i}", [128, 2048], BF16) for i in range(2)]
        ost = [sb10(f"p10_ost{i}", [128, 2048], BF16) for i in range(2)]
        otl = [sb10(f"p11_o{i}", [128, 2048], BF16) for i in range(2)]
        gtl = [sb10(f"p11_g{i}", [128, 2048], BF16) for i in range(2)]
        sqt = [sb10(f"p11_sq{i}", [128, 2048], F32) for i in range(2)]
        rst = sb10("p11_rs", [128, 32], BF16)
        ogt = [sb10(f"p11_og{i}", [128, 2048], BF16) for i in range(2)]
        ldk = [0, 0, 0]
        ktv = [0, 0, 0]
        o_dma = [0, 0]
        og_dma = [0, 0]
        rev = [0] * NCB
        iter_pe_end = [0] * (NCB + 1)
        s16_done = [0] * NCB
        fold_done = [[0, 0] for _ in range(NCB)]
        mask_done = [[0, 0] for _ in range(NCB)]
        inter_pe = [0] * NCB
        o_evac = [0] * NCB
        mbc = sb_mask.ap().unsqueeze(1).broadcast_to((128, 8, CB))
        for i in range(NCB + 1):
            if i < NCB:
                b = i % 3
                w = []
                if i >= 3:
                    w = [("pe", iter_pe_end[i - 2])]
                t0 = i * 128
                dma("sync", f"sc{b % 2}",
                    qg2[b].ap().rearrange("p (h t) -> p h t", t=128),
                    qg_fm.rearrange("(h d) t -> d h t", d=128)[:, :, t0:t0 + 128], waits=w)
                dma("sync", f"sc{b % 2}",
                    kh2[b].ap().rearrange("p (h t) -> p h t", t=128),
                    kh_fm.rearrange("(h d) t -> d h t", d=128)[:, :, t0:t0 + 128])
                ktv[b] = dma("sync", f"kt{b % 2}", kt2[b][:], kh_fm[:, t0:t0 + 128],
                             transpose=True)
                ldk[b] = dma("sync", f"sc{b % 2}", v2[b][:], v_tm[t0:t0 + 128, :])
                for uh in range(2):
                    base = 1024 - uh * 1024
                    for hh in range(8):
                        h = uh * 8 + hh
                        w = []
                        if hh == 0:
                            w = [(f"sc{b % 2}", ldk[b]), (f"kt{b % 2}", ktv[b])]
                            if i >= 1:
                                w.append(("dve", mask_done[i - 1][uh]))
                        op("pe", lambda e, h=h, hh=hh, b=b, base=base: e.matmul(
                            psB[:, base + hh * 128: base + (hh + 1) * 128],
                            kt2[b][:, h * 128:(h + 1) * 128],
                            v2[b][:, h * 128:(h + 1) * 128],
                            start=True, stop=True), waits=w, inc=(hh == 7))
                pe_u = n["pe"]
            if i >= 1:
                c1 = i - 1
                b1 = c1 % 3
                par1 = c1 % 2
                for h in range(16):
                    w = []
                    if h == 0:
                        if c1 >= 1:
                            w.append(("act", s16_done[c1 - 1]))
                        else:
                            w.append(("dve", p10_init))
                        if c1 >= 1:
                            w.append(("act", o_evac[c1 - 1]))
                    op("pe", lambda e, h=h, b1=b1, par1=par1: e.matmul(
                        psA[:, h * 128:(h + 1) * 128],
                        qg2[b1][:, h * 128:(h + 1) * 128],
                        S16[par1][:, h * 128:(h + 1) * 128],
                        start=(h % 4 == 0), stop=False), waits=w, inc=False)
                inter_pe[c1] = n["pe"]
                for h in range(16):
                    w = []
                    if h == 0:
                        w = [("dve", mask_done[c1][0])]
                    if h == 8:
                        w = [("dve", mask_done[c1][1])]
                    op("pe", lambda e, h=h, b1=b1, par1=par1: e.matmul(
                        psA[:, h * 128:(h + 1) * 128],
                        atm[par1][:, h * 128:(h + 1) * 128],
                        v2[b1][:, h * 128:(h + 1) * 128],
                        start=False, stop=True), waits=w, inc=(h == 15))
                pe_o = n["pe"]
            if i < NCB:
                for uh in range(2):
                    base = 1024 - uh * 1024
                    hc0 = uh * 1024
                    w = [("pe", pe_u)]
                    if uh == 0 and i >= 1:
                        w.append(("act", s16_done[i - 1]))
                    op("dve", lambda e, hc0=hc0, base=base: e.tensor_tensor(
                        S32[:, hc0:hc0 + 1024], S32[:, hc0:hc0 + 1024],
                        psB[:, base:base + 1024], ALU.add), waits=w)
                    dbc = decay.ap()[:, i::NCB][:, uh * 8:(uh + 1) * 8].unsqueeze(2)\
                        .broadcast_to((128, 8, 128))
                    fold_done[i][uh] = op("dve", lambda e, hc0=hc0, dbc=dbc: e.tensor_tensor(
                        S32.ap()[:, hc0:hc0 + 1024].rearrange("p (h v) -> p h v", v=128),
                        S32.ap()[:, hc0:hc0 + 1024].rearrange("p (h v) -> p h v", v=128),
                        dbc, ALU.mult))
                for uh in range(2):
                    hc0 = uh * 1024
                    w = [("dve", fold_done[i][uh])]
                    if uh == 0 and i >= 1:
                        w.append(("pe", inter_pe[i - 1]))
                    op("act", lambda e, hc0=hc0, i=i: e.activation(
                        S16[1 - (i % 2)][:, hc0:hc0 + 1024], S32[:, hc0:hc0 + 1024],
                        AF.Copy), waits=w)
                s16_done[i] = n["act"]
                for ab in range(2):
                    base = 1024 - ab * 1024
                    for hh in range(8):
                        h = ab * 8 + hh
                        w = [("dve", fold_done[i][ab])] if hh == 0 else []
                        op("pe", lambda e, h=h, hh=hh, b=b, base=base: e.matmul(
                            psB[:, base + hh * 128: base + (hh + 1) * 128],
                            kh2[b][:, h * 128:(h + 1) * 128],
                            qg2[b][:, h * 128:(h + 1) * 128],
                            start=True, stop=True), waits=w, inc=(hh == 7))
                    pe_at = n["pe"]
                    mask_done[i][ab] = op("dve", lambda e, i=i, ab=ab, base=base: e.tensor_tensor(
                        atm[i % 2].ap()[:, ab * 1024:(ab + 1) * 1024].rearrange(
                            "p (h c) -> p h c", c=128),
                        psB.ap()[:, base:base + 1024].rearrange("p (h c) -> p h c", c=128),
                        mbc, ALU.mult), waits=[("pe", pe_at)])
            if i >= 1:
                wv_ = [("pe", pe_o)]
                if c1 >= 2:
                    wv_.append((f"ow{par1}", o_dma[par1]))
                a_o = op("act", lambda e, par1=par1: e.activation(
                    ost[par1][:], psA[:], AF.Copy), waits=wv_)
                o_evac[c1] = a_o
                o_dma[par1] = dma("act", f"ow{par1}", o_tm[c1 * 128:(c1 + 1) * 128, :],
                                  ost[par1][:], waits=[("act", a_o)])
                tt = c1
                pb = par1
                c0 = pb * 16
                w = [(f"ow{pb}", o_dma[pb])]
                if tt >= 2:
                    w.append(("dve", rev[tt - 2]))
                tlo = dma("sync", f"p11a{pb}", otl[pb][:], o_tm[tt * 128:(tt + 1) * 128, :],
                          waits=w)
                tlg = dma("sync", f"p11b{pb}", gtl[pb][:], gate_tm[tt * 128:(tt + 1) * 128, :])
                a_sq = op("act", lambda e, pb=pb: e.activation(sqt[pb][:], otl[pb][:], AF.Square),
                          waits=[(f"p11a{pb}", tlo)] + ([("dve", rev[tt - 2])] if tt >= 2 else []))
                d_rd = op("dve", lambda e, pb=pb, c0=c0: e.tensor_reduce(
                    ldstat[:, c0:c0 + 16],
                    sqt[pb].ap().rearrange("p (h d) -> p h d", d=128), AX.X, ALU.add),
                    waits=[("act", a_sq)])
                op("act", lambda e, c0=c0: e.activation(
                    ldstat[:, c0:c0 + 16], ldstat[:, c0:c0 + 16], AF.Ln,
                    bias=cEPS[:], scale=1.0 / 128.0), waits=[("dve", d_rd)])
                a_r = op("act", lambda e, c0=c0: e.activation(
                    rst[:, c0:c0 + 16], ldstat[:, c0:c0 + 16], AF.Exp, scale=-0.5),
                    waits=([("dve", rev[tt - 2])] if tt >= 2 else []))
                rbc = rst.ap()[:, c0:c0 + 16].unsqueeze(2).broadcast_to((128, 16, 128))
                wv_ = [("act", a_r)]
                if tt >= 2:
                    wv_.append((f"ogw{pb}", og_dma[pb]))
                op("dve", lambda e, pb=pb, rbc=rbc: e.tensor_tensor(
                    ogt[pb].ap().rearrange("p (h d) -> p h d", d=128),
                    otl[pb].ap().rearrange("p (h d) -> p h d", d=128),
                    rbc, ALU.mult), waits=wv_)
                rev[tt] = op("dve", lambda e, pb=pb: e.tensor_tensor(
                    ogt[pb][:], ogt[pb][:], gtl[pb][:], ALU.mult), waits=[(f"p11b{pb}", tlg)])
                og_dma[pb] = dma("act", f"ogw{pb}", og_tm[tt * 128:(tt + 1) * 128, :], ogt[pb][:],
                                 waits=[("dve", rev[tt])])
            iter_pe_end[i] = n["pe"]
        barrier()

    with ExitStack() as ph:
        sb12 = lambda nm, s, d: ph.enter_context(nc.sbuf_tensor(nm, s, d))
        wos = sb12("p12_wos", [128, 16 * 2048], BF16)
        ogf = sb12("p12_ogf", [128, 16 * 1024], BF16)
        xtl = [sb12(f"p12_x{i}", [128, 2048], F32) for i in range(2)]
        gft = [sb12(f"p12_gf{i}", [128, 2048], BF16) for i in range(2)]
        outs = [sb12(f"p12_out{i}", [128, 2048], F32) for i in range(2)]
        for k in range(16):
            tw = dma("sync", "wld", wos[:, k * 2048:(k + 1) * 2048],
                     wo_b[k * 128:(k + 1) * 128, :],
                     waits=[cast_done["wo"]] if k == 0 else [])
        wo_all = n["wld"]
        for k in range(16):
            op("dve", lambda e, k=k: e.tensor_scalar(
                wos[:, k * 2048:(k + 1) * 2048], wos[:, k * 2048:(k + 1) * 2048],
                gnormc[:, 0:1], None, ALU.mult), waits=[("wld", wo_all)] if k == 0 else [])
        wo_ready = n["dve"]
        if dbg_wos is not None:
            for k in range(16):
                dma("act", "outw0", dbg_wos[k * 128:(k + 1) * 128, :],
                    wos[:, k * 2048:(k + 1) * 2048], waits=[("dve", wo_ready)])
        out_dma = [0, 0]
        cev = [0] * NT
        qtr_pe_mark = [0] * 4
        for qtr in range(4):
            ogld = 0
            for h in range(16):
                w = []
                if h == 0 and qtr >= 1:
                    w = [("pe", qtr_pe_mark[qtr - 1])]
                ogld = dma("sync", "ogf", ogf[:, h * 1024:(h + 1) * 1024],
                           og_tm[qtr * 1024:(qtr + 1) * 1024, h * 128:(h + 1) * 128],
                           waits=w, transpose=True)
            for tt8 in range(8):
                tt = qtr * 8 + tt8
                b = tt % 2
                w = [("dve", cev[tt - 2])] if tt >= 2 else []
                tx = dma("sync", f"px{b}", xtl[b][:], x_h[tt * 128:(tt + 1) * 128, :], waits=w)
                tg = dma("sync", f"pg{b}", gft[b][:], g_tm[tt * 128:(tt + 1) * 128, :])
                slot = tt % 2
                ps = psA if slot == 0 else psB
                for h in range(16):
                    w = []
                    if h == 0:
                        w = [("ogf", ogld), ("dve", wo_ready)]
                        if tt >= 2:
                            w.append(("dve", cev[tt - 2]))
                    for fg in range(4):
                        op("pe", lambda e, ps=ps, h=h, fg=fg, tt8=tt8: e.matmul(
                            ps[:, fg * 512:(fg + 1) * 512],
                            ogf[:, h * 1024 + tt8 * 128: h * 1024 + (tt8 + 1) * 128],
                            wos[:, h * 2048 + fg * 512: h * 2048 + (fg + 1) * 512],
                            start=(h == 0), stop=(h == 15)),
                            waits=w if fg == 0 else [], inc=(h == 15 and fg == 3))
                pe_done = n["pe"]
                wm_ = [("pe", pe_done), (f"pg{b}", tg)]
                if tt >= 2:
                    wm_.append((f"outw{b}", out_dma[b]))
                d_m = op("dve", lambda e, ps=ps, b=b: e.tensor_tensor(
                    outs[b][:], ps[:], gft[b][:], ALU.mult), waits=wm_)
                cev[tt] = op("dve", lambda e, b=b: e.tensor_tensor(
                    outs[b][:], outs[b][:], xtl[b][:], ALU.add),
                    waits=[(f"px{b}", tx), ("dve", d_m)])
                out_dma[b] = dma("act", f"outw{b}", out_h[tt * 128:(tt + 1) * 128, :], outs[b][:],
                                 waits=[("dve", cev[tt])])
            qtr_pe_mark[qtr] = n["pe"]
        waitonly("act", [("outw0", n["outw0"]), ("outw1", n["outw1"])])
        barrier()

    with nc.Block() as block:
        @block.tensor
        def _(e):
            for f in streams["pe"]:
                f(e)

        @block.vector
        def _(e):
            for f in streams["dve"]:
                f(e)

        @block.scalar
        def _(e):
            for f in streams["act"]:
                f(e)

        @block.sync
        def _(e):
            for f in streams["sync"]:
                f(e)

        @block.gpsimd
        def _(e):
            for f in streams["gq"]:
                f(e)

    es.close()
    return nc


def kernel(L0_attn_x, L0_attn_Wq, L0_attn_Wk, L0_attn_Wv, L0_attn_Wgk1, L0_attn_Wgk2,
           L0_attn_bgk2, L0_attn_Wg, L0_attn_gnorm_w, L0_attn_Wo,
           L0_ffn_W0, L0_ffn_bias0, L0_ffn_W1, L0_ffn_bias1):
    nc = build_program()
    wmap = {
        "Wq": np.ascontiguousarray(np.asarray(L0_attn_Wq, np.float32)),
        "Wk": np.ascontiguousarray(np.asarray(L0_attn_Wk, np.float32)),
        "Wv": np.ascontiguousarray(np.asarray(L0_attn_Wv, np.float32)),
        "Wgk1": np.ascontiguousarray(np.asarray(L0_attn_Wgk1, np.float32)),
        "Wgk2": np.ascontiguousarray(np.asarray(L0_attn_Wgk2, np.float32)),
        "bgk2": np.ascontiguousarray(np.asarray(L0_attn_bgk2, np.float32)),
        "Wg": np.ascontiguousarray(np.asarray(L0_attn_Wg, np.float32)),
        "gnorm": np.ascontiguousarray(np.asarray(L0_attn_gnorm_w, np.float32)),
        "Wo": np.ascontiguousarray(np.asarray(L0_attn_Wo, np.float32)),
        "W0": np.ascontiguousarray(np.asarray(L0_ffn_W0, np.float32)),
        "bias0": np.ascontiguousarray(np.asarray(L0_ffn_bias0, np.float32)),
        "W1": np.ascontiguousarray(np.asarray(L0_ffn_W1, np.float32)),
        "bias1": np.ascontiguousarray(np.asarray(L0_ffn_bias1, np.float32)),
    }
    xfull = np.asarray(L0_attn_x, np.float32)
    in_maps = [dict(wmap, x=np.ascontiguousarray(xfull[i])) for i in range(8)]
    res = run_bass_kernel_spmd(nc, in_maps, core_ids=list(range(8)))
    return np.stack([r["out"] for r in res.results], axis=0)
